# version 10
# baseline (speedup 1.0000x reference)
import contextlib
import numpy as np
import ml_dtypes
import concourse.bass as bass
import concourse.mybir as mybir
from concourse.bass_utils import run_bass_kernel_spmd

F32 = mybir.dt.float32
BF16 = mybir.dt.bfloat16
I32 = mybir.dt.int32
AF = mybir.ActivationFunctionType
ALU = mybir.AluOpType
AX = mybir.AxisListType.X
bfdt = ml_dtypes.bfloat16

S = 2048
E = 2048
HQ = 16
NKV = 4
D = 128
II = 8192
NTC = 16
NQB = 4
NHC = 16
NIC = 64
CSHIFT = 8.0
EPS = 1e-5
RSQD = float(1.0 / np.sqrt(128.0))

ENGINES = ("sync", "tensor", "scalar", "vector", "gpsimd")
PHASES = ("p0", "a", "b1", "b2", "c")
C_NFB = NQB
DEV_OUT = ()



_TAB_CACHE = {}


def _rope_tables_f32():
    if "cs" not in _TAB_CACHE:
        import subprocess
        import sys
        import tempfile
        code = (
            "import jax\n"
            "jax.config.update('jax_platforms', 'cpu')\n"
            "jax.config.update('jax_enable_x64', True)\n"
            "import numpy as np, jax.numpy as jnp, sys\n"
            "inv = 1.0 / 10000.0 ** (jnp.arange(0, 128, 2, dtype=jnp.float32) / 128)\n"
            "t = jnp.arange(2048, dtype=jnp.float32)\n"
            "fr = jnp.outer(t, inv)\n"
            "np.save(sys.argv[1], np.stack([np.asarray(jnp.cos(fr)), np.asarray(jnp.sin(fr))]))\n"
        )
        with tempfile.TemporaryDirectory() as td:
            p = td + "/t.npy"
            subprocess.run([sys.executable, "-c", code, p], check=True,
                           capture_output=True)
            cs = np.load(p)
        _TAB_CACHE["cs"] = (cs[0].astype(np.float32), cs[1].astype(np.float32))
    return _TAB_CACHE["cs"]


def _host_prep(L0_attn_x, L0_attn_Wq, L0_attn_Wk, L0_attn_Wv, L0_attn_Wo,
               L0_ffn_gate_w, L0_ffn_gate_scale, L0_ffn_up_w, L0_ffn_up_scale,
               L0_ffn_down_w, L0_ffn_down_scale):
    cm = np.ascontiguousarray
    x = np.asarray(L0_attn_x, dtype=np.float32)
    shared = {
        "wq": cm(np.asarray(L0_attn_Wq, np.float32)),
        "wk": cm(np.asarray(L0_attn_Wk, np.float32)),
        "wv": cm(np.asarray(L0_attn_Wv, np.float32)),
        "wo": cm(np.asarray(L0_attn_Wo, np.float32)),
        "gwt": cm(np.asarray(L0_ffn_gate_w, np.int32).T),
        "uwt": cm(np.asarray(L0_ffn_up_w, np.int32).T),
        "dwt": cm(np.asarray(L0_ffn_down_w, np.int32).T),
        "gsc": cm(np.asarray(L0_ffn_gate_scale, np.float32).reshape(NIC, 128).T),
        "usc": cm(np.asarray(L0_ffn_up_scale, np.float32).reshape(NIC, 128).T),
        "dscb": cm(np.broadcast_to(
            np.asarray(L0_ffn_down_scale, np.float32)[None, :], (128, E))),
    }
    cos32, sin32 = _rope_tables_f32()
    cos = np.concatenate([cos32, cos32], axis=1).astype(np.float64)
    sinn = np.concatenate([-sin32, sin32], axis=1).astype(np.float64)
    for nm, tab in (("cosk", cos.T), ("sink", sinn.T),
                    ("cosq", cos.T * RSQD), ("sinq", sinn.T * RSQD)):
        hi = tab.astype(np.float32)
        shared[nm] = cm(hi)
        shared[nm + "_lo"] = cm((tab - hi.astype(np.float64)).astype(np.float32))
    cmask = np.zeros((128, 4 * 512), dtype=np.float32)
    tk = np.arange(128)[:, None]
    tq = np.arange(512)[None, :]
    for p in range(4):
        cmask[:, p * 512:(p + 1) * 512] = np.where(tq >= p * 128 + tk, 0.0, -1e30)
    shared["cmask"] = cm(cmask.astype(bfdt))
    shared["identb"] = cm(np.eye(128, dtype=bfdt))
    shared["identf"] = cm(np.eye(128, dtype=np.float32))
    return [dict(shared, x=cm(x[c])) for c in range(8)]


class Ctr(dict):
    def bump(self, k, n=1):
        self[k] = self.get(k, 0) + n
        return self[k]


def build_nc():
    nc = bass.Bass("TRN2", target_bir_lowering=False)

    dr = {}
    dr["x"] = nc.dram_tensor("x", [S, E], F32, kind="ExternalInput")
    dr["wq"] = nc.dram_tensor("wq", [E, E], F32, kind="ExternalInput")
    dr["wk"] = nc.dram_tensor("wk", [E, 512], F32, kind="ExternalInput")
    dr["wv"] = nc.dram_tensor("wv", [E, 512], F32, kind="ExternalInput")
    dr["wo"] = nc.dram_tensor("wo", [E, E], F32, kind="ExternalInput")
    dr["gwt"] = nc.dram_tensor("gwt", [E, II], I32, kind="ExternalInput")
    dr["uwt"] = nc.dram_tensor("uwt", [E, II], I32, kind="ExternalInput")
    dr["dwt"] = nc.dram_tensor("dwt", [II, E], I32, kind="ExternalInput")
    dr["gsc"] = nc.dram_tensor("gsc", [128, NIC], F32, kind="ExternalInput")
    dr["usc"] = nc.dram_tensor("usc", [128, NIC], F32, kind="ExternalInput")
    dr["dscb"] = nc.dram_tensor("dscb", [128, E], F32, kind="ExternalInput")
    for t in ("cosk", "sink", "cosq", "sinq"):
        dr[t] = nc.dram_tensor(t, [128, S], F32, kind="ExternalInput")
        dr[t + "_lo"] = nc.dram_tensor(t + "_lo", [128, S], F32, kind="ExternalInput")
    dr["cmask"] = nc.dram_tensor("cmask", [128, 4 * 512], BF16, kind="ExternalInput")
    dr["identb"] = nc.dram_tensor("identb", [128, 128], BF16, kind="ExternalInput")
    dr["identf"] = nc.dram_tensor("identf", [128, 128], F32, kind="ExternalInput")
    dr["out"] = nc.dram_tensor("out", [S, E], F32, kind="ExternalOutput")

    def scratch(name, shape, dt):
        kind = "ExternalOutput" if name in DEV_OUT else "Internal"
        return nc.dram_tensor(name, shape, dt, kind=kind)
    dr["wq_h"] = scratch("wq_h", [HQ, NHC, 128, 128], BF16)
    dr["wq_l"] = scratch("wq_l", [HQ, NHC, 128, 128], BF16)
    dr["wk_h"] = scratch("wk_h", [NHC, 128, 512], BF16)
    dr["wk_l"] = scratch("wk_l", [NHC, 128, 512], BF16)
    dr["wv_h"] = scratch("wv_h", [NHC, 128, 512], BF16)
    dr["wv_l"] = scratch("wv_l", [NHC, 128, 512], BF16)
    dr["wo_h"] = scratch("wo_h", [NHC, 128, E], BF16)
    dr["wo_l"] = scratch("wo_l", [NHC, 128, E], BF16)
    dr["xnt_h"] = scratch("xnt_h", [NQB, 128, 8192], BF16)
    dr["xnt_l"] = scratch("xnt_l", [NQB, 128, 8192], BF16)
    dr["ao_h"] = scratch("ao_h", [NQB, 128, HQ * 512], BF16)
    dr["ao_l"] = scratch("ao_l", [NQB, 128, HQ * 512], BF16)
    dr["s_s"] = scratch("s_s", [NTC, 128, E], F32)
    dr["g_b"] = scratch("g_b", [NIC, 128, 2048], BF16)
    dr["u_b"] = scratch("u_b", [NIC, 128, 2048], BF16)
    dr["d_b"] = scratch("d_b", [NIC, 128, 2048], BF16)

    sems = {}

    def sem(name):
        if name not in sems:
            sems[name] = nc.alloc_semaphore(name)
        return sems[name]

    ctrs = {}

    def emit_conv_jobs(en, eng, C, jobs, cbuf=None):
        for kind, ic in jobs:
            C.bump("cv_o", 16)
            if en == "gpsimd":
                if kind == "d":
                    srca = dr["dwt"].ap()[ic * 128:(ic + 1) * 128, :]
                    dsts = dr["d_b"].ap()[ic]
                else:
                    w = dr["gwt"] if kind == "g" else dr["uwt"]
                    srca = w.ap()[:, ic * 128:(ic + 1) * 128].rearrange(
                        "(hh p) j -> p hh j", p=128)
                    dsts = dr[{"g": "g_b", "u": "u_b"}[kind]].ap()[ic].rearrange(
                        "p (hh j) -> p hh j", j=128)
                eng.dma_start(out=dsts, in_=srca).then_inc(sem("cv_o"), 16)

    def mk_emit(block, fn):
        for en in ENGINES:
            def closure(eng, en=en):
                fn(en, eng, ctrs.setdefault(en, Ctr()))
            getattr(block, en)(closure)

    pstack = contextlib.ExitStack()
    g_k_h = pstack.enter_context(nc.sbuf_tensor("g_k_h", [128, NKV * S], BF16))
    g_k_l = pstack.enter_context(nc.sbuf_tensor("g_k_l", [128, NKV * S], BF16))
    g_v_f = pstack.enter_context(nc.sbuf_tensor("g_v_f", [128, NTC * 512], F32))

    def _emit_p0(en, eng, C, T):
        wtmp, whi, wlo, chunks = T["wtmp"], T["whi"], T["wlo"], T["chunks"]
        for i, (nm, src, cols, dsth, dstl) in enumerate(chunks):
            b = i % 2
            ldk = "p0_ld%d" % b
            stk = "p0_st%d" % b
            C.bump(ldk, 16)
            if en == "sync":
                if i >= 2:
                    eng.wait_ge(sem(stk), C[stk])
                eng.dma_start(out=wtmp[:, b * 2048: b * 2048 + cols], in_=src).then_inc(sem(ldk), 16)
            C.bump("p0_hi")
            if en == "scalar":
                eng.wait_ge(sem(ldk), C[ldk])
                if i >= 2:
                    eng.wait_ge(sem(stk), C[stk])
                eng.copy(whi[:, b * 2048: b * 2048 + cols],
                         wtmp[:, b * 2048: b * 2048 + cols]).then_inc(sem("p0_hi"), 1)
            C.bump("p0_lo")
            if en == "vector":
                eng.wait_ge(sem("p0_hi"), C["p0_hi"])
                eng.tensor_sub(wlo[:, b * 2048: b * 2048 + cols],
                               wtmp[:, b * 2048: b * 2048 + cols],
                               whi[:, b * 2048: b * 2048 + cols]).then_inc(sem("p0_lo"), 1)
            C.bump(stk, 32)
            if en == "gpsimd":
                eng.wait_ge(sem("p0_hi"), C["p0_hi"])
                srch = whi[:, b * 2048: b * 2048 + cols]
                srcl = wlo[:, b * 2048: b * 2048 + cols]
                if nm == "wq":
                    srch = srch.rearrange("p (a j) -> p a j", j=128)
                    srcl = srcl.rearrange("p (a j) -> p a j", j=128)
                eng.dma_start(out=dsth, in_=srch).then_inc(sem(stk), 16)
                eng.wait_ge(sem("p0_lo"), C["p0_lo"])
                eng.dma_start(out=dstl, in_=srcl).then_inc(sem(stk), 16)
        if en == "gpsimd":
            eng.wait_ge(sem("p0_st0"), C["p0_st0"])
            eng.wait_ge(sem("p0_st1"), C["p0_st1"])

    def _emit_a(en, eng, C, T):
        xt, xn = T["xt"], T["xn"]
        xnt_h, xnt_l = T["xnt_h"], T["xnt_l"]
        wbh, wbl = T["wbh"], T["wbl"]
        ck, sk, stats = T["ck"], T["sk"], T["stats"]
        kf, kc, ks, ksw = T["kf"], T["kc"], T["ks"], T["ksw"]
        ptr, pkv = T["ptr"], T["pkv"]
        identf = T["identf"]

        tbuf = T["tbuf"]
        C.bump("a_id", 16)
        C.bump("a_if", 16 * 2)
        C.bump("a_tl0", 16)
        if en == "sync":
            eng.dma_start(out=identf[:], in_=dr["identf"].ap()[:, :]).then_inc(sem("a_id"), 16)
            eng.dma_start(out=ck[:], in_=dr["cosk"].ap()[:, :]).then_inc(sem("a_if"), 16)
            eng.dma_start(out=sk[:], in_=dr["sink"].ap()[:, :]).then_inc(sem("a_if"), 16)
            eng.dma_start(out=tbuf[:], in_=dr["cosk_lo"].ap()[:, :]).then_inc(sem("a_tl0"), 16)
        C.bump("a_tl1")
        if en == "vector":
            eng.wait_ge(sem("a_if"), 32)
            eng.wait_ge(sem("a_tl0"), C["a_tl0"])
            eng.tensor_add(ck[:], ck[:], tbuf[:]).then_inc(sem("a_tl1"), 1)
        C.bump("a_tl0", 16)
        if en == "sync":
            eng.wait_ge(sem("a_tl1"), C["a_tl1"])
            eng.dma_start(out=tbuf[:], in_=dr["sink_lo"].ap()[:, :]).then_inc(sem("a_tl0"), 16)
        C.bump("a_tl1")
        if en == "vector":
            eng.wait_ge(sem("a_tl0"), C["a_tl0"])
            eng.tensor_add(sk[:], sk[:], tbuf[:]).then_inc(sem("a_tl1"), 1)
        C.bump("a_eps")
        if en == "gpsimd":
            eng.memset(stats[:, 7:8], EPS).then_inc(sem("a_eps"), 1)

        for qb in range(NQB):
            for j in range(4):
                tc = qb * 4 + j
                b = tc % 2
                axk = "a_x%d" % b
                C.bump(axk, 16)
                if en == "sync":
                    if tc >= 2:
                        eng.wait_ge(sem("a_sq"), tc - 1)
                        eng.wait_ge(sem("a_xn"), 2 * tc - 2)
                    eng.dma_start(out=xt[:, b * 2048:(b + 1) * 2048],
                                  in_=dr["x"].ap()[tc * 128:(tc + 1) * 128, :]).then_inc(sem(axk), 16)
                C.bump("a_sq")
                if en == "scalar":
                    eng.wait_ge(sem(axk), C[axk])
                    eng.wait_ge(sem("a_tr"), 2 * tc)
                    eng.activation(xn[:], xt[:, b * 2048:(b + 1) * 2048], AF.Square,
                                   accum_out=stats[:, 3:4]).then_inc(sem("a_sq"), 1)
                C.bump("a_vr", 4)
                if en == "vector":
                    eng.wait_ge(sem(axk), C[axk])
                    eng.reduce_sum(stats[:, 0:1], xt[:, b * 2048:(b + 1) * 2048], AX).then_inc(sem("a_vr"), 1)
                    eng.wait_ge(sem("a_vr"), C["a_vr"] - 3)
                    eng.tensor_scalar_mul(stats[:, 1:2], stats[:, 0:1], 1.0 / E).then_inc(sem("a_vr"), 1)
                    eng.wait_ge(sem("a_vr"), C["a_vr"] - 2)
                    eng.tensor_mul(stats[:, 2:3], stats[:, 1:2], stats[:, 1:2]).then_inc(sem("a_vr"), 1)
                    eng.wait_ge(sem("a_sq"), C["a_sq"])
                    if tc == 0:
                        eng.wait_ge(sem("a_eps"), 1)
                    eng.tensor_scalar(stats[:, 4:5], stats[:, 3:4], 1.0 / E, None, ALU.mult).then_inc(sem("a_vr"), 1)
                    eng.wait_ge(sem("a_vr"), C["a_vr"])
                C.bump("a_vs")
                if en == "vector":
                    eng.tensor_sub(stats[:, 4:5], stats[:, 4:5], stats[:, 2:3]).then_inc(sem("a_vs"), 1)
                C.bump("a_sd")
                if en == "scalar":
                    eng.wait_ge(sem("a_vs"), C["a_vs"])
                    eng.activation(stats[:, 5:6], stats[:, 4:5], AF.Sqrt,
                                   bias=stats[:, 7:8]).then_inc(sem("a_sd"), 1)
                C.bump("a_xn")
                if en == "vector":
                    eng.wait_ge(sem("a_sd"), C["a_sd"])
                    eng.reciprocal(stats[:, 6:7], stats[:, 5:6]).then_inc(sem("a_xn"), 1)
                    eng.wait_ge(sem("a_xn"), C["a_xn"])
                C.bump("a_xn")
                if en == "vector":
                    eng.wait_ge(sem("a_tr"), 2 * tc)
                    eng.tensor_scalar(xn[:], xt[:, b * 2048:(b + 1) * 2048],
                                      stats[:, 1:2], stats[:, 6:7],
                                      ALU.subtract, ALU.mult).then_inc(sem("a_xn"), 1)
                for g in range(2):
                    gidx = C.bump("a_tr")
                    if en == "tensor":
                        if tc == 0 and g == 0:
                            eng.wait_ge(sem("a_id"), 16)
                        eng.wait_ge(sem("a_xn"), C["a_xn"])
                        if gidx > 2:
                            eng.wait_ge(sem("a_evh"), gidx - 2)
                            eng.wait_ge(sem("a_evl"), gidx - 2)
                        if j == 0 and g == 0 and qb > 0:
                            eng.wait_ge(sem("a_vev"), C["a_vev"])
                        for hl in range(8):
                            hc = g * 8 + hl
                            ins = eng.transpose(
                                ptr[:, g * 1024 + hl * 128: g * 1024 + (hl + 1) * 128],
                                xn[:, hc * 128:(hc + 1) * 128], identf[:])
                        ins.then_inc(sem("a_tr"), 1)
                    dsth = xnt_h[:].rearrange("p (h c) -> p h c", c=512)[
                        :, g * 8:(g + 1) * 8, j * 128:(j + 1) * 128]
                    dstl = xnt_l[:].rearrange("p (h c) -> p h c", c=512)[
                        :, g * 8:(g + 1) * 8, j * 128:(j + 1) * 128]
                    srcp = ptr[:, g * 1024:(g + 1) * 1024].rearrange("p (h c) -> p h c", c=128)
                    C.bump("a_evh")
                    if en == "scalar":
                        eng.wait_ge(sem("a_tr"), C["a_tr"])
                        eng.copy(dsth, srcp).then_inc(sem("a_evh"), 1)
                    C.bump("a_evl")
                    if en == "vector":
                        eng.wait_ge(sem("a_evh"), C["a_evh"])
                        eng.tensor_sub(dstl, srcp, dsth).then_inc(sem("a_evl"), 1)

            C.bump("a_wk", 32)
            if en == "sync":
                if qb > 0:
                    eng.wait_ge(sem("a_vm"), C["a_vm"])
                eng.dma_start(out=wbh[:].rearrange("p (h j) -> p h j", j=512),
                              in_=dr["wk_h"].ap().transpose([1, 0, 2])).then_inc(sem("a_wk"), 16)
                eng.dma_start(out=wbl[:].rearrange("p (h j) -> p h j", j=512),
                              in_=dr["wk_l"].ap().transpose([1, 0, 2])).then_inc(sem("a_wk"), 16)
            for kvh in range(NKV):
                pb = kvh % 2
                C.bump("a_km")
                if en == "tensor":
                    eng.wait_ge(sem("a_evh"), C["a_evh"])
                    eng.wait_ge(sem("a_evl"), C["a_evl"])
                    eng.wait_ge(sem("a_wk"), C["a_wk"])
                    if C["a_km"] > 2:
                        eng.wait_ge(sem("a_kev"), C["a_km"] - 2)
                    first = True
                    for hc in range(NHC):
                        wt_h = wbh[:, hc * 512 + kvh * 128: hc * 512 + (kvh + 1) * 128]
                        wt_l = wbl[:, hc * 512 + kvh * 128: hc * 512 + (kvh + 1) * 128]
                        rh = xnt_h[:, hc * 512:(hc + 1) * 512]
                        rl = xnt_l[:, hc * 512:(hc + 1) * 512]
                        dst = pkv[:, pb * 512:(pb + 1) * 512]
                        eng.matmul(dst, wt_h, rh, start=first, stop=False)
                        first = False
                        eng.matmul(dst, wt_h, rl, start=False, stop=False)
                        ins = eng.matmul(dst, wt_l, rh, start=False, stop=(hc == NHC - 1))
                    ins.then_inc(sem("a_km"), 1)
                C.bump("a_kev")
                if en == "scalar":
                    eng.wait_ge(sem("a_km"), C["a_km"])
                    if C["a_kev"] > 1:
                        eng.wait_ge(sem("a_krs"), C["a_kev"] - 1)
                    eng.copy(kf[:], pkv[:, pb * 512:(pb + 1) * 512])
                    eng.copy(ksw[0:64, :], pkv[64:128, pb * 512:(pb + 1) * 512])
                    eng.copy(ksw[64:128, :], pkv[0:64, pb * 512:(pb + 1) * 512]).then_inc(sem("a_kev"), 1)
                C.bump("a_vv", 3)
                C.bump("a_krs")
                if en == "vector":
                    eng.wait_ge(sem("a_kev"), C["a_kev"])
                    if qb == 0 and kvh == 0:
                        eng.wait_ge(sem("a_tl1"), 2)
                    cs = ck[:, qb * 512:(qb + 1) * 512]
                    ss = sk[:, qb * 512:(qb + 1) * 512]
                    eng.tensor_mul(kc[:], kf[:], cs)
                    eng.tensor_mul(ks[:], ksw[:], ss).then_inc(sem("a_vv"), 1)
                    eng.wait_ge(sem("a_vv"), C["a_vv"] - 2)
                    eng.tensor_add(kc[:], kc[:], ks[:]).then_inc(sem("a_vv"), 1)
                    eng.wait_ge(sem("a_vv"), C["a_vv"] - 1)
                    dh = g_k_h[:, kvh * S + qb * 512: kvh * S + (qb + 1) * 512]
                    dl = g_k_l[:, kvh * S + qb * 512: kvh * S + (qb + 1) * 512]
                    eng.tensor_copy(dh, kc[:]).then_inc(sem("a_vv"), 1)
                    eng.wait_ge(sem("a_vv"), C["a_vv"])
                    eng.tensor_sub(dl, kc[:], dh).then_inc(sem("a_krs"), 1)

            C.bump("a_wv", 32)
            if en == "sync":
                eng.wait_ge(sem("a_km"), C["a_km"])
                eng.dma_start(out=wbh[:].rearrange("p (h j) -> p h j", j=512),
                              in_=dr["wv_h"].ap().transpose([1, 0, 2])).then_inc(sem("a_wv"), 16)
                eng.dma_start(out=wbl[:].rearrange("p (h j) -> p h j", j=512),
                              in_=dr["wv_l"].ap().transpose([1, 0, 2])).then_inc(sem("a_wv"), 16)
            for j in range(4):
                tc = qb * 4 + j
                pb = j % 2
                C.bump("a_vm")
                if en == "tensor":
                    eng.wait_ge(sem("a_wv"), C["a_wv"])
                    if C["a_vm"] > 2:
                        eng.wait_ge(sem("a_vev"), C["a_vm"] - 2)
                    first = True
                    for hc in range(NHC):
                        lt_h = xnt_h[:, hc * 512 + j * 128: hc * 512 + (j + 1) * 128]
                        lt_l = xnt_l[:, hc * 512 + j * 128: hc * 512 + (j + 1) * 128]
                        rh = wbh[:, hc * 512:(hc + 1) * 512]
                        rl = wbl[:, hc * 512:(hc + 1) * 512]
                        dst = ptr[:, pb * 512:(pb + 1) * 512]
                        eng.matmul(dst, lt_h, rh, start=first, stop=False)
                        first = False
                        eng.matmul(dst, lt_h, rl, start=False, stop=False)
                        ins = eng.matmul(dst, lt_l, rh, start=False, stop=(hc == NHC - 1))
                    ins.then_inc(sem("a_vm"), 1)
                C.bump("a_vev")
                if en == "scalar":
                    eng.wait_ge(sem("a_vm"), C["a_vm"])
                    eng.copy(g_v_f[:, tc * 512:(tc + 1) * 512],
                             ptr[:, pb * 512:(pb + 1) * 512]).then_inc(sem("a_vev"), 1)

            C.bump("a_sp", 32)
            if en == "gpsimd":
                eng.wait_ge(sem("a_evh"), C["a_evh"])
                eng.wait_ge(sem("a_evl"), C["a_evl"])
                eng.dma_start(out=dr["xnt_h"].ap()[qb], in_=xnt_h[:]).then_inc(sem("a_sp"), 16)
                eng.dma_start(out=dr["xnt_l"].ap()[qb], in_=xnt_l[:]).then_inc(sem("a_sp"), 16)
            emit_conv_jobs(en, eng, C, [("g", qb * 16 + i) for i in range(16)])
            if qb < NQB - 1:
                if en in ("scalar", "vector"):
                    eng.wait_ge(sem("a_sp"), C["a_sp"])
                    eng.wait_ge(sem("a_vm"), C["a_vm"])
        if en == "gpsimd":
            eng.wait_ge(sem("a_sp"), C["a_sp"])

    def _emit_b1(en, eng, C, T):
        xnt_h, xnt_l = T["xnt_h"], T["xnt_l"]
        cq, sq = T["cq"], T["sq"]
        wqb_h, wqb_l = T["wqb_h"], T["wqb_l"]
        qf, qc, qs, qsw = T["qf"], T["qc"], T["qs"], T["qsw"]
        q_h, q_l = T["q_h"], T["q_l"]
        p_f, p_b, p_r = T["p_f"], T["p_b"], T["p_r"]
        ao_h, ao_l, ao_f = T["ao_h"], T["ao_l"], T["ao_f"]
        rr_b, rs_row, rr_row = T["rr_b"], T["rs_row"], T["rr_row"]
        negc, ones_b, ones_rf = T["negc"], T["ones_b"], T["ones_rf"]
        cmask, identb = T["cmask"], T["identb"]
        pS, pAO, pRS = T["pS"], T["pAO"], T["pRS"]
        psQ = T["psQ"][:, 0:512]

        C.bump("b_cl", 32)
        if en == "sync":
            eng.dma_start(out=cmask[:], in_=dr["cmask"].ap()[:, :]).then_inc(sem("b_cl"), 16)
            eng.dma_start(out=identb[:], in_=dr["identb"].ap()[:, :]).then_inc(sem("b_cl"), 16)
        C.bump("b_ic")
        if en == "gpsimd":
            eng.memset(negc[:], -CSHIFT)
            eng.memset(ones_b[:], 1.0)
            eng.memset(ones_rf[:], 1.0).then_inc(sem("b_ic"), 1)

        gg = 0
        for qb in range(NQB):
            C.bump("b_ld", 16 * 4)
            if en == "sync":
                if qb >= 1:
                    eng.wait_ge(sem("b_qbdone"), qb)
                eng.dma_start(out=xnt_h[:], in_=dr["xnt_h"].ap()[qb]).then_inc(sem("b_ld"), 16)
                eng.dma_start(out=xnt_l[:], in_=dr["xnt_l"].ap()[qb]).then_inc(sem("b_ld"), 16)
                eng.dma_start(out=cq[:], in_=dr["cosq"].ap()[:, qb * 512:(qb + 1) * 512]).then_inc(sem("b_ld"), 16)
                eng.dma_start(out=sq[:], in_=dr["sinq"].ap()[:, qb * 512:(qb + 1) * 512]).then_inc(sem("b_ld"), 16)
            C.bump("b_tl0", 32)
            if en == "sync":
                eng.dma_start(out=qc[:], in_=dr["cosq_lo"].ap()[:, qb * 512:(qb + 1) * 512]).then_inc(sem("b_tl0"), 16)
                eng.dma_start(out=qs[:], in_=dr["sinq_lo"].ap()[:, qb * 512:(qb + 1) * 512]).then_inc(sem("b_tl0"), 16)
            C.bump("b_tl1")
            if en == "vector":
                eng.wait_ge(sem("b_ld"), C["b_ld"])
                eng.wait_ge(sem("b_tl0"), C["b_tl0"])
                eng.tensor_add(cq[:], cq[:], qc[:]).then_inc(sem("b_tl1"), 1)
                eng.wait_ge(sem("b_tl1"), C["b_tl1"])
            C.bump("b_tl1")
            if en == "vector":
                eng.tensor_add(sq[:], sq[:], qs[:]).then_inc(sem("b_tl1"), 1)
            L = 4 * qb + 4
            groups = [list(range(L))[i:i + 2] for i in range(0, L, 2)]

            def emit_qproj(qh):
                kvh = qh // 4
                qpar = qh % 2
                wqk = "b_wq%d" % (qh % 2)
                C.bump(wqk, 32)
                if en == "sync":
                    if C[wqk] > 32:
                        eng.wait_ge(sem("b_qm"), C["b_qm"] - 1)
                    eng.dma_start(out=wqb_h[:, (qh % 2) * 2048:(qh % 2 + 1) * 2048
                                              ].rearrange("p (h j) -> p h j", j=128),
                                  in_=dr["wq_h"].ap()[qh].transpose([1, 0, 2])
                                  ).then_inc(sem(wqk), 16)
                    eng.dma_start(out=wqb_l[:, (qh % 2) * 2048:(qh % 2 + 1) * 2048
                                              ].rearrange("p (h j) -> p h j", j=128),
                                  in_=dr["wq_l"].ap()[qh].transpose([1, 0, 2])
                                  ).then_inc(sem(wqk), 16)
                C.bump("b_qm")
                if en == "tensor":
                    eng.wait_ge(sem("b_ld"), C["b_ld"])
                    eng.wait_ge(sem(wqk), C[wqk])
                    if C["b_qm"] > 1:
                        eng.wait_ge(sem("b_qev"), C["b_qm"] - 1)
                    first = True
                    for hc in range(NHC):
                        off = (qh % 2) * 2048 + hc * 128
                        wt_h = wqb_h[:, off: off + 128]
                        wt_l = wqb_l[:, off: off + 128]
                        rh = xnt_h[:, hc * 512:(hc + 1) * 512]
                        rl = xnt_l[:, hc * 512:(hc + 1) * 512]
                        eng.matmul(psQ, wt_h, rh, start=first, stop=False)
                        first = False
                        eng.matmul(psQ, wt_h, rl, start=False, stop=False)
                        ins = eng.matmul(psQ, wt_l, rh, start=False, stop=(hc == NHC - 1))
                    ins.then_inc(sem("b_qm"), 1)
                C.bump("b_qev")
                if en == "scalar":
                    eng.wait_ge(sem("b_qm"), C["b_qm"])
                    if C["b_qev"] > 1:
                        eng.wait_ge(sem("b_qrs"), C["b_qev"] - 1)
                    eng.copy(qf[:], psQ)
                    eng.copy(qsw[0:64, :], psQ[64:128, :])
                    eng.copy(qsw[64:128, :], psQ[0:64, :]).then_inc(sem("b_qev"), 1)
                C.bump("b_vv", 3)
                C.bump("b_qrs")
                if en == "vector":
                    if qh == 0:
                        eng.wait_ge(sem("b_tl1"), C["b_tl1"])
                    eng.wait_ge(sem("b_qev"), C["b_qev"])
                    eng.tensor_mul(qc[:], qf[:], cq[:])
                    eng.tensor_mul(qs[:], qsw[:], sq[:]).then_inc(sem("b_vv"), 1)
                    eng.wait_ge(sem("b_vv"), C["b_vv"] - 2)
                    eng.tensor_add(qc[:], qc[:], qs[:]).then_inc(sem("b_vv"), 1)
                    eng.wait_ge(sem("b_vv"), C["b_vv"] - 1)
                    qhsl = q_h[:, qpar * 512:(qpar + 1) * 512]
                    qlsl = q_l[:, qpar * 512:(qpar + 1) * 512]
                    eng.tensor_copy(qhsl, qc[:]).then_inc(sem("b_vv"), 1)
                    eng.wait_ge(sem("b_vv"), C["b_vv"])
                    eng.tensor_sub(qlsl, qc[:], qhsl).then_inc(sem("b_qrs"), 1)

            def emit_S(qh, gi, grp, gg):
                kvh = qh // 4
                qpar = qh % 2
                slot = gg % 2
                C.bump("b_S")
                if en == "tensor":
                    if gi == 0:
                        eng.wait_ge(sem("b_qrs"), C["b_qrs"])
                        if qb == 0 and qh == 0:
                            eng.wait_ge(sem("b_ic"), 1)
                            eng.wait_ge(sem("b_cl"), 32)
                    if gg > 2:
                        eng.wait_ge(sem("b_exp"), gg - 2)
                    for ti, tkc in enumerate(grp):
                        dst = pS[:, slot * 1024 + ti * 512: slot * 1024 + (ti + 1) * 512]
                        st = True
                        if tkc >= qb * 4:
                            o = tkc - qb * 4
                            eng.matmul(dst, identb[:], cmask[:, o * 512:(o + 1) * 512],
                                       start=True, stop=False)
                            st = False
                        kh = g_k_h[:, kvh * S + tkc * 128: kvh * S + (tkc + 1) * 128]
                        kl = g_k_l[:, kvh * S + tkc * 128: kvh * S + (tkc + 1) * 128]
                        qhsl = q_h[:, qpar * 512:(qpar + 1) * 512]
                        qlsl = q_l[:, qpar * 512:(qpar + 1) * 512]
                        eng.matmul(dst, kh, qhsl, start=st, stop=False)
                        eng.matmul(dst, kh, qlsl, start=False, stop=False)
                        ins = eng.matmul(dst, kl, qhsl, start=False, stop=True)
                    ins.then_inc(sem("b_S"), 1)
                C.bump("b_exp")
                if en == "scalar":
                    eng.wait_ge(sem("b_S"), C["b_S"])
                    if gg > 2:
                        eng.wait_ge(sem("b_pv"), gg - 2)
                        eng.wait_ge(sem("b_cast"), gg - 2)
                    n = len(grp)
                    eng.activation(p_f[:, slot * 1024: slot * 1024 + n * 512],
                                   pS[:, slot * 1024: slot * 1024 + n * 512],
                                   AF.Exp, bias=negc[:, 0:1]).then_inc(sem("b_exp"), 1)
                C.bump("b_vv")
                C.bump("b_cast")
                if en == "vector":
                    eng.wait_ge(sem("b_exp"), C["b_exp"])
                    n = len(grp)
                    eng.tensor_copy(p_b[:, slot * 1024: slot * 1024 + n * 512],
                                    p_f[:, slot * 1024: slot * 1024 + n * 512]
                                    ).then_inc(sem("b_vv"), 1)
                    eng.wait_ge(sem("b_vv"), C["b_vv"])
                    eng.tensor_sub(p_r[:, slot * 1024: slot * 1024 + n * 512],
                                   p_f[:, slot * 1024: slot * 1024 + n * 512],
                                   p_b[:, slot * 1024: slot * 1024 + n * 512]
                                   ).then_inc(sem("b_cast"), 1)

            def emit_PV(qh, gi, grp, gg, ng):
                kvh = qh // 4
                C.bump("b_pv")
                if en == "tensor":
                    eng.wait_ge(sem("b_exp"), C["b_exp_at%d" % gg])
                    if gi == 0:
                        eng.wait_ge(sem("b_aoev"), C.get("b_aoev", 0))
                    slot = gg % 2
                    for ti, tkc in enumerate(grp):
                        vt = g_v_f[:, tkc * 512 + kvh * 128: tkc * 512 + (kvh + 1) * 128]
                        eng.matmul(pAO[:, 0:512], vt,
                                   p_f[:, slot * 1024 + ti * 512: slot * 1024 + (ti + 1) * 512],
                                   start=(gi == 0 and ti == 0),
                                   stop=(gi == ng - 1 and ti == len(grp) - 1))
                    eng.wait_ge(sem("b_cast"), C["b_cast_at%d" % gg])
                    if gi == 0:
                        eng.wait_ge(sem("b_rrev"), C.get("b_rrev", 0))
                    for ti, tkc in enumerate(grp):
                        eng.matmul(pRS[0:1, 0:512], ones_b[:],
                                   p_b[:, slot * 1024 + ti * 512: slot * 1024 + (ti + 1) * 512],
                                   start=(gi == 0 and ti == 0), stop=False)
                        ins = eng.matmul(pRS[0:1, 0:512], ones_b[:],
                                         p_r[:, slot * 1024 + ti * 512: slot * 1024 + (ti + 1) * 512],
                                         start=False,
                                         stop=(gi == ng - 1 and ti == len(grp) - 1))
                    ins.then_inc(sem("b_pv"), 1)

            emit_qproj(0)
            for qh in range(HQ):
                ng = len(groups)
                pend = None
                for gi, grp in enumerate(groups):
                    gg = C.bump("b_gg")
                    C["b_exp_at%d" % gg] = 0
                    emit_S(qh, gi, grp, gg)
                    C["b_exp_at%d" % gg] = C["b_exp"]
                    C["b_cast_at%d" % gg] = C["b_cast"]
                    if pend is not None:
                        emit_PV(qh, pend[0], pend[1], pend[2], ng)
                    pend = (gi, grp, gg)
                if qh + 1 < HQ:
                    emit_qproj(qh + 1)
                emit_PV(qh, pend[0], pend[1], pend[2], ng)
                C.bump("b_rsev")
                if en == "scalar":
                    eng.wait_ge(sem("b_pv"), C["b_pv"])
                    eng.copy(rs_row[:], pRS[0:1, 0:512]).then_inc(sem("b_rsev"), 1)
                C.bump("b_rcp")
                if en == "vector":
                    eng.wait_ge(sem("b_rsev"), C["b_rsev"])
                    eng.reciprocal(rr_row[:], rs_row[:]).then_inc(sem("b_rcp"), 1)
                C.bump("b_bc")
                if en == "tensor":
                    eng.wait_ge(sem("b_rcp"), C["b_rcp"])
                    ins = eng.matmul(pRS[:, 0:512], ones_rf[0:1, :], rr_row[:],
                                     start=True, stop=True)
                    ins.then_inc(sem("b_bc"), 1)
                C.bump("b_rrev")
                if en == "scalar":
                    eng.wait_ge(sem("b_bc"), C["b_bc"])
                    eng.copy(rr_b[:], pRS[:, 0:512]).then_inc(sem("b_rrev"), 1)
                C.bump("b_aoev")
                if en == "scalar":
                    eng.wait_ge(sem("b_pv"), C["b_pv"])
                    if C["b_aoev"] > 1:
                        eng.wait_ge(sem("b_ao"), C["b_aoev"] - 1)
                    eng.copy(ao_f[:], pAO[:, 0:512]).then_inc(sem("b_aoev"), 1)
                C.bump("b_vv", 2)
                C.bump("b_ao")
                if en == "vector":
                    if qh == 0 and qb > 0:
                        eng.wait_ge(sem("b_aosp"), C["b_aosp"])
                    eng.wait_ge(sem("b_rrev"), C["b_rrev"])
                    eng.wait_ge(sem("b_aoev"), C["b_aoev"])
                    eng.tensor_mul(ao_f[:], ao_f[:], rr_b[:]).then_inc(sem("b_vv"), 1)
                    eng.wait_ge(sem("b_vv"), C["b_vv"] - 1)
                    dh = ao_h[:, qh * 512:(qh + 1) * 512]
                    dl = ao_l[:, qh * 512:(qh + 1) * 512]
                    eng.tensor_copy(dh, ao_f[:]).then_inc(sem("b_vv"), 1)
                    eng.wait_ge(sem("b_vv"), C["b_vv"])
                    eng.tensor_sub(dl, ao_f[:], dh).then_inc(sem("b_ao"), 1)
            C.bump("b_aosp", 32)
            if en == "gpsimd":
                eng.wait_ge(sem("b_ao"), C["b_ao"])
                eng.dma_start(out=dr["ao_h"].ap()[qb], in_=ao_h[:]).then_inc(sem("b_aosp"), 16)
                eng.dma_start(out=dr["ao_l"].ap()[qb], in_=ao_l[:]).then_inc(sem("b_aosp"), 16)
            C.bump("b_qbdone")
            if en == "gpsimd":
                eng.wait_ge(sem("b_aosp"), C["b_aosp"])
                eng.sem_inc(sem("b_qbdone"), 1)
            emit_conv_jobs(en, eng, C,
                           [("u", qb * 16 + i) for i in range(16)]
                           + [("d", qb * 16 + i) for i in range(16)])


        if en == "gpsimd":
            eng.wait_ge(sem("cv_o"), C.get("cv_o", 0))

    def _emit_b2(en, eng, C, T):
        aoh, aol = T["aoh"], T["aol"]
        wob_h, wob_l = T["wob_h"], T["wob_l"]
        s_sb, pWo = T["s_sb"], T["pWo"]
        for qb in range(NQB):
            for eh in range(2):
                it = qb * 2 + eh
                C.bump("w_ao", 32)
                if en == "sync":
                    if it >= 1:
                        eng.wait_ge(sem("w_done"), it)
                    eng.dma_start(out=aoh[:], in_=dr["ao_h"].ap()[qb]).then_inc(sem("w_ao"), 16)
                    eng.dma_start(out=aol[:], in_=dr["ao_l"].ap()[qb]).then_inc(sem("w_ao"), 16)
                for dc in range(NHC):
                    wb = dc % 2
                    wlk = "w_wl%d" % wb
                    C.bump(wlk, 32)
                    if en == "gpsimd":
                        if C[wlk] > 32:
                            eng.wait_ge(sem("w_mm"), C.get("w_mm", 0) - 1)
                        eng.dma_start(out=wob_h[:, wb * 1024:(wb + 1) * 1024],
                                      in_=dr["wo_h"].ap()[dc][:, eh * 1024:(eh + 1) * 1024]
                                      ).then_inc(sem(wlk), 16)
                        eng.dma_start(out=wob_l[:, wb * 1024:(wb + 1) * 1024],
                                      in_=dr["wo_l"].ap()[dc][:, eh * 1024:(eh + 1) * 1024]
                                      ).then_inc(sem(wlk), 16)
                    C.bump("w_mm")
                    if en == "tensor":
                        eng.wait_ge(sem("w_ao"), C["w_ao"])
                        eng.wait_ge(sem(wlk), C[wlk])
                        if dc == 0:
                            eng.wait_ge(sem("w_ev"), C.get("w_ev", 0))
                        for tqc in range(4):
                            ah = aoh[:, dc * 512 + tqc * 128: dc * 512 + (tqc + 1) * 128]
                            al = aol[:, dc * 512 + tqc * 128: dc * 512 + (tqc + 1) * 128]
                            d0 = pWo[:, tqc * 1024: tqc * 1024 + 512]
                            d1 = pWo[:, tqc * 1024 + 512: (tqc + 1) * 1024]
                            wh0 = wob_h[:, wb * 1024: wb * 1024 + 512]
                            wh1 = wob_h[:, wb * 1024 + 512: (wb + 1) * 1024]
                            wl0 = wob_l[:, wb * 1024: wb * 1024 + 512]
                            wl1 = wob_l[:, wb * 1024 + 512: (wb + 1) * 1024]
                            st = dc == 0
                            sp = dc == NHC - 1
                            eng.matmul(d0, ah, wh0, start=st, stop=False)
                            eng.matmul(d1, ah, wh1, start=st, stop=False)
                            eng.matmul(d0, ah, wl0, start=False, stop=False)
                            eng.matmul(d1, ah, wl1, start=False, stop=False)
                            eng.matmul(d0, al, wh0, start=False, stop=sp)
                            ins = eng.matmul(d1, al, wh1, start=False, stop=sp)
                        ins.then_inc(sem("w_mm"), 1)
                for tqc in range(4):
                    C.bump("w_ev")
                    if en == "scalar":
                        eng.wait_ge(sem("w_mm"), C["w_mm"])
                        if eh == 0 and qb > 0 and tqc == 0:
                            eng.wait_ge(sem("w_sout"), C.get("w_sout", 0))
                        eng.copy(s_sb[:, tqc * 2048 + eh * 1024: tqc * 2048 + (eh + 1) * 1024],
                                 pWo[:, tqc * 1024:(tqc + 1) * 1024]).then_inc(sem("w_ev"), 1)
                if eh == 1:
                    for tqc in range(4):
                        tc = qb * 4 + tqc
                        C.bump("w_sout", 16)
                        if en == "gpsimd":
                            eng.wait_ge(sem("w_ev"), C["w_ev"] - 3 + tqc)
                            eng.dma_start(out=dr["s_s"].ap()[tc],
                                          in_=s_sb[:, tqc * 2048:(tqc + 1) * 2048]
                                          ).then_inc(sem("w_sout"), 16)
                C.bump("w_done")
                if en == "gpsimd":
                    if eh == 1:
                        eng.wait_ge(sem("w_sout"), C["w_sout"])
                    else:
                        eng.wait_ge(sem("w_ev"), C["w_ev"])
                    eng.sem_inc(sem("w_done"), 1)
        if en == "gpsimd":
            eng.wait_ge(sem("w_sout"), C["w_sout"])

    def _emit_c(en, eng, C, T):
        xnt_h, xnt_l = T["xnt_h"], T["xnt_l"]
        gb, ub, db = T["gb"], T["ub"], T["db"]
        h_h, h_l = T["h_h"], T["h_l"]
        sg_s, hf_s, facc = T["sg_s"], T["hf_s"], T["facc"]
        gsc, usc, dscb = T["gsc"], T["usc"], T["dscb"]
        hs_row, negh = T["hs_row"], T["negh"]
        onesb, onesf1 = T["onesb"], T["onesf1"]
        st_t, xt_t, ep_g = T["st_t"], T["xt_t"], T["ep_g"]
        ps = T["ps"]

        C.bump("c_ic", 16 * 3)
        if en == "sync":
            eng.dma_start(out=gsc[:], in_=dr["gsc"].ap()[:, :]).then_inc(sem("c_ic"), 16)
            eng.dma_start(out=usc[:], in_=dr["usc"].ap()[:, :]).then_inc(sem("c_ic"), 16)
            eng.dma_start(out=dscb[:], in_=dr["dscb"].ap()[:, :]).then_inc(sem("c_ic"), 16)
        C.bump("c_ic2")
        if en == "gpsimd":
            eng.memset(onesb[:], 1.0)
            eng.memset(onesf1[:], 1.0).then_inc(sem("c_ic2"), 1)

        def emit_epi0(fb):
            C.bump("c_hst", 2)
            if en == "vector":
                eng.wait_ge(sem("c_hsev"), C["c_hsev"])
                eng.tensor_add(hs_row[0:1, 0:512], hs_row[0:1, 0:512],
                               hs_row[0:1, 512:1024]).then_inc(sem("c_hst"), 1)
                eng.wait_ge(sem("c_hst"), C["c_hst"] - 1)
                eng.tensor_scalar_mul(hs_row[0:1, 0:512], hs_row[0:1, 0:512],
                                      -128.0).then_inc(sem("c_hst"), 1)
            for tcl in range(4):
                C.bump("c_ht")
                if en == "tensor":
                    eng.wait_ge(sem("c_hst"), C["c_hst"])
                    eng.wait_ge(sem("c_devs"), C["c_devs"])
                    eng.wait_ge(sem("c_devv"), C["c_devv"])
                    ins = eng.transpose(ps[:, 3584 + tcl * 128: 3584 + tcl * 128 + 1],
                                        hs_row[0:1, tcl * 128:(tcl + 1) * 128],
                                        onesf1[0:1, 0:1])
                    ins.then_inc(sem("c_ht"), 1)
                C.bump("c_htev")
                if en == "scalar":
                    eng.wait_ge(sem("c_ht"), C["c_ht"])
                    eng.copy(negh[:, tcl:tcl + 1],
                             ps[:, 3584 + tcl * 128: 3584 + tcl * 128 + 1]).then_inc(sem("c_htev"), 1)

        def emit_epi_t(fb, tcl):
            tc = fb * 4 + tcl
            n_sx = C.bump("c_sx", 32) // 32
            if en == "gpsimd":
                if n_sx > 1:
                    eng.wait_ge(sem("c_ep3"), C.get("c_ep3", 0))
                eng.dma_start(out=st_t[:],
                              in_=dr["s_s"].ap()[tc]).then_inc(sem("c_sx"), 16)
                eng.dma_start(out=xt_t[:],
                              in_=dr["x"].ap()[tc * 128:(tc + 1) * 128, :]).then_inc(sem("c_sx"), 16)
            C.bump("c_vv")
            C.bump("c_ep1")
            if en == "vector":
                eng.wait_ge(sem("c_htev"), C["c_htev"])
                fsl = facc[:, tcl * 2048:(tcl + 1) * 2048]
                eng.tensor_scalar_add(fsl, fsl, negh[:, tcl:tcl + 1]).then_inc(sem("c_vv"), 1)
                eng.wait_ge(sem("c_vv"), C["c_vv"])
                eng.tensor_mul(fsl, fsl, dscb[:]).then_inc(sem("c_ep1"), 1)
            C.bump("c_ep2")
            if en == "scalar":
                eng.wait_ge(sem("c_ep1"), C["c_ep1"])
                eng.wait_ge(sem("c_out"), C.get("c_out", 0))
                eng.activation(ep_g[:], facc[:, tcl * 2048:(tcl + 1) * 2048],
                               AF.Silu).then_inc(sem("c_ep2"), 1)
            C.bump("c_vv")
            C.bump("c_ep3")
            if en == "vector":
                eng.wait_ge(sem("c_ep2"), C["c_ep2"])
                eng.wait_ge(sem("c_sx"), C["c_sx"])
                eng.tensor_mul(ep_g[:], st_t[:], ep_g[:]).then_inc(sem("c_vv"), 1)
                eng.wait_ge(sem("c_vv"), C["c_vv"])
                eng.tensor_add(ep_g[:], ep_g[:], xt_t[:]).then_inc(sem("c_ep3"), 1)
            C.bump("c_out", 16)
            if en == "gpsimd":
                eng.wait_ge(sem("c_ep3"), C["c_ep3"])
                eng.dma_start(out=dr["out"].ap()[tc * 128:(tc + 1) * 128, :],
                              in_=ep_g[:]).then_inc(sem("c_out"), 16)

        pend_epi = [None]
        for fb in range(C_NFB):
            C.bump("c_xl", 32)
            if en == "sync":
                if fb >= 1:
                    eng.wait_ge(sem("c_gdone"), 2 * fb)
                eng.dma_start(out=xnt_h[:], in_=dr["xnt_h"].ap()[fb]).then_inc(sem("c_xl"), 16)
                eng.dma_start(out=xnt_l[:], in_=dr["xnt_l"].ap()[fb]).then_inc(sem("c_xl"), 16)
            for ih in range(2):
                for il in range(32):
                    ic = ih * 32 + il
                    wbg = il % 2
                    gwk = "c_gw%d" % wbg
                    C.bump(gwk, 32)
                    if en == "sync":
                        if C[gwk] > 32:
                            eng.wait_ge(sem("c_gmm"), C.get("c_gmm", 0) - 1)
                        eng.dma_start(out=gb[:, wbg * 2048:(wbg + 1) * 2048],
                                      in_=dr["g_b"].ap()[ic]).then_inc(sem(gwk), 16)
                        eng.dma_start(out=ub[:, wbg * 2048:(wbg + 1) * 2048],
                                      in_=dr["u_b"].ap()[ic]).then_inc(sem(gwk), 16)
                    gpar = il % 2
                    C.bump("c_gmm")
                    if en == "tensor":
                        eng.wait_ge(sem(gwk), C[gwk])
                        eng.wait_ge(sem("c_xl"), C["c_xl"])
                        if il == 0:
                            eng.wait_ge(sem("c_devs"), C.get("c_devs", 0))
                            eng.wait_ge(sem("c_devv"), C.get("c_devv", 0))
                        if C["c_gmm"] > 2:
                            eng.wait_ge(sem("c_h"), C["c_gmm"] - 2)
                        dstg = ps[:, gpar * 512:(gpar + 1) * 512]
                        dstu = ps[:, 1024 + gpar * 512: 1024 + (gpar + 1) * 512]
                        for hc in range(NHC):
                            wt_g = gb[:, wbg * 2048 + hc * 128: wbg * 2048 + (hc + 1) * 128]
                            wt_u = ub[:, wbg * 2048 + hc * 128: wbg * 2048 + (hc + 1) * 128]
                            rh = xnt_h[:, hc * 512:(hc + 1) * 512]
                            rl = xnt_l[:, hc * 512:(hc + 1) * 512]
                            eng.matmul(dstg, wt_g, rh, start=(hc == 0), stop=False)
                            eng.matmul(dstg, wt_g, rl, start=False, stop=(hc == NHC - 1))
                            eng.matmul(dstu, wt_u, rh, start=(hc == 0), stop=False)
                            ins = eng.matmul(dstu, wt_u, rl, start=False, stop=(hc == NHC - 1))
                        ins.then_inc(sem("c_gmm"), 1)
                    C.bump("c_sg")
                    if en == "scalar":
                        eng.wait_ge(sem("c_gmm"), C["c_gmm"])
                        if fb == 0 and ic == 0:
                            eng.wait_ge(sem("c_ic"), 48)
                        if C["c_sg"] > 2:
                            eng.wait_ge(sem("c_h"), C["c_sg"] - 2)
                        eng.activation(sg_s[:, gpar * 512:(gpar + 1) * 512],
                                       ps[:, gpar * 512:(gpar + 1) * 512],
                                       AF.Silu, scale=gsc[:, ic:ic + 1]).then_inc(sem("c_sg"), 1)
                    C.bump("c_vv", 2)
                    C.bump("c_h")
                    if en == "vector":
                        eng.wait_ge(sem("c_sg"), C["c_sg"])
                        if fb == 0 and ic == 0:
                            eng.wait_ge(sem("c_ic"), 48)
                        if C.get("c_dmm_ph", 0) > 0 and il == 0:
                            eng.wait_ge(sem("c_dmm"), C["c_dmm_ph"])
                        eng.scalar_tensor_tensor(hf_s[:, gpar * 512:(gpar + 1) * 512],
                                                 ps[:, 1024 + gpar * 512: 1024 + (gpar + 1) * 512],
                                                 usc[:, ic:ic + 1],
                                                 sg_s[:, gpar * 512:(gpar + 1) * 512],
                                                 ALU.mult, ALU.mult).then_inc(sem("c_vv"), 1)
                        eng.wait_ge(sem("c_vv"), C["c_vv"] - 1)
                        dh = h_h[:, il * 512:(il + 1) * 512]
                        dl = h_l[:, il * 512:(il + 1) * 512]
                        eng.tensor_copy(dh, hf_s[:, gpar * 512:(gpar + 1) * 512]).then_inc(sem("c_vv"), 1)
                        eng.wait_ge(sem("c_vv"), C["c_vv"])
                        eng.tensor_sub(dl, hf_s[:, gpar * 512:(gpar + 1) * 512],
                                       dh).then_inc(sem("c_h"), 1)
                    if ih == 0 and pend_epi[0] is not None:
                        if il == 2:
                            emit_epi0(pend_epi[0])
                        elif il in (5, 8, 11, 14):
                            emit_epi_t(pend_epi[0], (il - 5) // 3)
                            if il == 14:
                                pend_epi[0] = None
                    C.bump("c_hs")
                    if en == "tensor":
                        eng.wait_ge(sem("c_h"), C["c_h"])
                        if fb == 0 and ic == 0:
                            eng.wait_ge(sem("c_ic2"), 1)
                        if il == 0:
                            eng.wait_ge(sem("c_hsev"), C.get("c_hsev", 0))
                        dh = h_h[:, il * 512:(il + 1) * 512]
                        dl = h_l[:, il * 512:(il + 1) * 512]
                        eng.matmul(ps[0:1, 2048:2560], onesb[:], dh,
                                   start=(il == 0), stop=False)
                        ins = eng.matmul(ps[0:1, 2048:2560], onesb[:], dl,
                                         start=False, stop=(il == 31))
                        ins.then_inc(sem("c_hs"), 1)
                C.bump("c_hsev")
                if en == "scalar":
                    eng.wait_ge(sem("c_hs"), C["c_hs"])
                    eng.wait_ge(sem("c_hst"), C.get("c_hst", 0))
                    eng.copy(hs_row[0:1, ih * 512:(ih + 1) * 512],
                             ps[0:1, 2048:2560]).then_inc(sem("c_hsev"), 1)
                C.bump("c_gdone")
                if en == "tensor":
                    eng.nop().then_inc(sem("c_gdone"), 1)
                for hhh in range(2):
                    for il in range(32):
                        ic = ih * 32 + il
                        wbd = il % 2
                        dwk = "c_dw%d" % wbd
                        C.bump(dwk, 16)
                        if en == "gpsimd":
                            if C[dwk] > 16:
                                eng.wait_ge(sem("c_dmm"), C.get("c_dmm", 0) - 1)
                            src = dr["d_b"].ap()[ic][:, hhh * 1024:(hhh + 1) * 1024]
                            eng.dma_start(out=db[:, wbd * 1024:(wbd + 1) * 1024],
                                          in_=src).then_inc(sem(dwk), 16)
                        C.bump("c_dmm")
                        if en == "tensor":
                            eng.wait_ge(sem(dwk), C[dwk])
                            eng.wait_ge(sem("c_h"), C["c_h"])
                            if il == 0:
                                eng.wait_ge(sem("c_devs"), C.get("c_devs", 0))
                                eng.wait_ge(sem("c_devv"), C.get("c_devv", 0))
                                eng.wait_ge(sem("c_hsev"), C["c_hsev"])
                                eng.wait_ge(sem("c_htev"), C.get("c_htev", 0))
                            for tcl in range(4):
                                lt_h = h_h[:, il * 512 + tcl * 128: il * 512 + (tcl + 1) * 128]
                                lt_l = h_l[:, il * 512 + tcl * 128: il * 512 + (tcl + 1) * 128]
                                d0 = ps[:, tcl * 1024: tcl * 1024 + 512]
                                d1 = ps[:, tcl * 1024 + 512: (tcl + 1) * 1024]
                                st = il == 0
                                sp = il == 31
                                eng.matmul(d0, lt_h, db[:, wbd * 1024: wbd * 1024 + 512],
                                           start=st, stop=False)
                                eng.matmul(d1, lt_h, db[:, wbd * 1024 + 512:(wbd + 1) * 1024],
                                           start=st, stop=False)
                                eng.matmul(d0, lt_l, db[:, wbd * 1024: wbd * 1024 + 512],
                                           start=False, stop=sp)
                                ins = eng.matmul(d1, lt_l, db[:, wbd * 1024 + 512:(wbd + 1) * 1024],
                                                 start=False, stop=sp)
                            ins.then_inc(sem("c_dmm"), 1)
                    for tcl in range(4):
                        if ih == 0:
                            C.bump("c_devs")
                            if en == "scalar":
                                eng.wait_ge(sem("c_dmm"), C["c_dmm"])
                                if fb > 0 and hhh == 0 and tcl == 0:
                                    eng.wait_ge(sem("c_ep2"), 4 * fb)
                                eng.copy(facc[:, tcl * 2048 + hhh * 1024:
                                              tcl * 2048 + (hhh + 1) * 1024],
                                         ps[:, tcl * 1024:(tcl + 1) * 1024]).then_inc(sem("c_devs"), 1)
                        else:
                            C.bump("c_devv")
                            if en == "vector":
                                eng.wait_ge(sem("c_dmm"), C["c_dmm"])
                                dstf = facc[:, tcl * 2048 + hhh * 1024:
                                            tcl * 2048 + (hhh + 1) * 1024]
                                eng.tensor_add(dstf, ps[:, tcl * 1024:(tcl + 1) * 1024],
                                               dstf).then_inc(sem("c_devv"), 1)
                C["c_dmm_ph"] = C["c_dmm"]
            pend_epi[0] = fb
            if fb == C_NFB - 1:
                emit_epi0(fb)
                for tcl in range(4):
                    emit_epi_t(fb, tcl)
                pend_epi[0] = None
        if en == "gpsimd":
            eng.wait_ge(sem("c_out"), C["c_out"])

    if "p0" in PHASES:
        with contextlib.ExitStack() as es:
            sb = lambda n, s, d: es.enter_context(nc.sbuf_tensor(n, s, d))
            T = {
                "wtmp": sb("p0_wtmp", [128, 4096], F32),
                "whi": sb("p0_whi", [128, 4096], BF16),
                "wlo": sb("p0_wlo", [128, 4096], BF16),
            }
            chunks = []
            for hc in range(NHC):
                src = dr["wq"].ap()[hc * 128:(hc + 1) * 128, :]
                dsth = dr["wq_h"].ap()[:, hc].transpose([1, 0, 2])
                dstl = dr["wq_l"].ap()[:, hc].transpose([1, 0, 2])
                chunks.append(("wq", src, 2048, dsth, dstl))
            for w, cols in (("wk", 512), ("wv", 512), ("wo", 2048)):
                for hc in range(NHC):
                    src = dr[w].ap()[hc * 128:(hc + 1) * 128, :]
                    chunks.append((w, src, cols, dr[w + "_h"].ap()[hc], dr[w + "_l"].ap()[hc]))
            T["chunks"] = chunks
            with nc.Block() as blk:
                mk_emit(blk, lambda en, eng, C: _emit_p0(en, eng, C, T))

    if "a" in PHASES:
        with contextlib.ExitStack() as es:
            sb = lambda n, s, d: es.enter_context(nc.sbuf_tensor(n, s, d))
            pt = lambda n, s: es.enter_context(nc.psum_tensor(n, s, F32))
            T = {
                "xt": sb("a_xt", [128, 4096], F32),
                "xn": sb("a_xn", [128, 2048], F32),
                "xnt_h": sb("a_xnth", [128, 8192], BF16),
                "xnt_l": sb("a_xntl", [128, 8192], BF16),
                "wbh": sb("a_wbh", [128, 8192], BF16),
                "wbl": sb("a_wbl", [128, 8192], BF16),
                "ck": sb("a_ck", [128, 2048], F32),
                "sk": sb("a_sk", [128, 2048], F32),
                "stats": sb("a_stats", [128, 8], F32),
                "kf": sb("a_kf", [128, 512], F32),
                "kc": sb("a_kc", [128, 512], F32),
                "ks": sb("a_ks", [128, 512], F32),
                "ksw": sb("a_ksw", [128, 512], F32),
                "identf": sb("a_identf", [128, 128], F32),
                "tbuf": sb("a_tbuf", [128, 2048], F32),
                "ptr": pt("a_ptr", [128, 2048]),
                "pkv": pt("a_pkv", [128, 1024]),
            }
            with nc.Block() as blk:
                mk_emit(blk, lambda en, eng, C: _emit_a(en, eng, C, T))

    if "b1" in PHASES:
        with contextlib.ExitStack() as es:
            sb = lambda n, s, d: es.enter_context(nc.sbuf_tensor(n, s, d))
            pt = lambda n, s: es.enter_context(nc.psum_tensor(n, s, F32))
            T = {
                "xnt_h": sb("b_xnth", [128, 8192], BF16),
                "xnt_l": sb("b_xntl", [128, 8192], BF16),
                "cq": sb("b_cq", [128, 512], F32),
                "sq": sb("b_sq", [128, 512], F32),
                "wqb_h": sb("b_wqbh", [128, 4096], BF16),
                "wqb_l": sb("b_wqbl", [128, 4096], BF16),
                "qf": sb("b_qf", [128, 512], F32),
                "qc": sb("b_qc", [128, 512], F32),
                "qs": sb("b_qs", [128, 512], F32),
                "qsw": sb("b_qsw", [128, 512], F32),
                "q_h": sb("b_qh", [128, 1024], BF16),
                "q_l": sb("b_ql", [128, 1024], BF16),
                "p_f": sb("b_pf", [128, 2048], F32),
                "p_b": sb("b_pb", [128, 2048], BF16),
                "p_r": sb("b_pr", [128, 2048], BF16),
                "ao_h": sb("b_aoh", [128, HQ * 512], BF16),
                "ao_l": sb("b_aol", [128, HQ * 512], BF16),
                "ao_f": sb("b_aof", [128, 512], F32),
                "rr_b": sb("b_rrb", [128, 512], F32),
                "rs_row": sb("b_rsrow", [1, 512], F32),
                "rr_row": sb("b_rrrow", [1, 512], F32),
                "negc": sb("b_negc", [128, 1], F32),
                "ones_b": sb("b_onesb", [128, 1], BF16),
                "ones_rf": sb("b_onesrf", [1, 128], F32),
                "cmask": sb("b_cmask", [128, 2048], BF16),
                "identb": sb("b_identb", [128, 128], BF16),
                "pS": pt("b_pS", [128, 2048]),
                "psQ": pt("b_psQ", [128, 512]),
                "pAO": pt("b_pAO", [128, 512]),
                "pRS": pt("b_pRS", [128, 512]),
            }
            with nc.Block() as blk:
                mk_emit(blk, lambda en, eng, C: _emit_b1(en, eng, C, T))

    if "devkv" in PHASES:
        dkh = nc.dram_tensor("dev_kh", [128, NKV * S], BF16, kind="ExternalOutput")
        dkl = nc.dram_tensor("dev_kl", [128, NKV * S], BF16, kind="ExternalOutput")
        dvf = nc.dram_tensor("dev_vf", [128, NTC * 512], F32, kind="ExternalOutput")
        with nc.Block() as blk:
            @blk.gpsimd
            def _(eng):
                eng.dma_start(out=dkh.ap()[:, :], in_=g_k_h[:]).then_inc(sem("devkv"), 16)
                eng.dma_start(out=dkl.ap()[:, :], in_=g_k_l[:]).then_inc(sem("devkv"), 16)
                eng.dma_start(out=dvf.ap()[:, :], in_=g_v_f[:]).then_inc(sem("devkv"), 16)
                eng.wait_ge(sem("devkv"), 48)

    pstack.close()

    if "b2" in PHASES:
        with contextlib.ExitStack() as es:
            sb = lambda n, s, d: es.enter_context(nc.sbuf_tensor(n, s, d))
            pt = lambda n, s: es.enter_context(nc.psum_tensor(n, s, F32))
            T = {
                "aoh": sb("w_aoh", [128, HQ * 512], BF16),
                "aol": sb("w_aol", [128, HQ * 512], BF16),
                "wob_h": sb("w_wobh", [128, 2048], BF16),
                "wob_l": sb("w_wobl", [128, 2048], BF16),
                "s_sb": sb("w_ssb", [128, 8192], F32),
                "pWo": pt("w_pWo", [128, 4096]),
            }
            with nc.Block() as blk:
                mk_emit(blk, lambda en, eng, C: _emit_b2(en, eng, C, T))

    if "c" in PHASES:
        with contextlib.ExitStack() as es:
            sb = lambda n, s, d: es.enter_context(nc.sbuf_tensor(n, s, d))
            pt = lambda n, s: es.enter_context(nc.psum_tensor(n, s, F32))
            T = {
                "xnt_h": sb("c_xnth", [128, 8192], BF16),
                "xnt_l": sb("c_xntl", [128, 8192], BF16),
                "gb": sb("c_gb", [128, 4096], BF16),
                "ub": sb("c_ub", [128, 4096], BF16),
                "db": sb("c_db", [128, 2048], BF16),
                "h_h": sb("c_hh", [128, 32 * 512], BF16),
                "h_l": sb("c_hl", [128, 32 * 512], BF16),
                "sg_s": sb("c_sgs", [128, 1024], F32),
                "hf_s": sb("c_hfs", [128, 1024], F32),
                "facc": sb("c_facc", [128, 4 * 2048], F32),
                "gsc": sb("c_gsc", [128, NIC], F32),
                "usc": sb("c_usc", [128, NIC], F32),
                "dscb": sb("c_dscb", [128, E], F32),
                "hs_row": sb("c_hsrow", [1, 1024], F32),
                "negh": sb("c_negh", [128, 4], F32),
                "onesb": sb("c_onesb", [128, 1], BF16),
                "onesf1": sb("c_onesf1", [1, 1], F32),
                "st_t": sb("c_stt", [128, 2048], F32),
                "xt_t": sb("c_xtt", [128, 2048], F32),
                "ep_g": sb("c_epg", [128, 2048], F32),
                "ps": pt("c_ps", [128, 4096]),
            }
            with nc.Block() as blk:
                mk_emit(blk, lambda en, eng, C: _emit_c(en, eng, C, T))

    return nc


def kernel(**inputs):
    in_maps = _host_prep(**inputs)
    nc = build_nc()
    res = run_bass_kernel_spmd(nc, in_maps, core_ids=list(range(8)))
    return np.stack([np.asarray(res.results[c]["out"]) for c in range(8)], axis=0)


# version 11
# speedup vs baseline: 1.0098x; 1.0098x over previous
import contextlib
import numpy as np
import ml_dtypes
import concourse.bass as bass
import concourse.mybir as mybir
from concourse.bass_utils import run_bass_kernel_spmd

F32 = mybir.dt.float32
BF16 = mybir.dt.bfloat16
I32 = mybir.dt.int32
AF = mybir.ActivationFunctionType
ALU = mybir.AluOpType
AX = mybir.AxisListType.X
bfdt = ml_dtypes.bfloat16

S = 2048
E = 2048
HQ = 16
NKV = 4
D = 128
II = 8192
NTC = 16
NQB = 4
NHC = 16
NIC = 64
CSHIFT = 8.0
EPS = 1e-5
RSQD = float(1.0 / np.sqrt(128.0))

ENGINES = ("sync", "tensor", "scalar", "vector", "gpsimd")
PHASES = ("p0", "a", "b1", "b2", "c")
C_NFB = NQB
DEV_OUT = ()



_TAB_CACHE = {}


def _rope_tables_f32():
    if "cs" not in _TAB_CACHE:
        import subprocess
        import sys
        import tempfile
        code = (
            "import jax\n"
            "jax.config.update('jax_platforms', 'cpu')\n"
            "jax.config.update('jax_enable_x64', True)\n"
            "import numpy as np, jax.numpy as jnp, sys\n"
            "inv = 1.0 / 10000.0 ** (jnp.arange(0, 128, 2, dtype=jnp.float32) / 128)\n"
            "t = jnp.arange(2048, dtype=jnp.float32)\n"
            "fr = jnp.outer(t, inv)\n"
            "np.save(sys.argv[1], np.stack([np.asarray(jnp.cos(fr)), np.asarray(jnp.sin(fr))]))\n"
        )
        with tempfile.TemporaryDirectory() as td:
            p = td + "/t.npy"
            subprocess.run([sys.executable, "-c", code, p], check=True,
                           capture_output=True)
            cs = np.load(p)
        _TAB_CACHE["cs"] = (cs[0].astype(np.float32), cs[1].astype(np.float32))
    return _TAB_CACHE["cs"]


def _host_prep(L0_attn_x, L0_attn_Wq, L0_attn_Wk, L0_attn_Wv, L0_attn_Wo,
               L0_ffn_gate_w, L0_ffn_gate_scale, L0_ffn_up_w, L0_ffn_up_scale,
               L0_ffn_down_w, L0_ffn_down_scale):
    cm = np.ascontiguousarray
    x = np.asarray(L0_attn_x, dtype=np.float32)
    shared = {
        "wq": cm(np.asarray(L0_attn_Wq, np.float32)),
        "wk": cm(np.asarray(L0_attn_Wk, np.float32)),
        "wv": cm(np.asarray(L0_attn_Wv, np.float32)),
        "wo": cm(np.asarray(L0_attn_Wo, np.float32)),
        "gwt": cm(np.asarray(L0_ffn_gate_w, np.int32).T),
        "uwt": cm(np.asarray(L0_ffn_up_w, np.int32).T),
        "dwt": cm(np.asarray(L0_ffn_down_w, np.int32).T),
        "gsc": cm(np.asarray(L0_ffn_gate_scale, np.float32).reshape(NIC, 128).T),
        "usc": cm(np.asarray(L0_ffn_up_scale, np.float32).reshape(NIC, 128).T),
        "dscb": cm(np.broadcast_to(
            np.asarray(L0_ffn_down_scale, np.float32)[None, :], (128, E))),
    }
    cos32, sin32 = _rope_tables_f32()
    cos = np.concatenate([cos32, cos32], axis=1).astype(np.float64)
    sinn = np.concatenate([-sin32, sin32], axis=1).astype(np.float64)
    for nm, tab in (("cosk", cos.T), ("sink", sinn.T),
                    ("cosq", cos.T * RSQD), ("sinq", sinn.T * RSQD)):
        hi = tab.astype(np.float32)
        shared[nm] = cm(hi)
        shared[nm + "_lo"] = cm((tab - hi.astype(np.float64)).astype(np.float32))
    cmask = np.zeros((128, 4 * 512), dtype=np.float32)
    tk = np.arange(128)[:, None]
    tq = np.arange(512)[None, :]
    for p in range(4):
        cmask[:, p * 512:(p + 1) * 512] = np.where(tq >= p * 128 + tk, 0.0, -1e30)
    shared["cmask"] = cm(cmask.astype(bfdt))
    shared["identb"] = cm(np.eye(128, dtype=bfdt))
    shared["identf"] = cm(np.eye(128, dtype=np.float32))
    return [dict(shared, x=cm(x[c])) for c in range(8)]


class Ctr(dict):
    def bump(self, k, n=1):
        self[k] = self.get(k, 0) + n
        return self[k]


def build_nc():
    nc = bass.Bass("TRN2", target_bir_lowering=False)

    dr = {}
    dr["x"] = nc.dram_tensor("x", [S, E], F32, kind="ExternalInput")
    dr["wq"] = nc.dram_tensor("wq", [E, E], F32, kind="ExternalInput")
    dr["wk"] = nc.dram_tensor("wk", [E, 512], F32, kind="ExternalInput")
    dr["wv"] = nc.dram_tensor("wv", [E, 512], F32, kind="ExternalInput")
    dr["wo"] = nc.dram_tensor("wo", [E, E], F32, kind="ExternalInput")
    dr["gwt"] = nc.dram_tensor("gwt", [E, II], I32, kind="ExternalInput")
    dr["uwt"] = nc.dram_tensor("uwt", [E, II], I32, kind="ExternalInput")
    dr["dwt"] = nc.dram_tensor("dwt", [II, E], I32, kind="ExternalInput")
    dr["gsc"] = nc.dram_tensor("gsc", [128, NIC], F32, kind="ExternalInput")
    dr["usc"] = nc.dram_tensor("usc", [128, NIC], F32, kind="ExternalInput")
    dr["dscb"] = nc.dram_tensor("dscb", [128, E], F32, kind="ExternalInput")
    for t in ("cosk", "sink", "cosq", "sinq"):
        dr[t] = nc.dram_tensor(t, [128, S], F32, kind="ExternalInput")
        dr[t + "_lo"] = nc.dram_tensor(t + "_lo", [128, S], F32, kind="ExternalInput")
    dr["cmask"] = nc.dram_tensor("cmask", [128, 4 * 512], BF16, kind="ExternalInput")
    dr["identb"] = nc.dram_tensor("identb", [128, 128], BF16, kind="ExternalInput")
    dr["identf"] = nc.dram_tensor("identf", [128, 128], F32, kind="ExternalInput")
    dr["out"] = nc.dram_tensor("out", [S, E], F32, kind="ExternalOutput")

    def scratch(name, shape, dt):
        kind = "ExternalOutput" if name in DEV_OUT else "Internal"
        return nc.dram_tensor(name, shape, dt, kind=kind)
    dr["wq_h"] = scratch("wq_h", [HQ, NHC, 128, 128], BF16)
    dr["wq_l"] = scratch("wq_l", [HQ, NHC, 128, 128], BF16)
    dr["wk_h"] = scratch("wk_h", [NHC, 128, 512], BF16)
    dr["wk_l"] = scratch("wk_l", [NHC, 128, 512], BF16)
    dr["wv_h"] = scratch("wv_h", [NHC, 128, 512], BF16)
    dr["wv_l"] = scratch("wv_l", [NHC, 128, 512], BF16)
    dr["wo_h"] = scratch("wo_h", [NHC, 128, E], BF16)
    dr["wo_l"] = scratch("wo_l", [NHC, 128, E], BF16)
    dr["xnt_h"] = scratch("xnt_h", [NQB, 128, 8192], BF16)
    dr["xnt_l"] = scratch("xnt_l", [NQB, 128, 8192], BF16)
    dr["s_s"] = scratch("s_s", [NTC, 128, E], F32)
    dr["g_b"] = scratch("g_b", [NIC, 128, 2048], BF16)
    dr["u_b"] = scratch("u_b", [NIC, 128, 2048], BF16)
    dr["d_b"] = scratch("d_b", [NIC, 128, 2048], BF16)

    sems = {}

    def sem(name):
        if name not in sems:
            sems[name] = nc.alloc_semaphore(name)
        return sems[name]

    ctrs = {}

    def emit_conv_jobs(en, eng, C, jobs, cbuf=None):
        for kind, ic in jobs:
            C.bump("cv_o", 16)
            if en == "gpsimd":
                if kind == "d":
                    srca = dr["dwt"].ap()[ic * 128:(ic + 1) * 128, :]
                    dsts = dr["d_b"].ap()[ic]
                else:
                    w = dr["gwt"] if kind == "g" else dr["uwt"]
                    srca = w.ap()[:, ic * 128:(ic + 1) * 128].rearrange(
                        "(hh p) j -> p hh j", p=128)
                    dsts = dr[{"g": "g_b", "u": "u_b"}[kind]].ap()[ic].rearrange(
                        "p (hh j) -> p hh j", j=128)
                eng.dma_start(out=dsts, in_=srca).then_inc(sem("cv_o"), 16)

    def mk_emit(block, fn):
        for en in ENGINES:
            def closure(eng, en=en):
                fn(en, eng, ctrs.setdefault(en, Ctr()))
            getattr(block, en)(closure)

    pstack = contextlib.ExitStack()
    g_k_h = pstack.enter_context(nc.sbuf_tensor("g_k_h", [128, NKV * S], BF16))
    g_k_l = pstack.enter_context(nc.sbuf_tensor("g_k_l", [128, NKV * S], BF16))
    g_v_f = pstack.enter_context(nc.sbuf_tensor("g_v_f", [128, NTC * 512], F32))

    def _emit_p0(en, eng, C, T):
        wtmp, whi, wlo, chunks = T["wtmp"], T["whi"], T["wlo"], T["chunks"]
        for i, (nm, src, cols, dsth, dstl) in enumerate(chunks):
            b = i % 2
            ldk = "p0_ld%d" % b
            stk = "p0_st%d" % b
            C.bump(ldk, 16)
            if en == "sync":
                if i >= 2:
                    eng.wait_ge(sem(stk), C[stk])
                eng.dma_start(out=wtmp[:, b * 2048: b * 2048 + cols], in_=src).then_inc(sem(ldk), 16)
            C.bump("p0_hi")
            if en == "scalar":
                eng.wait_ge(sem(ldk), C[ldk])
                if i >= 2:
                    eng.wait_ge(sem(stk), C[stk])
                eng.copy(whi[:, b * 2048: b * 2048 + cols],
                         wtmp[:, b * 2048: b * 2048 + cols]).then_inc(sem("p0_hi"), 1)
            C.bump("p0_lo")
            if en == "vector":
                eng.wait_ge(sem("p0_hi"), C["p0_hi"])
                eng.tensor_sub(wlo[:, b * 2048: b * 2048 + cols],
                               wtmp[:, b * 2048: b * 2048 + cols],
                               whi[:, b * 2048: b * 2048 + cols]).then_inc(sem("p0_lo"), 1)
            C.bump(stk, 32)
            if en == "gpsimd":
                eng.wait_ge(sem("p0_hi"), C["p0_hi"])
                srch = whi[:, b * 2048: b * 2048 + cols]
                srcl = wlo[:, b * 2048: b * 2048 + cols]
                if nm == "wq":
                    srch = srch.rearrange("p (a j) -> p a j", j=128)
                    srcl = srcl.rearrange("p (a j) -> p a j", j=128)
                eng.dma_start(out=dsth, in_=srch).then_inc(sem(stk), 16)
                eng.wait_ge(sem("p0_lo"), C["p0_lo"])
                eng.dma_start(out=dstl, in_=srcl).then_inc(sem(stk), 16)
        if en == "gpsimd":
            eng.wait_ge(sem("p0_st0"), C["p0_st0"])
            eng.wait_ge(sem("p0_st1"), C["p0_st1"])

    def _emit_a(en, eng, C, T):
        xt, xn = T["xt"], T["xn"]
        xnt_h, xnt_l = T["xnt_h"], T["xnt_l"]
        wbh, wbl = T["wbh"], T["wbl"]
        ck, sk, stats = T["ck"], T["sk"], T["stats"]
        kf, kc, ks, ksw = T["kf"], T["kc"], T["ks"], T["ksw"]
        ptr, pkv = T["ptr"], T["pkv"]
        identf = T["identf"]

        tbuf = T["tbuf"]
        C.bump("a_id", 16)
        C.bump("a_if", 16 * 2)
        C.bump("a_tl0", 16)
        if en == "sync":
            eng.dma_start(out=identf[:], in_=dr["identf"].ap()[:, :]).then_inc(sem("a_id"), 16)
            eng.dma_start(out=ck[:], in_=dr["cosk"].ap()[:, :]).then_inc(sem("a_if"), 16)
            eng.dma_start(out=sk[:], in_=dr["sink"].ap()[:, :]).then_inc(sem("a_if"), 16)
            eng.dma_start(out=tbuf[:], in_=dr["cosk_lo"].ap()[:, :]).then_inc(sem("a_tl0"), 16)
        C.bump("a_tl1")
        if en == "vector":
            eng.wait_ge(sem("a_if"), 32)
            eng.wait_ge(sem("a_tl0"), C["a_tl0"])
            eng.tensor_add(ck[:], ck[:], tbuf[:]).then_inc(sem("a_tl1"), 1)
        C.bump("a_tl0", 16)
        if en == "sync":
            eng.wait_ge(sem("a_tl1"), C["a_tl1"])
            eng.dma_start(out=tbuf[:], in_=dr["sink_lo"].ap()[:, :]).then_inc(sem("a_tl0"), 16)
        C.bump("a_tl1")
        if en == "vector":
            eng.wait_ge(sem("a_tl0"), C["a_tl0"])
            eng.tensor_add(sk[:], sk[:], tbuf[:]).then_inc(sem("a_tl1"), 1)
        C.bump("a_eps")
        if en == "gpsimd":
            eng.memset(stats[:, 7:8], EPS).then_inc(sem("a_eps"), 1)

        for qb in range(NQB):
            for j in range(4):
                tc = qb * 4 + j
                b = tc % 2
                axk = "a_x%d" % b
                C.bump(axk, 16)
                if en == "sync":
                    if tc >= 2:
                        eng.wait_ge(sem("a_sq"), tc - 1)
                        eng.wait_ge(sem("a_xn"), 2 * tc - 2)
                    eng.dma_start(out=xt[:, b * 2048:(b + 1) * 2048],
                                  in_=dr["x"].ap()[tc * 128:(tc + 1) * 128, :]).then_inc(sem(axk), 16)
                C.bump("a_sq")
                if en == "scalar":
                    eng.wait_ge(sem(axk), C[axk])
                    eng.wait_ge(sem("a_tr"), 2 * tc)
                    eng.activation(xn[:], xt[:, b * 2048:(b + 1) * 2048], AF.Square,
                                   accum_out=stats[:, 3:4]).then_inc(sem("a_sq"), 1)
                C.bump("a_vr", 4)
                if en == "vector":
                    eng.wait_ge(sem(axk), C[axk])
                    eng.reduce_sum(stats[:, 0:1], xt[:, b * 2048:(b + 1) * 2048], AX).then_inc(sem("a_vr"), 1)
                    eng.wait_ge(sem("a_vr"), C["a_vr"] - 3)
                    eng.tensor_scalar_mul(stats[:, 1:2], stats[:, 0:1], 1.0 / E).then_inc(sem("a_vr"), 1)
                    eng.wait_ge(sem("a_vr"), C["a_vr"] - 2)
                    eng.tensor_mul(stats[:, 2:3], stats[:, 1:2], stats[:, 1:2]).then_inc(sem("a_vr"), 1)
                    eng.wait_ge(sem("a_sq"), C["a_sq"])
                    if tc == 0:
                        eng.wait_ge(sem("a_eps"), 1)
                    eng.tensor_scalar(stats[:, 4:5], stats[:, 3:4], 1.0 / E, None, ALU.mult).then_inc(sem("a_vr"), 1)
                    eng.wait_ge(sem("a_vr"), C["a_vr"])
                C.bump("a_vs")
                if en == "vector":
                    eng.tensor_sub(stats[:, 4:5], stats[:, 4:5], stats[:, 2:3]).then_inc(sem("a_vs"), 1)
                C.bump("a_sd")
                if en == "scalar":
                    eng.wait_ge(sem("a_vs"), C["a_vs"])
                    eng.activation(stats[:, 5:6], stats[:, 4:5], AF.Sqrt,
                                   bias=stats[:, 7:8]).then_inc(sem("a_sd"), 1)
                C.bump("a_xn")
                if en == "vector":
                    eng.wait_ge(sem("a_sd"), C["a_sd"])
                    eng.reciprocal(stats[:, 6:7], stats[:, 5:6]).then_inc(sem("a_xn"), 1)
                    eng.wait_ge(sem("a_xn"), C["a_xn"])
                C.bump("a_xn")
                if en == "vector":
                    eng.wait_ge(sem("a_tr"), 2 * tc)
                    eng.tensor_scalar(xn[:], xt[:, b * 2048:(b + 1) * 2048],
                                      stats[:, 1:2], stats[:, 6:7],
                                      ALU.subtract, ALU.mult).then_inc(sem("a_xn"), 1)
                for g in range(2):
                    gidx = C.bump("a_tr")
                    if en == "tensor":
                        if tc == 0 and g == 0:
                            eng.wait_ge(sem("a_id"), 16)
                        eng.wait_ge(sem("a_xn"), C["a_xn"])
                        if gidx > 2:
                            eng.wait_ge(sem("a_evh"), gidx - 2)
                            eng.wait_ge(sem("a_evl"), gidx - 2)
                        if j == 0 and g == 0 and qb > 0:
                            eng.wait_ge(sem("a_vev"), C["a_vev"])
                        for hl in range(8):
                            hc = g * 8 + hl
                            ins = eng.transpose(
                                ptr[:, g * 1024 + hl * 128: g * 1024 + (hl + 1) * 128],
                                xn[:, hc * 128:(hc + 1) * 128], identf[:])
                        ins.then_inc(sem("a_tr"), 1)
                    dsth = xnt_h[:].rearrange("p (h c) -> p h c", c=512)[
                        :, g * 8:(g + 1) * 8, j * 128:(j + 1) * 128]
                    dstl = xnt_l[:].rearrange("p (h c) -> p h c", c=512)[
                        :, g * 8:(g + 1) * 8, j * 128:(j + 1) * 128]
                    srcp = ptr[:, g * 1024:(g + 1) * 1024].rearrange("p (h c) -> p h c", c=128)
                    C.bump("a_evh")
                    if en == "scalar":
                        eng.wait_ge(sem("a_tr"), C["a_tr"])
                        eng.copy(dsth, srcp).then_inc(sem("a_evh"), 1)
                    C.bump("a_evl")
                    if en == "vector":
                        eng.wait_ge(sem("a_evh"), C["a_evh"])
                        eng.tensor_sub(dstl, srcp, dsth).then_inc(sem("a_evl"), 1)

            C.bump("a_wk", 32)
            if en == "sync":
                if qb > 0:
                    eng.wait_ge(sem("a_vm"), C["a_vm"])
                eng.dma_start(out=wbh[:].rearrange("p (h j) -> p h j", j=512),
                              in_=dr["wk_h"].ap().transpose([1, 0, 2])).then_inc(sem("a_wk"), 16)
                eng.dma_start(out=wbl[:].rearrange("p (h j) -> p h j", j=512),
                              in_=dr["wk_l"].ap().transpose([1, 0, 2])).then_inc(sem("a_wk"), 16)
            for kvh in range(NKV):
                pb = kvh % 2
                C.bump("a_km")
                if en == "tensor":
                    eng.wait_ge(sem("a_evh"), C["a_evh"])
                    eng.wait_ge(sem("a_evl"), C["a_evl"])
                    eng.wait_ge(sem("a_wk"), C["a_wk"])
                    if C["a_km"] > 2:
                        eng.wait_ge(sem("a_kev"), C["a_km"] - 2)
                    first = True
                    for hc in range(NHC):
                        wt_h = wbh[:, hc * 512 + kvh * 128: hc * 512 + (kvh + 1) * 128]
                        wt_l = wbl[:, hc * 512 + kvh * 128: hc * 512 + (kvh + 1) * 128]
                        rh = xnt_h[:, hc * 512:(hc + 1) * 512]
                        rl = xnt_l[:, hc * 512:(hc + 1) * 512]
                        dst = pkv[:, pb * 512:(pb + 1) * 512]
                        eng.matmul(dst, wt_h, rh, start=first, stop=False)
                        first = False
                        eng.matmul(dst, wt_h, rl, start=False, stop=False)
                        ins = eng.matmul(dst, wt_l, rh, start=False, stop=(hc == NHC - 1))
                    ins.then_inc(sem("a_km"), 1)
                C.bump("a_kev")
                if en == "scalar":
                    eng.wait_ge(sem("a_km"), C["a_km"])
                    if C["a_kev"] > 1:
                        eng.wait_ge(sem("a_krs"), C["a_kev"] - 1)
                    eng.copy(kf[:], pkv[:, pb * 512:(pb + 1) * 512])
                    eng.copy(ksw[0:64, :], pkv[64:128, pb * 512:(pb + 1) * 512])
                    eng.copy(ksw[64:128, :], pkv[0:64, pb * 512:(pb + 1) * 512]).then_inc(sem("a_kev"), 1)
                C.bump("a_vv", 3)
                C.bump("a_krs")
                if en == "vector":
                    eng.wait_ge(sem("a_kev"), C["a_kev"])
                    if qb == 0 and kvh == 0:
                        eng.wait_ge(sem("a_tl1"), 2)
                    cs = ck[:, qb * 512:(qb + 1) * 512]
                    ss = sk[:, qb * 512:(qb + 1) * 512]
                    eng.tensor_mul(kc[:], kf[:], cs)
                    eng.tensor_mul(ks[:], ksw[:], ss).then_inc(sem("a_vv"), 1)
                    eng.wait_ge(sem("a_vv"), C["a_vv"] - 2)
                    eng.tensor_add(kc[:], kc[:], ks[:]).then_inc(sem("a_vv"), 1)
                    eng.wait_ge(sem("a_vv"), C["a_vv"] - 1)
                    dh = g_k_h[:, kvh * S + qb * 512: kvh * S + (qb + 1) * 512]
                    dl = g_k_l[:, kvh * S + qb * 512: kvh * S + (qb + 1) * 512]
                    eng.tensor_copy(dh, kc[:]).then_inc(sem("a_vv"), 1)
                    eng.wait_ge(sem("a_vv"), C["a_vv"])
                    eng.tensor_sub(dl, kc[:], dh).then_inc(sem("a_krs"), 1)

            C.bump("a_wv", 32)
            if en == "sync":
                eng.wait_ge(sem("a_km"), C["a_km"])
                eng.dma_start(out=wbh[:].rearrange("p (h j) -> p h j", j=512),
                              in_=dr["wv_h"].ap().transpose([1, 0, 2])).then_inc(sem("a_wv"), 16)
                eng.dma_start(out=wbl[:].rearrange("p (h j) -> p h j", j=512),
                              in_=dr["wv_l"].ap().transpose([1, 0, 2])).then_inc(sem("a_wv"), 16)
            for j in range(4):
                tc = qb * 4 + j
                pb = j % 2
                C.bump("a_vm")
                if en == "tensor":
                    eng.wait_ge(sem("a_wv"), C["a_wv"])
                    if C["a_vm"] > 2:
                        eng.wait_ge(sem("a_vev"), C["a_vm"] - 2)
                    first = True
                    for hc in range(NHC):
                        lt_h = xnt_h[:, hc * 512 + j * 128: hc * 512 + (j + 1) * 128]
                        lt_l = xnt_l[:, hc * 512 + j * 128: hc * 512 + (j + 1) * 128]
                        rh = wbh[:, hc * 512:(hc + 1) * 512]
                        rl = wbl[:, hc * 512:(hc + 1) * 512]
                        dst = ptr[:, pb * 512:(pb + 1) * 512]
                        eng.matmul(dst, lt_h, rh, start=first, stop=False)
                        first = False
                        eng.matmul(dst, lt_h, rl, start=False, stop=False)
                        ins = eng.matmul(dst, lt_l, rh, start=False, stop=(hc == NHC - 1))
                    ins.then_inc(sem("a_vm"), 1)
                C.bump("a_vev")
                if en == "scalar":
                    eng.wait_ge(sem("a_vm"), C["a_vm"])
                    eng.copy(g_v_f[:, tc * 512:(tc + 1) * 512],
                             ptr[:, pb * 512:(pb + 1) * 512]).then_inc(sem("a_vev"), 1)

            C.bump("a_sp", 32)
            if en == "gpsimd":
                eng.wait_ge(sem("a_evh"), C["a_evh"])
                eng.wait_ge(sem("a_evl"), C["a_evl"])
                eng.dma_start(out=dr["xnt_h"].ap()[qb], in_=xnt_h[:]).then_inc(sem("a_sp"), 16)
                eng.dma_start(out=dr["xnt_l"].ap()[qb], in_=xnt_l[:]).then_inc(sem("a_sp"), 16)
            emit_conv_jobs(en, eng, C, [("g", qb * 16 + i) for i in range(16)])
            if qb < NQB - 1:
                if en in ("scalar", "vector"):
                    eng.wait_ge(sem("a_sp"), C["a_sp"])
                    eng.wait_ge(sem("a_vm"), C["a_vm"])
        if en == "gpsimd":
            eng.wait_ge(sem("a_sp"), C["a_sp"])

    def _emit_b1(en, eng, C, T):
        xnt_h, xnt_l = T["xnt_h"], T["xnt_l"]
        cq, sq = T["cq"], T["sq"]
        wqb_h, wqb_l = T["wqb_h"], T["wqb_l"]
        qf, qc, qs, qsw = T["qf"], T["qc"], T["qs"], T["qsw"]
        q_h, q_l = T["q_h"], T["q_l"]
        p_f, p_b, p_r = T["p_f"], T["p_b"], T["p_r"]
        ao_h, ao_l, ao_f = T["ao_h"], T["ao_l"], T["ao_f"]
        rr_b, rs_row, rr_row = T["rr_b"], T["rs_row"], T["rr_row"]
        negc, ones_b, ones_rf = T["negc"], T["ones_b"], T["ones_rf"]
        cmask, identb = T["cmask"], T["identb"]
        pARE = T["pARE"]
        pS = pARE[:, 0:2048]
        psQ = pARE[:, 2048:2560]
        pAO = pARE[:, 2560:3072]
        pRS = pARE[:, 3072:3584]
        wob_h, wob_l, s_half = T["wob_h"], T["wob_l"], T["s_half"]

        C.bump("b_cl", 32)
        if en == "sync":
            eng.dma_start(out=cmask[:], in_=dr["cmask"].ap()[:, :]).then_inc(sem("b_cl"), 16)
            eng.dma_start(out=identb[:], in_=dr["identb"].ap()[:, :]).then_inc(sem("b_cl"), 16)
        C.bump("b_ic")
        if en == "gpsimd":
            eng.memset(negc[:], -CSHIFT)
            eng.memset(ones_b[:], 1.0)
            eng.memset(ones_rf[:], 1.0).then_inc(sem("b_ic"), 1)

        gg = 0
        for qb in range(NQB):
            C.bump("b_ld", 16 * 4)
            if en == "sync":
                if qb >= 1:
                    eng.wait_ge(sem("b_qbdone"), qb)
                eng.dma_start(out=xnt_h[:], in_=dr["xnt_h"].ap()[qb]).then_inc(sem("b_ld"), 16)
                eng.dma_start(out=xnt_l[:], in_=dr["xnt_l"].ap()[qb]).then_inc(sem("b_ld"), 16)
                eng.dma_start(out=cq[:], in_=dr["cosq"].ap()[:, qb * 512:(qb + 1) * 512]).then_inc(sem("b_ld"), 16)
                eng.dma_start(out=sq[:], in_=dr["sinq"].ap()[:, qb * 512:(qb + 1) * 512]).then_inc(sem("b_ld"), 16)
            C.bump("b_tl0", 32)
            if en == "sync":
                if qb > 0:
                    eng.wait_ge(sem("b_qrs"), C["b_qrs"])
                eng.dma_start(out=qc[:], in_=dr["cosq_lo"].ap()[:, qb * 512:(qb + 1) * 512]).then_inc(sem("b_tl0"), 16)
                eng.dma_start(out=qs[:], in_=dr["sinq_lo"].ap()[:, qb * 512:(qb + 1) * 512]).then_inc(sem("b_tl0"), 16)
            C.bump("b_tl1")
            if en == "vector":
                eng.wait_ge(sem("b_ld"), C["b_ld"])
                eng.wait_ge(sem("b_tl0"), C["b_tl0"])
                eng.tensor_add(cq[:], cq[:], qc[:]).then_inc(sem("b_tl1"), 1)
                eng.wait_ge(sem("b_tl1"), C["b_tl1"])
            C.bump("b_tl1")
            if en == "vector":
                eng.tensor_add(sq[:], sq[:], qs[:]).then_inc(sem("b_tl1"), 1)
            L = 4 * qb + 4
            groups = [list(range(L))[i:i + 2] for i in range(0, L, 2)]

            def emit_qproj(qh):
                kvh = qh // 4
                qpar = qh % 2
                wqk = "b_wq%d" % (qh % 2)
                C.bump(wqk, 32)
                if en == "sync":
                    if C[wqk] > 32:
                        eng.wait_ge(sem("b_qm"), C["b_qm"] - 1)
                    eng.dma_start(out=wqb_h[:, (qh % 2) * 2048:(qh % 2 + 1) * 2048
                                              ].rearrange("p (h j) -> p h j", j=128),
                                  in_=dr["wq_h"].ap()[qh].transpose([1, 0, 2])
                                  ).then_inc(sem(wqk), 16)
                    eng.dma_start(out=wqb_l[:, (qh % 2) * 2048:(qh % 2 + 1) * 2048
                                              ].rearrange("p (h j) -> p h j", j=128),
                                  in_=dr["wq_l"].ap()[qh].transpose([1, 0, 2])
                                  ).then_inc(sem(wqk), 16)
                C.bump("b_qm")
                if en == "tensor":
                    eng.wait_ge(sem("b_ld"), C["b_ld"])
                    eng.wait_ge(sem(wqk), C[wqk])
                    if qh == 0:
                        eng.wait_ge(sem("w_ev"), C.get("w_ev", 0))
                    if C["b_qm"] > 1:
                        eng.wait_ge(sem("b_qev"), C["b_qm"] - 1)
                    first = True
                    for hc in range(NHC):
                        off = (qh % 2) * 2048 + hc * 128
                        wt_h = wqb_h[:, off: off + 128]
                        wt_l = wqb_l[:, off: off + 128]
                        rh = xnt_h[:, hc * 512:(hc + 1) * 512]
                        rl = xnt_l[:, hc * 512:(hc + 1) * 512]
                        eng.matmul(psQ, wt_h, rh, start=first, stop=False)
                        first = False
                        eng.matmul(psQ, wt_h, rl, start=False, stop=False)
                        ins = eng.matmul(psQ, wt_l, rh, start=False, stop=(hc == NHC - 1))
                    ins.then_inc(sem("b_qm"), 1)
                C.bump("b_qev")
                if en == "scalar":
                    eng.wait_ge(sem("b_qm"), C["b_qm"])
                    if C["b_qev"] > 1:
                        eng.wait_ge(sem("b_qrs"), C["b_qev"] - 1)
                    eng.copy(qf[:], psQ)
                    eng.copy(qsw[0:64, :], psQ[64:128, :])
                    eng.copy(qsw[64:128, :], psQ[0:64, :]).then_inc(sem("b_qev"), 1)
                C.bump("b_vv", 3)
                C.bump("b_qrs")
                if en == "vector":
                    if qh == 0:
                        eng.wait_ge(sem("b_tl1"), C["b_tl1"])
                    eng.wait_ge(sem("b_qev"), C["b_qev"])
                    eng.tensor_mul(qc[:], qf[:], cq[:])
                    eng.tensor_mul(qs[:], qsw[:], sq[:]).then_inc(sem("b_vv"), 1)
                    eng.wait_ge(sem("b_vv"), C["b_vv"] - 2)
                    eng.tensor_add(qc[:], qc[:], qs[:]).then_inc(sem("b_vv"), 1)
                    eng.wait_ge(sem("b_vv"), C["b_vv"] - 1)
                    qhsl = q_h[:, qpar * 512:(qpar + 1) * 512]
                    qlsl = q_l[:, qpar * 512:(qpar + 1) * 512]
                    eng.tensor_copy(qhsl, qc[:]).then_inc(sem("b_vv"), 1)
                    eng.wait_ge(sem("b_vv"), C["b_vv"])
                    eng.tensor_sub(qlsl, qc[:], qhsl).then_inc(sem("b_qrs"), 1)

            def emit_S(qh, gi, grp, gg):
                kvh = qh // 4
                qpar = qh % 2
                slot = gg % 2
                C.bump("b_S")
                if en == "tensor":
                    if gi == 0:
                        eng.wait_ge(sem("b_qrs"), C["b_qrs"])
                        if qb == 0 and qh == 0:
                            eng.wait_ge(sem("b_ic"), 1)
                            eng.wait_ge(sem("b_cl"), 32)
                    if gg > 2:
                        eng.wait_ge(sem("b_exp"), gg - 2)
                    for ti, tkc in enumerate(grp):
                        dst = pS[:, slot * 1024 + ti * 512: slot * 1024 + (ti + 1) * 512]
                        st = True
                        if tkc >= qb * 4:
                            o = tkc - qb * 4
                            eng.matmul(dst, identb[:], cmask[:, o * 512:(o + 1) * 512],
                                       start=True, stop=False)
                            st = False
                        kh = g_k_h[:, kvh * S + tkc * 128: kvh * S + (tkc + 1) * 128]
                        kl = g_k_l[:, kvh * S + tkc * 128: kvh * S + (tkc + 1) * 128]
                        qhsl = q_h[:, qpar * 512:(qpar + 1) * 512]
                        qlsl = q_l[:, qpar * 512:(qpar + 1) * 512]
                        eng.matmul(dst, kh, qhsl, start=st, stop=False)
                        eng.matmul(dst, kh, qlsl, start=False, stop=False)
                        ins = eng.matmul(dst, kl, qhsl, start=False, stop=True)
                    ins.then_inc(sem("b_S"), 1)
                C.bump("b_exp")
                if en == "scalar":
                    eng.wait_ge(sem("b_S"), C["b_S"])
                    if gg > 2:
                        eng.wait_ge(sem("b_pv"), gg - 2)
                        eng.wait_ge(sem("b_cast"), gg - 2)
                    n = len(grp)
                    eng.activation(p_f[:, slot * 1024: slot * 1024 + n * 512],
                                   pS[:, slot * 1024: slot * 1024 + n * 512],
                                   AF.Exp, bias=negc[:, 0:1]).then_inc(sem("b_exp"), 1)
                C.bump("b_vv")
                C.bump("b_cast")
                if en == "vector":
                    eng.wait_ge(sem("b_exp"), C["b_exp"])
                    n = len(grp)
                    eng.tensor_copy(p_b[:, slot * 1024: slot * 1024 + n * 512],
                                    p_f[:, slot * 1024: slot * 1024 + n * 512]
                                    ).then_inc(sem("b_vv"), 1)
                    eng.wait_ge(sem("b_vv"), C["b_vv"])
                    eng.tensor_sub(p_r[:, slot * 1024: slot * 1024 + n * 512],
                                   p_f[:, slot * 1024: slot * 1024 + n * 512],
                                   p_b[:, slot * 1024: slot * 1024 + n * 512]
                                   ).then_inc(sem("b_cast"), 1)

            def emit_PV(qh, gi, grp, gg, ng):
                kvh = qh // 4
                C.bump("b_pv")
                if en == "tensor":
                    eng.wait_ge(sem("b_exp"), C["b_exp_at%d" % gg])
                    if gi == 0:
                        eng.wait_ge(sem("b_aoev"), C.get("b_aoev", 0))
                    slot = gg % 2
                    for ti, tkc in enumerate(grp):
                        vt = g_v_f[:, tkc * 512 + kvh * 128: tkc * 512 + (kvh + 1) * 128]
                        eng.matmul(pAO[:, 0:512], vt,
                                   p_f[:, slot * 1024 + ti * 512: slot * 1024 + (ti + 1) * 512],
                                   start=(gi == 0 and ti == 0),
                                   stop=(gi == ng - 1 and ti == len(grp) - 1))
                    eng.wait_ge(sem("b_cast"), C["b_cast_at%d" % gg])
                    if gi == 0:
                        eng.wait_ge(sem("b_rrev"), C.get("b_rrev", 0))
                    for ti, tkc in enumerate(grp):
                        eng.matmul(pRS[0:1, 0:512], ones_b[:],
                                   p_b[:, slot * 1024 + ti * 512: slot * 1024 + (ti + 1) * 512],
                                   start=(gi == 0 and ti == 0), stop=False)
                        ins = eng.matmul(pRS[0:1, 0:512], ones_b[:],
                                         p_r[:, slot * 1024 + ti * 512: slot * 1024 + (ti + 1) * 512],
                                         start=False,
                                         stop=(gi == ng - 1 and ti == len(grp) - 1))
                    ins.then_inc(sem("b_pv"), 1)

            emit_qproj(0)
            for qh in range(HQ):
                ng = len(groups)
                pend = None
                for gi, grp in enumerate(groups):
                    gg = C.bump("b_gg")
                    C["b_exp_at%d" % gg] = 0
                    emit_S(qh, gi, grp, gg)
                    C["b_exp_at%d" % gg] = C["b_exp"]
                    C["b_cast_at%d" % gg] = C["b_cast"]
                    if pend is not None:
                        emit_PV(qh, pend[0], pend[1], pend[2], ng)
                    pend = (gi, grp, gg)
                if qh + 1 < HQ:
                    emit_qproj(qh + 1)
                emit_PV(qh, pend[0], pend[1], pend[2], ng)
                C.bump("b_rsev")
                if en == "scalar":
                    eng.wait_ge(sem("b_pv"), C["b_pv"])
                    eng.copy(rs_row[:], pRS[0:1, 0:512]).then_inc(sem("b_rsev"), 1)
                C.bump("b_rcp")
                if en == "vector":
                    eng.wait_ge(sem("b_rsev"), C["b_rsev"])
                    eng.reciprocal(rr_row[:], rs_row[:]).then_inc(sem("b_rcp"), 1)
                C.bump("b_bc")
                if en == "tensor":
                    eng.wait_ge(sem("b_rcp"), C["b_rcp"])
                    ins = eng.matmul(pRS[:, 0:512], ones_rf[0:1, :], rr_row[:],
                                     start=True, stop=True)
                    ins.then_inc(sem("b_bc"), 1)
                C.bump("b_rrev")
                if en == "scalar":
                    eng.wait_ge(sem("b_bc"), C["b_bc"])
                    eng.copy(rr_b[:], pRS[:, 0:512]).then_inc(sem("b_rrev"), 1)
                C.bump("b_aoev")
                if en == "scalar":
                    eng.wait_ge(sem("b_pv"), C["b_pv"])
                    if C["b_aoev"] > 1:
                        eng.wait_ge(sem("b_ao"), C["b_aoev"] - 1)
                    eng.copy(ao_f[:], pAO[:, 0:512]).then_inc(sem("b_aoev"), 1)
                C.bump("b_vv", 2)
                C.bump("b_ao")
                if en == "vector":
                    if qh == 0 and qb > 0:
                        eng.wait_ge(sem("w_mm"), C.get("w_mm", 0))
                    eng.wait_ge(sem("b_rrev"), C["b_rrev"])
                    eng.wait_ge(sem("b_aoev"), C["b_aoev"])
                    eng.tensor_mul(ao_f[:], ao_f[:], rr_b[:]).then_inc(sem("b_vv"), 1)
                    eng.wait_ge(sem("b_vv"), C["b_vv"] - 1)
                    dh = ao_h[:, qh * 512:(qh + 1) * 512]
                    dl = ao_l[:, qh * 512:(qh + 1) * 512]
                    eng.tensor_copy(dh, ao_f[:]).then_inc(sem("b_vv"), 1)
                    eng.wait_ge(sem("b_vv"), C["b_vv"])
                    eng.tensor_sub(dl, ao_f[:], dh).then_inc(sem("b_ao"), 1)
            C.bump("b_qbdone")
            if en == "gpsimd":
                eng.wait_ge(sem("b_qm"), C["b_qm"])
                eng.sem_inc(sem("b_qbdone"), 1)
            for eh in range(2):
                for dc in range(NHC):
                    wlk = "w_wl%d" % (dc % 2)
                    C.bump(wlk, 32)
                    if en == "sync":
                        if C[wlk] > 32:
                            eng.wait_ge(sem("w_mm"), C.get("w_mm", 0) - 1)
                        eng.dma_start(out=wob_h[:, (dc % 2) * 1024:(dc % 2 + 1) * 1024],
                                      in_=dr["wo_h"].ap()[dc][:, eh * 1024:(eh + 1) * 1024]
                                      ).then_inc(sem(wlk), 16)
                        eng.dma_start(out=wob_l[:, (dc % 2) * 1024:(dc % 2 + 1) * 1024],
                                      in_=dr["wo_l"].ap()[dc][:, eh * 1024:(eh + 1) * 1024]
                                      ).then_inc(sem(wlk), 16)
                    C.bump("w_mm")
                    if en == "tensor":
                        eng.wait_ge(sem(wlk), C[wlk])
                        if eh == 0 and dc == 0:
                            eng.wait_ge(sem("b_ao"), C["b_ao"])
                        if dc == 0:
                            eng.wait_ge(sem("w_ev"), C.get("w_ev", 0))
                        wb = dc % 2
                        for tqc in range(4):
                            ah = ao_h[:, dc * 512 + tqc * 128: dc * 512 + (tqc + 1) * 128]
                            al = ao_l[:, dc * 512 + tqc * 128: dc * 512 + (tqc + 1) * 128]
                            d0 = pARE[:, tqc * 1024: tqc * 1024 + 512]
                            d1 = pARE[:, tqc * 1024 + 512: (tqc + 1) * 1024]
                            wh0 = wob_h[:, wb * 1024: wb * 1024 + 512]
                            wh1 = wob_h[:, wb * 1024 + 512: (wb + 1) * 1024]
                            wl0 = wob_l[:, wb * 1024: wb * 1024 + 512]
                            wl1 = wob_l[:, wb * 1024 + 512: (wb + 1) * 1024]
                            st = dc == 0
                            sp = dc == NHC - 1
                            eng.matmul(d0, ah, wh0, start=st, stop=False)
                            eng.matmul(d1, ah, wh1, start=st, stop=False)
                            eng.matmul(d0, ah, wl0, start=False, stop=False)
                            eng.matmul(d1, ah, wl1, start=False, stop=False)
                            eng.matmul(d0, al, wh0, start=False, stop=sp)
                            ins = eng.matmul(d1, al, wh1, start=False, stop=sp)
                        ins.then_inc(sem("w_mm"), 1)
                for tqc in range(4):
                    spar = tqc % 2
                    sok = "w_sout%d" % spar
                    C.bump("w_ev")
                    if en == "scalar":
                        eng.wait_ge(sem("w_mm"), C["w_mm"])
                        eng.wait_ge(sem(sok), C.get(sok, 0))
                        eng.copy(s_half[:, spar * 1024:(spar + 1) * 1024],
                                 pARE[:, tqc * 1024:(tqc + 1) * 1024]).then_inc(sem("w_ev"), 1)
                    C.bump(sok, 16)
                    if en == "gpsimd":
                        eng.wait_ge(sem("w_ev"), C["w_ev"])
                        tc = qb * 4 + tqc
                        eng.dma_start(out=dr["s_s"].ap()[tc][:, eh * 1024:(eh + 1) * 1024],
                                      in_=s_half[:, spar * 1024:(spar + 1) * 1024]
                                      ).then_inc(sem(sok), 16)
            emit_conv_jobs(en, eng, C,
                           [("u", qb * 16 + i) for i in range(16)]
                           + [("d", qb * 16 + i) for i in range(16)])


        if en == "gpsimd":
            eng.wait_ge(sem("cv_o"), C.get("cv_o", 0))
            eng.wait_ge(sem("w_sout0"), C.get("w_sout0", 0))
            eng.wait_ge(sem("w_sout1"), C.get("w_sout1", 0))

    def _emit_c(en, eng, C, T):
        xnt_h, xnt_l = T["xnt_h"], T["xnt_l"]
        gb, ub, db = T["gb"], T["ub"], T["db"]
        h_h, h_l = T["h_h"], T["h_l"]
        sg_s, hf_s, facc = T["sg_s"], T["hf_s"], T["facc"]
        gsc, usc, dscb = T["gsc"], T["usc"], T["dscb"]
        hs_row, negh = T["hs_row"], T["negh"]
        onesb, onesf1 = T["onesb"], T["onesf1"]
        st_t, xt_t, ep_g = T["st_t"], T["xt_t"], T["ep_g"]
        ps = T["ps"]

        C.bump("c_ic", 16 * 3)
        if en == "sync":
            eng.dma_start(out=gsc[:], in_=dr["gsc"].ap()[:, :]).then_inc(sem("c_ic"), 16)
            eng.dma_start(out=usc[:], in_=dr["usc"].ap()[:, :]).then_inc(sem("c_ic"), 16)
            eng.dma_start(out=dscb[:], in_=dr["dscb"].ap()[:, :]).then_inc(sem("c_ic"), 16)
        C.bump("c_ic2")
        if en == "gpsimd":
            eng.memset(onesb[:], 1.0)
            eng.memset(onesf1[:], 1.0).then_inc(sem("c_ic2"), 1)

        def emit_epi0(fb):
            C.bump("c_hst", 2)
            if en == "vector":
                eng.wait_ge(sem("c_hsev"), C["c_hsev"])
                eng.tensor_add(hs_row[0:1, 0:512], hs_row[0:1, 0:512],
                               hs_row[0:1, 512:1024]).then_inc(sem("c_hst"), 1)
                eng.wait_ge(sem("c_hst"), C["c_hst"] - 1)
                eng.tensor_scalar_mul(hs_row[0:1, 0:512], hs_row[0:1, 0:512],
                                      -128.0).then_inc(sem("c_hst"), 1)
            for tcl in range(4):
                C.bump("c_ht")
                if en == "tensor":
                    eng.wait_ge(sem("c_hst"), C["c_hst"])
                    eng.wait_ge(sem("c_devs"), C["c_devs"])
                    eng.wait_ge(sem("c_devv"), C["c_devv"])
                    ins = eng.transpose(ps[:, 3584 + tcl * 128: 3584 + tcl * 128 + 1],
                                        hs_row[0:1, tcl * 128:(tcl + 1) * 128],
                                        onesf1[0:1, 0:1])
                    ins.then_inc(sem("c_ht"), 1)
                C.bump("c_htev")
                if en == "scalar":
                    eng.wait_ge(sem("c_ht"), C["c_ht"])
                    eng.copy(negh[:, tcl:tcl + 1],
                             ps[:, 3584 + tcl * 128: 3584 + tcl * 128 + 1]).then_inc(sem("c_htev"), 1)

        def emit_epi_t(fb, tcl):
            tc = fb * 4 + tcl
            n_sx = C.bump("c_sx", 32) // 32
            if en == "gpsimd":
                if n_sx > 1:
                    eng.wait_ge(sem("c_ep3"), C.get("c_ep3", 0))
                eng.dma_start(out=st_t[:],
                              in_=dr["s_s"].ap()[tc]).then_inc(sem("c_sx"), 16)
                eng.dma_start(out=xt_t[:],
                              in_=dr["x"].ap()[tc * 128:(tc + 1) * 128, :]).then_inc(sem("c_sx"), 16)
            C.bump("c_vv")
            C.bump("c_ep1")
            if en == "vector":
                eng.wait_ge(sem("c_htev"), C["c_htev"])
                fsl = facc[:, tcl * 2048:(tcl + 1) * 2048]
                eng.tensor_scalar_add(fsl, fsl, negh[:, tcl:tcl + 1]).then_inc(sem("c_vv"), 1)
                eng.wait_ge(sem("c_vv"), C["c_vv"])
                eng.tensor_mul(fsl, fsl, dscb[:]).then_inc(sem("c_ep1"), 1)
            C.bump("c_ep2")
            if en == "scalar":
                eng.wait_ge(sem("c_ep1"), C["c_ep1"])
                eng.wait_ge(sem("c_out"), C.get("c_out", 0))
                eng.activation(ep_g[:], facc[:, tcl * 2048:(tcl + 1) * 2048],
                               AF.Silu).then_inc(sem("c_ep2"), 1)
            C.bump("c_vv")
            C.bump("c_ep3")
            if en == "vector":
                eng.wait_ge(sem("c_ep2"), C["c_ep2"])
                eng.wait_ge(sem("c_sx"), C["c_sx"])
                eng.tensor_mul(ep_g[:], st_t[:], ep_g[:]).then_inc(sem("c_vv"), 1)
                eng.wait_ge(sem("c_vv"), C["c_vv"])
                eng.tensor_add(ep_g[:], ep_g[:], xt_t[:]).then_inc(sem("c_ep3"), 1)
            C.bump("c_out", 16)
            if en == "gpsimd":
                eng.wait_ge(sem("c_ep3"), C["c_ep3"])
                eng.dma_start(out=dr["out"].ap()[tc * 128:(tc + 1) * 128, :],
                              in_=ep_g[:]).then_inc(sem("c_out"), 16)

        pend_epi = [None]
        for fb in range(C_NFB):
            C.bump("c_xl", 32)
            if en == "sync":
                if fb >= 1:
                    eng.wait_ge(sem("c_gdone"), 2 * fb)
                eng.dma_start(out=xnt_h[:], in_=dr["xnt_h"].ap()[fb]).then_inc(sem("c_xl"), 16)
                eng.dma_start(out=xnt_l[:], in_=dr["xnt_l"].ap()[fb]).then_inc(sem("c_xl"), 16)
            for ih in range(2):
                for il in range(32):
                    ic = ih * 32 + il
                    wbg = il % 2
                    gwk = "c_gw%d" % wbg
                    C.bump(gwk, 32)
                    if en == "sync":
                        if C[gwk] > 32:
                            eng.wait_ge(sem("c_gmm"), C.get("c_gmm", 0) - 1)
                        eng.dma_start(out=gb[:, wbg * 2048:(wbg + 1) * 2048],
                                      in_=dr["g_b"].ap()[ic]).then_inc(sem(gwk), 16)
                        eng.dma_start(out=ub[:, wbg * 2048:(wbg + 1) * 2048],
                                      in_=dr["u_b"].ap()[ic]).then_inc(sem(gwk), 16)
                    gpar = il % 2
                    C.bump("c_gmm")
                    if en == "tensor":
                        eng.wait_ge(sem(gwk), C[gwk])
                        eng.wait_ge(sem("c_xl"), C["c_xl"])
                        if il == 0:
                            eng.wait_ge(sem("c_devs"), C.get("c_devs", 0))
                            eng.wait_ge(sem("c_devv"), C.get("c_devv", 0))
                        if C["c_gmm"] > 2:
                            eng.wait_ge(sem("c_h"), C["c_gmm"] - 2)
                        dstg = ps[:, gpar * 512:(gpar + 1) * 512]
                        dstu = ps[:, 1024 + gpar * 512: 1024 + (gpar + 1) * 512]
                        for hc in range(NHC):
                            wt_g = gb[:, wbg * 2048 + hc * 128: wbg * 2048 + (hc + 1) * 128]
                            wt_u = ub[:, wbg * 2048 + hc * 128: wbg * 2048 + (hc + 1) * 128]
                            rh = xnt_h[:, hc * 512:(hc + 1) * 512]
                            rl = xnt_l[:, hc * 512:(hc + 1) * 512]
                            eng.matmul(dstg, wt_g, rh, start=(hc == 0), stop=False)
                            eng.matmul(dstg, wt_g, rl, start=False, stop=(hc == NHC - 1))
                            eng.matmul(dstu, wt_u, rh, start=(hc == 0), stop=False)
                            ins = eng.matmul(dstu, wt_u, rl, start=False, stop=(hc == NHC - 1))
                        ins.then_inc(sem("c_gmm"), 1)
                    C.bump("c_sg")
                    if en == "scalar":
                        eng.wait_ge(sem("c_gmm"), C["c_gmm"])
                        if fb == 0 and ic == 0:
                            eng.wait_ge(sem("c_ic"), 48)
                        if C["c_sg"] > 2:
                            eng.wait_ge(sem("c_h"), C["c_sg"] - 2)
                        eng.activation(sg_s[:, gpar * 512:(gpar + 1) * 512],
                                       ps[:, gpar * 512:(gpar + 1) * 512],
                                       AF.Silu, scale=gsc[:, ic:ic + 1]).then_inc(sem("c_sg"), 1)
                    C.bump("c_vv", 2)
                    C.bump("c_h")
                    if en == "vector":
                        eng.wait_ge(sem("c_sg"), C["c_sg"])
                        if fb == 0 and ic == 0:
                            eng.wait_ge(sem("c_ic"), 48)
                        if C.get("c_dmm_ph", 0) > 0 and il == 0:
                            eng.wait_ge(sem("c_dmm"), C["c_dmm_ph"])
                        eng.scalar_tensor_tensor(hf_s[:, gpar * 512:(gpar + 1) * 512],
                                                 ps[:, 1024 + gpar * 512: 1024 + (gpar + 1) * 512],
                                                 usc[:, ic:ic + 1],
                                                 sg_s[:, gpar * 512:(gpar + 1) * 512],
                                                 ALU.mult, ALU.mult).then_inc(sem("c_vv"), 1)
                        eng.wait_ge(sem("c_vv"), C["c_vv"] - 1)
                        dh = h_h[:, il * 512:(il + 1) * 512]
                        dl = h_l[:, il * 512:(il + 1) * 512]
                        eng.tensor_copy(dh, hf_s[:, gpar * 512:(gpar + 1) * 512]).then_inc(sem("c_vv"), 1)
                        eng.wait_ge(sem("c_vv"), C["c_vv"])
                        eng.tensor_sub(dl, hf_s[:, gpar * 512:(gpar + 1) * 512],
                                       dh).then_inc(sem("c_h"), 1)
                    if ih == 0 and pend_epi[0] is not None:
                        if il == 2:
                            emit_epi0(pend_epi[0])
                        elif il in (5, 8, 11, 14):
                            emit_epi_t(pend_epi[0], (il - 5) // 3)
                            if il == 14:
                                pend_epi[0] = None
                    C.bump("c_hs")
                    if en == "tensor":
                        eng.wait_ge(sem("c_h"), C["c_h"])
                        if fb == 0 and ic == 0:
                            eng.wait_ge(sem("c_ic2"), 1)
                        if il == 0:
                            eng.wait_ge(sem("c_hsev"), C.get("c_hsev", 0))
                        dh = h_h[:, il * 512:(il + 1) * 512]
                        dl = h_l[:, il * 512:(il + 1) * 512]
                        eng.matmul(ps[0:1, 2048:2560], onesb[:], dh,
                                   start=(il == 0), stop=False)
                        ins = eng.matmul(ps[0:1, 2048:2560], onesb[:], dl,
                                         start=False, stop=(il == 31))
                        ins.then_inc(sem("c_hs"), 1)
                C.bump("c_hsev")
                if en == "scalar":
                    eng.wait_ge(sem("c_hs"), C["c_hs"])
                    eng.wait_ge(sem("c_hst"), C.get("c_hst", 0))
                    eng.copy(hs_row[0:1, ih * 512:(ih + 1) * 512],
                             ps[0:1, 2048:2560]).then_inc(sem("c_hsev"), 1)
                C.bump("c_gdone")
                if en == "tensor":
                    eng.nop().then_inc(sem("c_gdone"), 1)
                for hhh in range(2):
                    for il in range(32):
                        ic = ih * 32 + il
                        wbd = il % 2
                        dwk = "c_dw%d" % wbd
                        C.bump(dwk, 16)
                        if en == "gpsimd":
                            if C[dwk] > 16:
                                eng.wait_ge(sem("c_dmm"), C.get("c_dmm", 0) - 1)
                            src = dr["d_b"].ap()[ic][:, hhh * 1024:(hhh + 1) * 1024]
                            eng.dma_start(out=db[:, wbd * 1024:(wbd + 1) * 1024],
                                          in_=src).then_inc(sem(dwk), 16)
                        C.bump("c_dmm")
                        if en == "tensor":
                            eng.wait_ge(sem(dwk), C[dwk])
                            eng.wait_ge(sem("c_h"), C["c_h"])
                            if il == 0:
                                eng.wait_ge(sem("c_devs"), C.get("c_devs", 0))
                                eng.wait_ge(sem("c_devv"), C.get("c_devv", 0))
                                eng.wait_ge(sem("c_hsev"), C["c_hsev"])
                                eng.wait_ge(sem("c_htev"), C.get("c_htev", 0))
                            for tcl in range(4):
                                lt_h = h_h[:, il * 512 + tcl * 128: il * 512 + (tcl + 1) * 128]
                                lt_l = h_l[:, il * 512 + tcl * 128: il * 512 + (tcl + 1) * 128]
                                d0 = ps[:, tcl * 1024: tcl * 1024 + 512]
                                d1 = ps[:, tcl * 1024 + 512: (tcl + 1) * 1024]
                                st = il == 0
                                sp = il == 31
                                eng.matmul(d0, lt_h, db[:, wbd * 1024: wbd * 1024 + 512],
                                           start=st, stop=False)
                                eng.matmul(d1, lt_h, db[:, wbd * 1024 + 512:(wbd + 1) * 1024],
                                           start=st, stop=False)
                                eng.matmul(d0, lt_l, db[:, wbd * 1024: wbd * 1024 + 512],
                                           start=False, stop=sp)
                                ins = eng.matmul(d1, lt_l, db[:, wbd * 1024 + 512:(wbd + 1) * 1024],
                                                 start=False, stop=sp)
                            ins.then_inc(sem("c_dmm"), 1)
                    for tcl in range(4):
                        if ih == 0:
                            C.bump("c_devs")
                            if en == "scalar":
                                eng.wait_ge(sem("c_dmm"), C["c_dmm"])
                                if fb > 0 and hhh == 0 and tcl == 0:
                                    eng.wait_ge(sem("c_ep2"), 4 * fb)
                                eng.copy(facc[:, tcl * 2048 + hhh * 1024:
                                              tcl * 2048 + (hhh + 1) * 1024],
                                         ps[:, tcl * 1024:(tcl + 1) * 1024]).then_inc(sem("c_devs"), 1)
                        else:
                            C.bump("c_devv")
                            if en == "vector":
                                eng.wait_ge(sem("c_dmm"), C["c_dmm"])
                                dstf = facc[:, tcl * 2048 + hhh * 1024:
                                            tcl * 2048 + (hhh + 1) * 1024]
                                eng.tensor_add(dstf, ps[:, tcl * 1024:(tcl + 1) * 1024],
                                               dstf).then_inc(sem("c_devv"), 1)
                C["c_dmm_ph"] = C["c_dmm"]
            pend_epi[0] = fb
            if fb == C_NFB - 1:
                emit_epi0(fb)
                for tcl in range(4):
                    emit_epi_t(fb, tcl)
                pend_epi[0] = None
        if en == "gpsimd":
            eng.wait_ge(sem("c_out"), C["c_out"])

    if "p0" in PHASES:
        with contextlib.ExitStack() as es:
            sb = lambda n, s, d: es.enter_context(nc.sbuf_tensor(n, s, d))
            T = {
                "wtmp": sb("p0_wtmp", [128, 4096], F32),
                "whi": sb("p0_whi", [128, 4096], BF16),
                "wlo": sb("p0_wlo", [128, 4096], BF16),
            }
            chunks = []
            for hc in range(NHC):
                src = dr["wq"].ap()[hc * 128:(hc + 1) * 128, :]
                dsth = dr["wq_h"].ap()[:, hc].transpose([1, 0, 2])
                dstl = dr["wq_l"].ap()[:, hc].transpose([1, 0, 2])
                chunks.append(("wq", src, 2048, dsth, dstl))
            for w, cols in (("wk", 512), ("wv", 512), ("wo", 2048)):
                for hc in range(NHC):
                    src = dr[w].ap()[hc * 128:(hc + 1) * 128, :]
                    chunks.append((w, src, cols, dr[w + "_h"].ap()[hc], dr[w + "_l"].ap()[hc]))
            T["chunks"] = chunks
            with nc.Block() as blk:
                mk_emit(blk, lambda en, eng, C: _emit_p0(en, eng, C, T))

    if "a" in PHASES:
        with contextlib.ExitStack() as es:
            sb = lambda n, s, d: es.enter_context(nc.sbuf_tensor(n, s, d))
            pt = lambda n, s: es.enter_context(nc.psum_tensor(n, s, F32))
            T = {
                "xt": sb("a_xt", [128, 4096], F32),
                "xn": sb("a_xn", [128, 2048], F32),
                "xnt_h": sb("a_xnth", [128, 8192], BF16),
                "xnt_l": sb("a_xntl", [128, 8192], BF16),
                "wbh": sb("a_wbh", [128, 8192], BF16),
                "wbl": sb("a_wbl", [128, 8192], BF16),
                "ck": sb("a_ck", [128, 2048], F32),
                "sk": sb("a_sk", [128, 2048], F32),
                "stats": sb("a_stats", [128, 8], F32),
                "kf": sb("a_kf", [128, 512], F32),
                "kc": sb("a_kc", [128, 512], F32),
                "ks": sb("a_ks", [128, 512], F32),
                "ksw": sb("a_ksw", [128, 512], F32),
                "identf": sb("a_identf", [128, 128], F32),
                "tbuf": sb("a_tbuf", [128, 2048], F32),
                "ptr": pt("a_ptr", [128, 2048]),
                "pkv": pt("a_pkv", [128, 1024]),
            }
            with nc.Block() as blk:
                mk_emit(blk, lambda en, eng, C: _emit_a(en, eng, C, T))

    if "b1" in PHASES:
        with contextlib.ExitStack() as es:
            sb = lambda n, s, d: es.enter_context(nc.sbuf_tensor(n, s, d))
            pt = lambda n, s: es.enter_context(nc.psum_tensor(n, s, F32))
            T = {
                "xnt_h": sb("b_xnth", [128, 8192], BF16),
                "xnt_l": sb("b_xntl", [128, 8192], BF16),
                "cq": sb("b_cq", [128, 512], F32),
                "sq": sb("b_sq", [128, 512], F32),
                "wqb_h": sb("b_wqbh", [128, 4096], BF16),
                "wqb_l": sb("b_wqbl", [128, 4096], BF16),
                "qf": sb("b_qf", [128, 512], F32),
                "qc": sb("b_qc", [128, 512], F32),
                "qs": sb("b_qs", [128, 512], F32),
                "qsw": sb("b_qsw", [128, 512], F32),
                "q_h": sb("b_qh", [128, 1024], BF16),
                "q_l": sb("b_ql", [128, 1024], BF16),
                "p_f": sb("b_pf", [128, 2048], F32),
                "p_b": sb("b_pb", [128, 2048], BF16),
                "p_r": sb("b_pr", [128, 2048], BF16),
                "ao_h": sb("b_aoh", [128, HQ * 512], BF16),
                "ao_l": sb("b_aol", [128, HQ * 512], BF16),
                "ao_f": sb("b_aof", [128, 512], F32),
                "rr_b": sb("b_rrb", [128, 512], F32),
                "rs_row": sb("b_rsrow", [1, 512], F32),
                "rr_row": sb("b_rrrow", [1, 512], F32),
                "negc": sb("b_negc", [128, 1], F32),
                "ones_b": sb("b_onesb", [128, 1], BF16),
                "ones_rf": sb("b_onesrf", [1, 128], F32),
                "cmask": sb("b_cmask", [128, 2048], BF16),
                "identb": sb("b_identb", [128, 128], BF16),
                "pARE": pt("b_pARE", [128, 4096]),
                "wob_h": sb("b_wobh", [128, 2048], BF16),
                "wob_l": sb("b_wobl", [128, 2048], BF16),
                "s_half": sb("b_shalf", [128, 2048], F32),
            }
            with nc.Block() as blk:
                mk_emit(blk, lambda en, eng, C: _emit_b1(en, eng, C, T))

    if "devkv" in PHASES:
        dkh = nc.dram_tensor("dev_kh", [128, NKV * S], BF16, kind="ExternalOutput")
        dkl = nc.dram_tensor("dev_kl", [128, NKV * S], BF16, kind="ExternalOutput")
        dvf = nc.dram_tensor("dev_vf", [128, NTC * 512], F32, kind="ExternalOutput")
        with nc.Block() as blk:
            @blk.gpsimd
            def _(eng):
                eng.dma_start(out=dkh.ap()[:, :], in_=g_k_h[:]).then_inc(sem("devkv"), 16)
                eng.dma_start(out=dkl.ap()[:, :], in_=g_k_l[:]).then_inc(sem("devkv"), 16)
                eng.dma_start(out=dvf.ap()[:, :], in_=g_v_f[:]).then_inc(sem("devkv"), 16)
                eng.wait_ge(sem("devkv"), 48)

    pstack.close()

    if "c" in PHASES:
        with contextlib.ExitStack() as es:
            sb = lambda n, s, d: es.enter_context(nc.sbuf_tensor(n, s, d))
            pt = lambda n, s: es.enter_context(nc.psum_tensor(n, s, F32))
            T = {
                "xnt_h": sb("c_xnth", [128, 8192], BF16),
                "xnt_l": sb("c_xntl", [128, 8192], BF16),
                "gb": sb("c_gb", [128, 4096], BF16),
                "ub": sb("c_ub", [128, 4096], BF16),
                "db": sb("c_db", [128, 2048], BF16),
                "h_h": sb("c_hh", [128, 32 * 512], BF16),
                "h_l": sb("c_hl", [128, 32 * 512], BF16),
                "sg_s": sb("c_sgs", [128, 1024], F32),
                "hf_s": sb("c_hfs", [128, 1024], F32),
                "facc": sb("c_facc", [128, 4 * 2048], F32),
                "gsc": sb("c_gsc", [128, NIC], F32),
                "usc": sb("c_usc", [128, NIC], F32),
                "dscb": sb("c_dscb", [128, E], F32),
                "hs_row": sb("c_hsrow", [1, 1024], F32),
                "negh": sb("c_negh", [128, 4], F32),
                "onesb": sb("c_onesb", [128, 1], BF16),
                "onesf1": sb("c_onesf1", [1, 1], F32),
                "st_t": sb("c_stt", [128, 2048], F32),
                "xt_t": sb("c_xtt", [128, 2048], F32),
                "ep_g": sb("c_epg", [128, 2048], F32),
                "ps": pt("c_ps", [128, 4096]),
            }
            with nc.Block() as blk:
                mk_emit(blk, lambda en, eng, C: _emit_c(en, eng, C, T))

    return nc


def kernel(**inputs):
    in_maps = _host_prep(**inputs)
    nc = build_nc()
    res = run_bass_kernel_spmd(nc, in_maps, core_ids=list(range(8)))
    return np.stack([np.asarray(res.results[c]["out"]) for c in range(8)], axis=0)


# version 12
# speedup vs baseline: 1.0499x; 1.0398x over previous
import contextlib
import numpy as np
import ml_dtypes
import concourse.bass as bass
import concourse.mybir as mybir
from concourse.bass_utils import run_bass_kernel_spmd

F32 = mybir.dt.float32
BF16 = mybir.dt.bfloat16
I32 = mybir.dt.int32
AF = mybir.ActivationFunctionType
ALU = mybir.AluOpType
AX = mybir.AxisListType.X
bfdt = ml_dtypes.bfloat16

S = 2048
E = 2048
HQ = 16
NKV = 4
D = 128
II = 8192
NTC = 16
NQB = 4
NHC = 16
NIC = 64
CSHIFT = 8.0
EPS = 1e-5
RSQD = float(1.0 / np.sqrt(128.0))

ENGINES = ("sync", "tensor", "scalar", "vector", "gpsimd")
PHASES = ("p0", "a", "b1", "b2", "c")
C_NFB = NQB
DEV_OUT = ()



_TAB_CACHE = {}


def _rope_tables_f32():
    if "cs" not in _TAB_CACHE:
        import subprocess
        import sys
        import tempfile
        code = (
            "import jax\n"
            "jax.config.update('jax_platforms', 'cpu')\n"
            "jax.config.update('jax_enable_x64', True)\n"
            "import numpy as np, jax.numpy as jnp, sys\n"
            "inv = 1.0 / 10000.0 ** (jnp.arange(0, 128, 2, dtype=jnp.float32) / 128)\n"
            "t = jnp.arange(2048, dtype=jnp.float32)\n"
            "fr = jnp.outer(t, inv)\n"
            "np.save(sys.argv[1], np.stack([np.asarray(jnp.cos(fr)), np.asarray(jnp.sin(fr))]))\n"
        )
        with tempfile.TemporaryDirectory() as td:
            p = td + "/t.npy"
            subprocess.run([sys.executable, "-c", code, p], check=True,
                           capture_output=True)
            cs = np.load(p)
        _TAB_CACHE["cs"] = (cs[0].astype(np.float32), cs[1].astype(np.float32))
    return _TAB_CACHE["cs"]


def _host_prep(L0_attn_x, L0_attn_Wq, L0_attn_Wk, L0_attn_Wv, L0_attn_Wo,
               L0_ffn_gate_w, L0_ffn_gate_scale, L0_ffn_up_w, L0_ffn_up_scale,
               L0_ffn_down_w, L0_ffn_down_scale):
    cm = np.ascontiguousarray
    x = np.asarray(L0_attn_x, dtype=np.float32)
    shared = {
        "wq": cm(np.asarray(L0_attn_Wq, np.float32)),
        "wk": cm(np.asarray(L0_attn_Wk, np.float32)),
        "wv": cm(np.asarray(L0_attn_Wv, np.float32)),
        "wo": cm(np.asarray(L0_attn_Wo, np.float32)),
        "gwt": cm(np.asarray(L0_ffn_gate_w, np.int32).T),
        "uwt": cm(np.asarray(L0_ffn_up_w, np.int32).T),
        "dwt": cm(np.asarray(L0_ffn_down_w, np.int32).T),
        "gsc": cm(np.asarray(L0_ffn_gate_scale, np.float32).reshape(NIC, 128).T),
        "usc": cm(np.asarray(L0_ffn_up_scale, np.float32).reshape(NIC, 128).T),
        "dscb": cm(np.broadcast_to(
            np.asarray(L0_ffn_down_scale, np.float32)[None, :], (128, E))),
    }
    cos32, sin32 = _rope_tables_f32()
    cos = np.concatenate([cos32, cos32], axis=1).astype(np.float64)
    sinn = np.concatenate([-sin32, sin32], axis=1).astype(np.float64)
    for nm, tab in (("cosk", cos.T), ("sink", sinn.T),
                    ("cosq", cos.T * RSQD), ("sinq", sinn.T * RSQD)):
        hi = tab.astype(np.float32)
        shared[nm] = cm(hi)
        shared[nm + "_lo"] = cm((tab - hi.astype(np.float64)).astype(np.float32))
    cmask = np.zeros((128, 4 * 512), dtype=np.float32)
    tk = np.arange(128)[:, None]
    tq = np.arange(512)[None, :]
    for p in range(4):
        cmask[:, p * 512:(p + 1) * 512] = np.where(tq >= p * 128 + tk, 0.0, -1e30)
    shared["cmask"] = cm(cmask.astype(bfdt))
    shared["identb"] = cm(np.eye(128, dtype=bfdt))
    shared["identf"] = cm(np.eye(128, dtype=np.float32))
    return [dict(shared, x=cm(x[c])) for c in range(8)]


class Ctr(dict):
    def bump(self, k, n=1):
        self[k] = self.get(k, 0) + n
        return self[k]


def build_nc():
    nc = bass.Bass("TRN2", target_bir_lowering=False)

    dr = {}
    dr["x"] = nc.dram_tensor("x", [S, E], F32, kind="ExternalInput")
    dr["wq"] = nc.dram_tensor("wq", [E, E], F32, kind="ExternalInput")
    dr["wk"] = nc.dram_tensor("wk", [E, 512], F32, kind="ExternalInput")
    dr["wv"] = nc.dram_tensor("wv", [E, 512], F32, kind="ExternalInput")
    dr["wo"] = nc.dram_tensor("wo", [E, E], F32, kind="ExternalInput")
    dr["gwt"] = nc.dram_tensor("gwt", [E, II], I32, kind="ExternalInput")
    dr["uwt"] = nc.dram_tensor("uwt", [E, II], I32, kind="ExternalInput")
    dr["dwt"] = nc.dram_tensor("dwt", [II, E], I32, kind="ExternalInput")
    dr["gsc"] = nc.dram_tensor("gsc", [128, NIC], F32, kind="ExternalInput")
    dr["usc"] = nc.dram_tensor("usc", [128, NIC], F32, kind="ExternalInput")
    dr["dscb"] = nc.dram_tensor("dscb", [128, E], F32, kind="ExternalInput")
    for t in ("cosk", "sink", "cosq", "sinq"):
        dr[t] = nc.dram_tensor(t, [128, S], F32, kind="ExternalInput")
        dr[t + "_lo"] = nc.dram_tensor(t + "_lo", [128, S], F32, kind="ExternalInput")
    dr["cmask"] = nc.dram_tensor("cmask", [128, 4 * 512], BF16, kind="ExternalInput")
    dr["identb"] = nc.dram_tensor("identb", [128, 128], BF16, kind="ExternalInput")
    dr["identf"] = nc.dram_tensor("identf", [128, 128], F32, kind="ExternalInput")
    dr["out"] = nc.dram_tensor("out", [S, E], F32, kind="ExternalOutput")

    def scratch(name, shape, dt):
        kind = "ExternalOutput" if name in DEV_OUT else "Internal"
        return nc.dram_tensor(name, shape, dt, kind=kind)
    dr["wq_h"] = scratch("wq_h", [HQ, NHC, 128, 128], BF16)
    dr["wq_l"] = scratch("wq_l", [HQ, NHC, 128, 128], BF16)
    dr["wk_h"] = scratch("wk_h", [NHC, 128, 512], BF16)
    dr["wk_l"] = scratch("wk_l", [NHC, 128, 512], BF16)
    dr["wv_h"] = scratch("wv_h", [NHC, 128, 512], BF16)
    dr["wv_l"] = scratch("wv_l", [NHC, 128, 512], BF16)
    dr["wo_h"] = scratch("wo_h", [NHC, 128, E], BF16)
    dr["wo_l"] = scratch("wo_l", [NHC, 128, E], BF16)
    dr["xnt_h"] = scratch("xnt_h", [NQB, 128, 8192], BF16)
    dr["xnt_l"] = scratch("xnt_l", [NQB, 128, 8192], BF16)
    dr["s_s"] = scratch("s_s", [NTC, 128, E], F32)
    dr["g_b"] = scratch("g_b", [NIC, 128, 2048], BF16)
    dr["u_b"] = scratch("u_b", [NIC, 128, 2048], BF16)
    dr["d_b"] = scratch("d_b", [NIC, 128, 2048], BF16)

    sems = {}

    def sem(name):
        if name not in sems:
            sems[name] = nc.alloc_semaphore(name)
        return sems[name]

    ctrs = {}

    def emit_conv_jobs(en, eng, C, jobs, cbuf=None):
        for kind, ic in jobs:
            C.bump("cv_o", 16)
            if en == "gpsimd":
                if kind == "d":
                    srca = dr["dwt"].ap()[ic * 128:(ic + 1) * 128, :]
                    dsts = dr["d_b"].ap()[ic]
                else:
                    w = dr["gwt"] if kind == "g" else dr["uwt"]
                    srca = w.ap()[:, ic * 128:(ic + 1) * 128].rearrange(
                        "(hh p) j -> p hh j", p=128)
                    dsts = dr[{"g": "g_b", "u": "u_b"}[kind]].ap()[ic].rearrange(
                        "p (hh j) -> p hh j", j=128)
                eng.dma_start(out=dsts, in_=srca).then_inc(sem("cv_o"), 16)

    def mk_emit(block, fn):
        for en in ENGINES:
            def closure(eng, en=en):
                fn(en, eng, ctrs.setdefault(en, Ctr()))
            getattr(block, en)(closure)

    pstack = contextlib.ExitStack()
    g_k_h = pstack.enter_context(nc.sbuf_tensor("g_k_h", [128, NKV * S], BF16))
    g_k_l = pstack.enter_context(nc.sbuf_tensor("g_k_l", [128, NKV * S], BF16))
    g_v_h = pstack.enter_context(nc.sbuf_tensor("g_v_h", [128, NTC * 512], BF16))
    g_v_l = pstack.enter_context(nc.sbuf_tensor("g_v_l", [128, NTC * 512], BF16))

    def _emit_p0(en, eng, C, T):
        wtmp, whi, wlo, chunks = T["wtmp"], T["whi"], T["wlo"], T["chunks"]
        for i, (nm, src, cols, dsth, dstl) in enumerate(chunks):
            b = i % 2
            ldk = "p0_ld%d" % b
            stk = "p0_st%d" % b
            C.bump(ldk, 16)
            if en == "sync":
                if i >= 2:
                    eng.wait_ge(sem(stk), C[stk])
                eng.dma_start(out=wtmp[:, b * 2048: b * 2048 + cols], in_=src).then_inc(sem(ldk), 16)
            C.bump("p0_hi")
            if en == "scalar":
                eng.wait_ge(sem(ldk), C[ldk])
                if i >= 2:
                    eng.wait_ge(sem(stk), C[stk])
                eng.copy(whi[:, b * 2048: b * 2048 + cols],
                         wtmp[:, b * 2048: b * 2048 + cols]).then_inc(sem("p0_hi"), 1)
            C.bump("p0_lo")
            if en == "vector":
                eng.wait_ge(sem("p0_hi"), C["p0_hi"])
                eng.tensor_sub(wlo[:, b * 2048: b * 2048 + cols],
                               wtmp[:, b * 2048: b * 2048 + cols],
                               whi[:, b * 2048: b * 2048 + cols]).then_inc(sem("p0_lo"), 1)
            C.bump(stk, 32)
            if en == "gpsimd":
                eng.wait_ge(sem("p0_hi"), C["p0_hi"])
                srch = whi[:, b * 2048: b * 2048 + cols]
                srcl = wlo[:, b * 2048: b * 2048 + cols]
                if nm == "wq":
                    srch = srch.rearrange("p (a j) -> p a j", j=128)
                    srcl = srcl.rearrange("p (a j) -> p a j", j=128)
                eng.dma_start(out=dsth, in_=srch).then_inc(sem(stk), 16)
                eng.wait_ge(sem("p0_lo"), C["p0_lo"])
                eng.dma_start(out=dstl, in_=srcl).then_inc(sem(stk), 16)
        if en == "gpsimd":
            eng.wait_ge(sem("p0_st0"), C["p0_st0"])
            eng.wait_ge(sem("p0_st1"), C["p0_st1"])

    def _emit_a(en, eng, C, T):
        xt, xn = T["xt"], T["xn"]
        xnt_h, xnt_l = T["xnt_h"], T["xnt_l"]
        wbh, wbl = T["wbh"], T["wbl"]
        ck, sk, stats = T["ck"], T["sk"], T["stats"]
        kf, kc, ks, ksw = T["kf"], T["kc"], T["ks"], T["ksw"]
        ptr, pkv = T["ptr"], T["pkv"]
        identf = T["identf"]

        tbuf = T["tbuf"]
        C.bump("a_id", 16)
        C.bump("a_if", 16 * 2)
        C.bump("a_tl0", 16)
        if en == "sync":
            eng.dma_start(out=identf[:], in_=dr["identf"].ap()[:, :]).then_inc(sem("a_id"), 16)
            eng.dma_start(out=ck[:], in_=dr["cosk"].ap()[:, :]).then_inc(sem("a_if"), 16)
            eng.dma_start(out=sk[:], in_=dr["sink"].ap()[:, :]).then_inc(sem("a_if"), 16)
            eng.dma_start(out=tbuf[:], in_=dr["cosk_lo"].ap()[:, :]).then_inc(sem("a_tl0"), 16)
        C.bump("a_tl1")
        if en == "vector":
            eng.wait_ge(sem("a_if"), 32)
            eng.wait_ge(sem("a_tl0"), C["a_tl0"])
            eng.tensor_add(ck[:], ck[:], tbuf[:]).then_inc(sem("a_tl1"), 1)
        C.bump("a_tl0", 16)
        if en == "sync":
            eng.wait_ge(sem("a_tl1"), C["a_tl1"])
            eng.dma_start(out=tbuf[:], in_=dr["sink_lo"].ap()[:, :]).then_inc(sem("a_tl0"), 16)
        C.bump("a_tl1")
        if en == "vector":
            eng.wait_ge(sem("a_tl0"), C["a_tl0"])
            eng.tensor_add(sk[:], sk[:], tbuf[:]).then_inc(sem("a_tl1"), 1)
        C.bump("a_eps")
        if en == "gpsimd":
            eng.memset(stats[:, 7:8], EPS).then_inc(sem("a_eps"), 1)

        for qb in range(NQB):
            for j in range(4):
                tc = qb * 4 + j
                b = tc % 2
                axk = "a_x%d" % b
                C.bump(axk, 16)
                if en == "sync":
                    if tc >= 2:
                        eng.wait_ge(sem("a_sq"), tc - 1)
                        eng.wait_ge(sem("a_xn"), 2 * tc - 2)
                    eng.dma_start(out=xt[:, b * 2048:(b + 1) * 2048],
                                  in_=dr["x"].ap()[tc * 128:(tc + 1) * 128, :]).then_inc(sem(axk), 16)
                C.bump("a_sq")
                if en == "scalar":
                    eng.wait_ge(sem(axk), C[axk])
                    eng.wait_ge(sem("a_tr"), 2 * tc)
                    eng.activation(xn[:], xt[:, b * 2048:(b + 1) * 2048], AF.Square,
                                   accum_out=stats[:, 3:4]).then_inc(sem("a_sq"), 1)
                C.bump("a_vr", 4)
                if en == "vector":
                    eng.wait_ge(sem(axk), C[axk])
                    eng.reduce_sum(stats[:, 0:1], xt[:, b * 2048:(b + 1) * 2048], AX).then_inc(sem("a_vr"), 1)
                    eng.wait_ge(sem("a_vr"), C["a_vr"] - 3)
                    eng.tensor_scalar_mul(stats[:, 1:2], stats[:, 0:1], 1.0 / E).then_inc(sem("a_vr"), 1)
                    eng.wait_ge(sem("a_vr"), C["a_vr"] - 2)
                    eng.tensor_mul(stats[:, 2:3], stats[:, 1:2], stats[:, 1:2]).then_inc(sem("a_vr"), 1)
                    eng.wait_ge(sem("a_sq"), C["a_sq"])
                    if tc == 0:
                        eng.wait_ge(sem("a_eps"), 1)
                    eng.tensor_scalar(stats[:, 4:5], stats[:, 3:4], 1.0 / E, None, ALU.mult).then_inc(sem("a_vr"), 1)
                    eng.wait_ge(sem("a_vr"), C["a_vr"])
                C.bump("a_vs")
                if en == "vector":
                    eng.tensor_sub(stats[:, 4:5], stats[:, 4:5], stats[:, 2:3]).then_inc(sem("a_vs"), 1)
                C.bump("a_sd")
                if en == "scalar":
                    eng.wait_ge(sem("a_vs"), C["a_vs"])
                    eng.activation(stats[:, 5:6], stats[:, 4:5], AF.Sqrt,
                                   bias=stats[:, 7:8]).then_inc(sem("a_sd"), 1)
                C.bump("a_xn")
                if en == "vector":
                    eng.wait_ge(sem("a_sd"), C["a_sd"])
                    eng.reciprocal(stats[:, 6:7], stats[:, 5:6]).then_inc(sem("a_xn"), 1)
                    eng.wait_ge(sem("a_xn"), C["a_xn"])
                C.bump("a_xn")
                if en == "vector":
                    eng.wait_ge(sem("a_tr"), 2 * tc)
                    eng.tensor_scalar(xn[:], xt[:, b * 2048:(b + 1) * 2048],
                                      stats[:, 1:2], stats[:, 6:7],
                                      ALU.subtract, ALU.mult).then_inc(sem("a_xn"), 1)
                for g in range(2):
                    gidx = C.bump("a_tr")
                    if en == "tensor":
                        if tc == 0 and g == 0:
                            eng.wait_ge(sem("a_id"), 16)
                        eng.wait_ge(sem("a_xn"), C["a_xn"])
                        if gidx > 2:
                            eng.wait_ge(sem("a_evh"), gidx - 2)
                            eng.wait_ge(sem("a_evl"), gidx - 2)
                        if j == 0 and g == 0 and qb > 0:
                            eng.wait_ge(sem("a_vevh"), C["a_vevh"])
                            eng.wait_ge(sem("a_vevl"), C["a_vevl"])
                        for hl in range(8):
                            hc = g * 8 + hl
                            ins = eng.transpose(
                                ptr[:, g * 1024 + hl * 128: g * 1024 + (hl + 1) * 128],
                                xn[:, hc * 128:(hc + 1) * 128], identf[:])
                        ins.then_inc(sem("a_tr"), 1)
                    dsth = xnt_h[:].rearrange("p (h c) -> p h c", c=512)[
                        :, g * 8:(g + 1) * 8, j * 128:(j + 1) * 128]
                    dstl = xnt_l[:].rearrange("p (h c) -> p h c", c=512)[
                        :, g * 8:(g + 1) * 8, j * 128:(j + 1) * 128]
                    srcp = ptr[:, g * 1024:(g + 1) * 1024].rearrange("p (h c) -> p h c", c=128)
                    C.bump("a_evh")
                    if en == "scalar":
                        eng.wait_ge(sem("a_tr"), C["a_tr"])
                        eng.copy(dsth, srcp).then_inc(sem("a_evh"), 1)
                    C.bump("a_evl")
                    if en == "vector":
                        eng.wait_ge(sem("a_evh"), C["a_evh"])
                        eng.tensor_sub(dstl, srcp, dsth).then_inc(sem("a_evl"), 1)

            C.bump("a_wk", 32)
            if en == "sync":
                if qb > 0:
                    eng.wait_ge(sem("a_vm"), C["a_vm"])
                eng.dma_start(out=wbh[:].rearrange("p (h j) -> p h j", j=512),
                              in_=dr["wk_h"].ap().transpose([1, 0, 2])).then_inc(sem("a_wk"), 16)
                eng.dma_start(out=wbl[:].rearrange("p (h j) -> p h j", j=512),
                              in_=dr["wk_l"].ap().transpose([1, 0, 2])).then_inc(sem("a_wk"), 16)
            for kvh in range(NKV):
                pb = kvh % 2
                C.bump("a_km")
                if en == "tensor":
                    eng.wait_ge(sem("a_evh"), C["a_evh"])
                    eng.wait_ge(sem("a_evl"), C["a_evl"])
                    eng.wait_ge(sem("a_wk"), C["a_wk"])
                    if C["a_km"] > 2:
                        eng.wait_ge(sem("a_kev"), C["a_km"] - 2)
                    first = True
                    for hc in range(NHC):
                        wt_h = wbh[:, hc * 512 + kvh * 128: hc * 512 + (kvh + 1) * 128]
                        wt_l = wbl[:, hc * 512 + kvh * 128: hc * 512 + (kvh + 1) * 128]
                        rh = xnt_h[:, hc * 512:(hc + 1) * 512]
                        rl = xnt_l[:, hc * 512:(hc + 1) * 512]
                        dst = pkv[:, pb * 512:(pb + 1) * 512]
                        eng.matmul(dst, wt_h, rh, start=first, stop=False)
                        first = False
                        eng.matmul(dst, wt_h, rl, start=False, stop=False)
                        ins = eng.matmul(dst, wt_l, rh, start=False, stop=(hc == NHC - 1))
                    ins.then_inc(sem("a_km"), 1)
                C.bump("a_kev")
                if en == "scalar":
                    eng.wait_ge(sem("a_km"), C["a_km"])
                    if C["a_kev"] > 1:
                        eng.wait_ge(sem("a_krs"), C["a_kev"] - 1)
                    eng.copy(kf[:], pkv[:, pb * 512:(pb + 1) * 512])
                    eng.copy(ksw[0:64, :], pkv[64:128, pb * 512:(pb + 1) * 512])
                    eng.copy(ksw[64:128, :], pkv[0:64, pb * 512:(pb + 1) * 512]).then_inc(sem("a_kev"), 1)
                C.bump("a_vv", 3)
                C.bump("a_krs")
                if en == "vector":
                    eng.wait_ge(sem("a_kev"), C["a_kev"])
                    if qb == 0 and kvh == 0:
                        eng.wait_ge(sem("a_tl1"), 2)
                    cs = ck[:, qb * 512:(qb + 1) * 512]
                    ss = sk[:, qb * 512:(qb + 1) * 512]
                    eng.tensor_mul(kc[:], kf[:], cs)
                    eng.tensor_mul(ks[:], ksw[:], ss).then_inc(sem("a_vv"), 1)
                    eng.wait_ge(sem("a_vv"), C["a_vv"] - 2)
                    eng.tensor_add(kc[:], kc[:], ks[:]).then_inc(sem("a_vv"), 1)
                    eng.wait_ge(sem("a_vv"), C["a_vv"] - 1)
                    dh = g_k_h[:, kvh * S + qb * 512: kvh * S + (qb + 1) * 512]
                    dl = g_k_l[:, kvh * S + qb * 512: kvh * S + (qb + 1) * 512]
                    eng.tensor_copy(dh, kc[:]).then_inc(sem("a_vv"), 1)
                    eng.wait_ge(sem("a_vv"), C["a_vv"])
                    eng.tensor_sub(dl, kc[:], dh).then_inc(sem("a_krs"), 1)

            C.bump("a_wv", 32)
            if en == "sync":
                eng.wait_ge(sem("a_km"), C["a_km"])
                eng.dma_start(out=wbh[:].rearrange("p (h j) -> p h j", j=512),
                              in_=dr["wv_h"].ap().transpose([1, 0, 2])).then_inc(sem("a_wv"), 16)
                eng.dma_start(out=wbl[:].rearrange("p (h j) -> p h j", j=512),
                              in_=dr["wv_l"].ap().transpose([1, 0, 2])).then_inc(sem("a_wv"), 16)
            for j in range(4):
                tc = qb * 4 + j
                pb = j % 2
                C.bump("a_vm")
                if en == "tensor":
                    eng.wait_ge(sem("a_wv"), C["a_wv"])
                    if C["a_vm"] > 2:
                        eng.wait_ge(sem("a_vevh"), C["a_vm"] - 2)
                        eng.wait_ge(sem("a_vevl"), C["a_vm"] - 2)
                    first = True
                    for hc in range(NHC):
                        lt_h = xnt_h[:, hc * 512 + j * 128: hc * 512 + (j + 1) * 128]
                        lt_l = xnt_l[:, hc * 512 + j * 128: hc * 512 + (j + 1) * 128]
                        rh = wbh[:, hc * 512:(hc + 1) * 512]
                        rl = wbl[:, hc * 512:(hc + 1) * 512]
                        dst = ptr[:, pb * 512:(pb + 1) * 512]
                        eng.matmul(dst, lt_h, rh, start=first, stop=False)
                        first = False
                        eng.matmul(dst, lt_h, rl, start=False, stop=False)
                        ins = eng.matmul(dst, lt_l, rh, start=False, stop=(hc == NHC - 1))
                    ins.then_inc(sem("a_vm"), 1)
                C.bump("a_vevh")
                if en == "scalar":
                    eng.wait_ge(sem("a_vm"), C["a_vm"])
                    eng.copy(g_v_h[:, tc * 512:(tc + 1) * 512],
                             ptr[:, pb * 512:(pb + 1) * 512]).then_inc(sem("a_vevh"), 1)
                C.bump("a_vevl")
                if en == "vector":
                    eng.wait_ge(sem("a_vevh"), C["a_vevh"])
                    eng.tensor_sub(g_v_l[:, tc * 512:(tc + 1) * 512],
                                   ptr[:, pb * 512:(pb + 1) * 512],
                                   g_v_h[:, tc * 512:(tc + 1) * 512]).then_inc(sem("a_vevl"), 1)

            C.bump("a_sp", 32)
            if en == "gpsimd":
                eng.wait_ge(sem("a_evh"), C["a_evh"])
                eng.wait_ge(sem("a_evl"), C["a_evl"])
                eng.dma_start(out=dr["xnt_h"].ap()[qb], in_=xnt_h[:]).then_inc(sem("a_sp"), 16)
                eng.dma_start(out=dr["xnt_l"].ap()[qb], in_=xnt_l[:]).then_inc(sem("a_sp"), 16)
            emit_conv_jobs(en, eng, C, [("g", qb * 16 + i) for i in range(16)])
            if qb < NQB - 1:
                if en in ("scalar", "vector"):
                    eng.wait_ge(sem("a_sp"), C["a_sp"])
                    eng.wait_ge(sem("a_vm"), C["a_vm"])
        if en == "gpsimd":
            eng.wait_ge(sem("a_sp"), C["a_sp"])

    def _emit_b1(en, eng, C, T):
        xnt_h, xnt_l = T["xnt_h"], T["xnt_l"]
        cq, sq = T["cq"], T["sq"]
        wqb_h, wqb_l = T["wqb_h"], T["wqb_l"]
        qf, qc, qs, qsw = T["qf"], T["qc"], T["qs"], T["qsw"]
        q_h, q_l = T["q_h"], T["q_l"]
        p_f, p_b, p_r = T["p_f"], T["p_b"], T["p_r"]
        ao_h, ao_l, ao_f = T["ao_h"], T["ao_l"], T["ao_f"]
        rr_b, rs_row, rr_row = T["rr_b"], T["rs_row"], T["rr_row"]
        negc, ones_b, ones_rf = T["negc"], T["ones_b"], T["ones_rf"]
        cmask, identb = T["cmask"], T["identb"]
        pARE = T["pARE"]
        pS = pARE[:, 0:2048]
        psQ = pARE[:, 2048:2560]
        pAO = pARE[:, 2560:3072]
        pRS = pARE[:, 3072:3584]
        wob_h, wob_l, s_half = T["wob_h"], T["wob_l"], T["s_half"]

        C.bump("b_cl", 32)
        if en == "sync":
            eng.dma_start(out=cmask[:], in_=dr["cmask"].ap()[:, :]).then_inc(sem("b_cl"), 16)
            eng.dma_start(out=identb[:], in_=dr["identb"].ap()[:, :]).then_inc(sem("b_cl"), 16)
        C.bump("b_ic")
        if en == "gpsimd":
            eng.memset(negc[:], -CSHIFT)
            eng.memset(ones_b[:], 1.0)
            eng.memset(ones_rf[:], 1.0).then_inc(sem("b_ic"), 1)

        gg = 0
        for qb in range(NQB):
            C.bump("b_ld", 16 * 4)
            if en == "sync":
                if qb >= 1:
                    eng.wait_ge(sem("b_qbdone"), qb)
                eng.dma_start(out=xnt_h[:], in_=dr["xnt_h"].ap()[qb]).then_inc(sem("b_ld"), 16)
                eng.dma_start(out=xnt_l[:], in_=dr["xnt_l"].ap()[qb]).then_inc(sem("b_ld"), 16)
                eng.dma_start(out=cq[:], in_=dr["cosq"].ap()[:, qb * 512:(qb + 1) * 512]).then_inc(sem("b_ld"), 16)
                eng.dma_start(out=sq[:], in_=dr["sinq"].ap()[:, qb * 512:(qb + 1) * 512]).then_inc(sem("b_ld"), 16)
            C.bump("b_tl0", 32)
            if en == "sync":
                if qb > 0:
                    eng.wait_ge(sem("b_qrs"), C["b_qrs"])
                eng.dma_start(out=qc[:], in_=dr["cosq_lo"].ap()[:, qb * 512:(qb + 1) * 512]).then_inc(sem("b_tl0"), 16)
                eng.dma_start(out=qs[:], in_=dr["sinq_lo"].ap()[:, qb * 512:(qb + 1) * 512]).then_inc(sem("b_tl0"), 16)
            C.bump("b_tl1")
            if en == "vector":
                eng.wait_ge(sem("b_ld"), C["b_ld"])
                eng.wait_ge(sem("b_tl0"), C["b_tl0"])
                eng.tensor_add(cq[:], cq[:], qc[:]).then_inc(sem("b_tl1"), 1)
                eng.wait_ge(sem("b_tl1"), C["b_tl1"])
            C.bump("b_tl1")
            if en == "vector":
                eng.tensor_add(sq[:], sq[:], qs[:]).then_inc(sem("b_tl1"), 1)
            L = 4 * qb + 4
            groups = [list(range(L))[i:i + 2] for i in range(0, L, 2)]

            def emit_qproj(qh):
                kvh = qh // 4
                qpar = qh % 2
                wqk = "b_wq%d" % (qh % 2)
                C.bump(wqk, 32)
                if en == "sync":
                    if C[wqk] > 32:
                        eng.wait_ge(sem("b_qm"), C["b_qm"] - 1)
                    eng.dma_start(out=wqb_h[:, (qh % 2) * 2048:(qh % 2 + 1) * 2048
                                              ].rearrange("p (h j) -> p h j", j=128),
                                  in_=dr["wq_h"].ap()[qh].transpose([1, 0, 2])
                                  ).then_inc(sem(wqk), 16)
                    eng.dma_start(out=wqb_l[:, (qh % 2) * 2048:(qh % 2 + 1) * 2048
                                              ].rearrange("p (h j) -> p h j", j=128),
                                  in_=dr["wq_l"].ap()[qh].transpose([1, 0, 2])
                                  ).then_inc(sem(wqk), 16)
                C.bump("b_qm")
                if en == "tensor":
                    eng.wait_ge(sem("b_ld"), C["b_ld"])
                    eng.wait_ge(sem(wqk), C[wqk])
                    if qh == 0:
                        eng.wait_ge(sem("w_ev"), C.get("w_ev", 0))
                    if C["b_qm"] > 1:
                        eng.wait_ge(sem("b_qev"), C["b_qm"] - 1)
                    first = True
                    for hc in range(NHC):
                        off = (qh % 2) * 2048 + hc * 128
                        wt_h = wqb_h[:, off: off + 128]
                        wt_l = wqb_l[:, off: off + 128]
                        rh = xnt_h[:, hc * 512:(hc + 1) * 512]
                        rl = xnt_l[:, hc * 512:(hc + 1) * 512]
                        eng.matmul(psQ, wt_h, rh, start=first, stop=False)
                        first = False
                        eng.matmul(psQ, wt_h, rl, start=False, stop=False)
                        ins = eng.matmul(psQ, wt_l, rh, start=False, stop=(hc == NHC - 1))
                    ins.then_inc(sem("b_qm"), 1)
                C.bump("b_qev")
                if en == "scalar":
                    eng.wait_ge(sem("b_qm"), C["b_qm"])
                    if C["b_qev"] > 1:
                        eng.wait_ge(sem("b_qrs"), C["b_qev"] - 1)
                    eng.copy(qf[:], psQ)
                    eng.copy(qsw[0:64, :], psQ[64:128, :])
                    eng.copy(qsw[64:128, :], psQ[0:64, :]).then_inc(sem("b_qev"), 1)
                C.bump("b_vv", 3)
                C.bump("b_qrs")
                if en == "vector":
                    if qh == 0:
                        eng.wait_ge(sem("b_tl1"), C["b_tl1"])
                    eng.wait_ge(sem("b_qev"), C["b_qev"])
                    eng.tensor_mul(qc[:], qf[:], cq[:])
                    eng.tensor_mul(qs[:], qsw[:], sq[:]).then_inc(sem("b_vv"), 1)
                    eng.wait_ge(sem("b_vv"), C["b_vv"] - 2)
                    eng.tensor_add(qc[:], qc[:], qs[:]).then_inc(sem("b_vv"), 1)
                    eng.wait_ge(sem("b_vv"), C["b_vv"] - 1)
                    qhsl = q_h[:, qpar * 512:(qpar + 1) * 512]
                    qlsl = q_l[:, qpar * 512:(qpar + 1) * 512]
                    eng.tensor_copy(qhsl, qc[:]).then_inc(sem("b_vv"), 1)
                    eng.wait_ge(sem("b_vv"), C["b_vv"])
                    eng.tensor_sub(qlsl, qc[:], qhsl).then_inc(sem("b_qrs"), 1)

            def emit_S(qh, gi, grp, gg):
                kvh = qh // 4
                qpar = qh % 2
                slot = gg % 2
                C.bump("b_S")
                if en == "tensor":
                    if gi == 0:
                        eng.wait_ge(sem("b_qrs"), C["b_qrs"])
                        if qb == 0 and qh == 0:
                            eng.wait_ge(sem("b_ic"), 1)
                            eng.wait_ge(sem("b_cl"), 32)
                    if gg > 2:
                        eng.wait_ge(sem("b_exp"), gg - 2)
                    for ti, tkc in enumerate(grp):
                        dst = pS[:, slot * 1024 + ti * 512: slot * 1024 + (ti + 1) * 512]
                        st = True
                        if tkc >= qb * 4:
                            o = tkc - qb * 4
                            eng.matmul(dst, identb[:], cmask[:, o * 512:(o + 1) * 512],
                                       start=True, stop=False)
                            st = False
                        kh = g_k_h[:, kvh * S + tkc * 128: kvh * S + (tkc + 1) * 128]
                        kl = g_k_l[:, kvh * S + tkc * 128: kvh * S + (tkc + 1) * 128]
                        qhsl = q_h[:, qpar * 512:(qpar + 1) * 512]
                        qlsl = q_l[:, qpar * 512:(qpar + 1) * 512]
                        eng.matmul(dst, kh, qhsl, start=st, stop=False)
                        eng.matmul(dst, kh, qlsl, start=False, stop=False)
                        ins = eng.matmul(dst, kl, qhsl, start=False, stop=True)
                    ins.then_inc(sem("b_S"), 1)
                C.bump("b_exp")
                if en == "scalar":
                    eng.wait_ge(sem("b_S"), C["b_S"])
                    if gg > 2:
                        eng.wait_ge(sem("b_pv"), gg - 2)
                        eng.wait_ge(sem("b_cast"), gg - 2)
                    n = len(grp)
                    eng.activation(p_f[:, slot * 1024: slot * 1024 + n * 512],
                                   pS[:, slot * 1024: slot * 1024 + n * 512],
                                   AF.Exp, bias=negc[:, 0:1]).then_inc(sem("b_exp"), 1)
                C.bump("b_vv")
                C.bump("b_cast")
                if en == "vector":
                    eng.wait_ge(sem("b_exp"), C["b_exp"])
                    n = len(grp)
                    eng.tensor_copy(p_b[:, slot * 1024: slot * 1024 + n * 512],
                                    p_f[:, slot * 1024: slot * 1024 + n * 512]
                                    ).then_inc(sem("b_vv"), 1)
                    eng.wait_ge(sem("b_vv"), C["b_vv"])
                    eng.tensor_sub(p_r[:, slot * 1024: slot * 1024 + n * 512],
                                   p_f[:, slot * 1024: slot * 1024 + n * 512],
                                   p_b[:, slot * 1024: slot * 1024 + n * 512]
                                   ).then_inc(sem("b_cast"), 1)

            def emit_PV(qh, gi, grp, gg, ng):
                kvh = qh // 4
                C.bump("b_pv")
                if en == "tensor":
                    eng.wait_ge(sem("b_cast"), C["b_cast_at%d" % gg])
                    if gi == 0:
                        eng.wait_ge(sem("b_aoev"), C.get("b_aoev", 0))
                    slot = gg % 2
                    for ti, tkc in enumerate(grp):
                        vh = g_v_h[:, tkc * 512 + kvh * 128: tkc * 512 + (kvh + 1) * 128]
                        vl = g_v_l[:, tkc * 512 + kvh * 128: tkc * 512 + (kvh + 1) * 128]
                        pb_ = p_b[:, slot * 1024 + ti * 512: slot * 1024 + (ti + 1) * 512]
                        pr_ = p_r[:, slot * 1024 + ti * 512: slot * 1024 + (ti + 1) * 512]
                        eng.matmul(pAO[:, 0:512], vh, pb_,
                                   start=(gi == 0 and ti == 0), stop=False)
                        eng.matmul(pAO[:, 0:512], vh, pr_, start=False, stop=False)
                        eng.matmul(pAO[:, 0:512], vl, pb_, start=False,
                                   stop=(gi == ng - 1 and ti == len(grp) - 1))
                    if gi == 0:
                        eng.wait_ge(sem("b_rrev"), C.get("b_rrev", 0))
                    for ti, tkc in enumerate(grp):
                        eng.matmul(pRS[0:1, 0:512], ones_b[:],
                                   p_b[:, slot * 1024 + ti * 512: slot * 1024 + (ti + 1) * 512],
                                   start=(gi == 0 and ti == 0), stop=False)
                        ins = eng.matmul(pRS[0:1, 0:512], ones_b[:],
                                         p_r[:, slot * 1024 + ti * 512: slot * 1024 + (ti + 1) * 512],
                                         start=False,
                                         stop=(gi == ng - 1 and ti == len(grp) - 1))
                    ins.then_inc(sem("b_pv"), 1)

            emit_qproj(0)
            for qh in range(HQ):
                ng = len(groups)
                pend = None
                for gi, grp in enumerate(groups):
                    gg = C.bump("b_gg")
                    C["b_exp_at%d" % gg] = 0
                    emit_S(qh, gi, grp, gg)
                    C["b_exp_at%d" % gg] = C["b_exp"]
                    C["b_cast_at%d" % gg] = C["b_cast"]
                    if pend is not None:
                        emit_PV(qh, pend[0], pend[1], pend[2], ng)
                    pend = (gi, grp, gg)
                if qh + 1 < HQ:
                    emit_qproj(qh + 1)
                emit_PV(qh, pend[0], pend[1], pend[2], ng)
                C.bump("b_rsev")
                if en == "scalar":
                    eng.wait_ge(sem("b_pv"), C["b_pv"])
                    eng.copy(rs_row[:], pRS[0:1, 0:512]).then_inc(sem("b_rsev"), 1)
                C.bump("b_rcp")
                if en == "vector":
                    eng.wait_ge(sem("b_rsev"), C["b_rsev"])
                    eng.reciprocal(rr_row[:], rs_row[:]).then_inc(sem("b_rcp"), 1)
                C.bump("b_bc")
                if en == "tensor":
                    eng.wait_ge(sem("b_rcp"), C["b_rcp"])
                    ins = eng.matmul(pRS[:, 0:512], ones_rf[0:1, :], rr_row[:],
                                     start=True, stop=True)
                    ins.then_inc(sem("b_bc"), 1)
                C.bump("b_rrev")
                if en == "scalar":
                    eng.wait_ge(sem("b_bc"), C["b_bc"])
                    eng.copy(rr_b[:], pRS[:, 0:512]).then_inc(sem("b_rrev"), 1)
                C.bump("b_aoev")
                if en == "scalar":
                    eng.wait_ge(sem("b_pv"), C["b_pv"])
                    if C["b_aoev"] > 1:
                        eng.wait_ge(sem("b_ao"), C["b_aoev"] - 1)
                    eng.copy(ao_f[:], pAO[:, 0:512]).then_inc(sem("b_aoev"), 1)
                C.bump("b_vv", 2)
                C.bump("b_ao")
                if en == "vector":
                    if qh == 0 and qb > 0:
                        eng.wait_ge(sem("w_mm"), C.get("w_mm", 0))
                    eng.wait_ge(sem("b_rrev"), C["b_rrev"])
                    eng.wait_ge(sem("b_aoev"), C["b_aoev"])
                    eng.tensor_mul(ao_f[:], ao_f[:], rr_b[:]).then_inc(sem("b_vv"), 1)
                    eng.wait_ge(sem("b_vv"), C["b_vv"] - 1)
                    dh = ao_h[:, qh * 512:(qh + 1) * 512]
                    dl = ao_l[:, qh * 512:(qh + 1) * 512]
                    eng.tensor_copy(dh, ao_f[:]).then_inc(sem("b_vv"), 1)
                    eng.wait_ge(sem("b_vv"), C["b_vv"])
                    eng.tensor_sub(dl, ao_f[:], dh).then_inc(sem("b_ao"), 1)
            C.bump("b_qbdone")
            if en == "gpsimd":
                eng.wait_ge(sem("b_qm"), C["b_qm"])
                eng.sem_inc(sem("b_qbdone"), 1)
            for eh in range(2):
                for dc in range(NHC):
                    wlk = "w_wl%d" % (dc % 2)
                    C.bump(wlk, 32)
                    if en == "sync":
                        if C[wlk] > 32:
                            eng.wait_ge(sem("w_mm"), C.get("w_mm", 0) - 1)
                        eng.dma_start(out=wob_h[:, (dc % 2) * 1024:(dc % 2 + 1) * 1024],
                                      in_=dr["wo_h"].ap()[dc][:, eh * 1024:(eh + 1) * 1024]
                                      ).then_inc(sem(wlk), 16)
                        eng.dma_start(out=wob_l[:, (dc % 2) * 1024:(dc % 2 + 1) * 1024],
                                      in_=dr["wo_l"].ap()[dc][:, eh * 1024:(eh + 1) * 1024]
                                      ).then_inc(sem(wlk), 16)
                    C.bump("w_mm")
                    if en == "tensor":
                        eng.wait_ge(sem(wlk), C[wlk])
                        if eh == 0 and dc == 0:
                            eng.wait_ge(sem("b_ao"), C["b_ao"])
                        if dc == 0:
                            eng.wait_ge(sem("w_ev"), C.get("w_ev", 0))
                        wb = dc % 2
                        for tqc in range(4):
                            ah = ao_h[:, dc * 512 + tqc * 128: dc * 512 + (tqc + 1) * 128]
                            al = ao_l[:, dc * 512 + tqc * 128: dc * 512 + (tqc + 1) * 128]
                            d0 = pARE[:, tqc * 1024: tqc * 1024 + 512]
                            d1 = pARE[:, tqc * 1024 + 512: (tqc + 1) * 1024]
                            wh0 = wob_h[:, wb * 1024: wb * 1024 + 512]
                            wh1 = wob_h[:, wb * 1024 + 512: (wb + 1) * 1024]
                            wl0 = wob_l[:, wb * 1024: wb * 1024 + 512]
                            wl1 = wob_l[:, wb * 1024 + 512: (wb + 1) * 1024]
                            st = dc == 0
                            sp = dc == NHC - 1
                            eng.matmul(d0, ah, wh0, start=st, stop=False)
                            eng.matmul(d1, ah, wh1, start=st, stop=False)
                            eng.matmul(d0, ah, wl0, start=False, stop=False)
                            eng.matmul(d1, ah, wl1, start=False, stop=False)
                            eng.matmul(d0, al, wh0, start=False, stop=sp)
                            ins = eng.matmul(d1, al, wh1, start=False, stop=sp)
                        ins.then_inc(sem("w_mm"), 1)
                for tqc in range(4):
                    spar = tqc % 2
                    sok = "w_sout%d" % spar
                    C.bump("w_ev")
                    if en == "scalar":
                        eng.wait_ge(sem("w_mm"), C["w_mm"])
                        eng.wait_ge(sem(sok), C.get(sok, 0))
                        eng.copy(s_half[:, spar * 1024:(spar + 1) * 1024],
                                 pARE[:, tqc * 1024:(tqc + 1) * 1024]).then_inc(sem("w_ev"), 1)
                    C.bump(sok, 16)
                    if en == "gpsimd":
                        eng.wait_ge(sem("w_ev"), C["w_ev"])
                        tc = qb * 4 + tqc
                        eng.dma_start(out=dr["s_s"].ap()[tc][:, eh * 1024:(eh + 1) * 1024],
                                      in_=s_half[:, spar * 1024:(spar + 1) * 1024]
                                      ).then_inc(sem(sok), 16)
            emit_conv_jobs(en, eng, C,
                           [("u", qb * 16 + i) for i in range(16)]
                           + [("d", qb * 16 + i) for i in range(16)])


        if en == "gpsimd":
            eng.wait_ge(sem("cv_o"), C.get("cv_o", 0))
            eng.wait_ge(sem("w_sout0"), C.get("w_sout0", 0))
            eng.wait_ge(sem("w_sout1"), C.get("w_sout1", 0))

    def _emit_c(en, eng, C, T):
        xnt_h, xnt_l = T["xnt_h"], T["xnt_l"]
        gb, ub, db = T["gb"], T["ub"], T["db"]
        h_h, h_l = T["h_h"], T["h_l"]
        sg_s, hf_s, facc = T["sg_s"], T["hf_s"], T["facc"]
        gsc, usc, dscb = T["gsc"], T["usc"], T["dscb"]
        hs_row, negh = T["hs_row"], T["negh"]
        onesb, onesf1 = T["onesb"], T["onesf1"]
        st_t, xt_t, ep_g = T["st_t"], T["xt_t"], T["ep_g"]
        ps = T["ps"]

        C.bump("c_ic", 16 * 3)
        if en == "sync":
            eng.dma_start(out=gsc[:], in_=dr["gsc"].ap()[:, :]).then_inc(sem("c_ic"), 16)
            eng.dma_start(out=usc[:], in_=dr["usc"].ap()[:, :]).then_inc(sem("c_ic"), 16)
            eng.dma_start(out=dscb[:], in_=dr["dscb"].ap()[:, :]).then_inc(sem("c_ic"), 16)
        C.bump("c_ic2")
        if en == "gpsimd":
            eng.memset(onesb[:], 1.0)
            eng.memset(onesf1[:], 1.0).then_inc(sem("c_ic2"), 1)

        def emit_epi0(fb):
            C.bump("c_hst", 2)
            if en == "vector":
                eng.wait_ge(sem("c_hsev"), C["c_hsev"])
                eng.tensor_add(hs_row[0:1, 0:512], hs_row[0:1, 0:512],
                               hs_row[0:1, 512:1024]).then_inc(sem("c_hst"), 1)
                eng.wait_ge(sem("c_hst"), C["c_hst"] - 1)
                eng.tensor_scalar_mul(hs_row[0:1, 0:512], hs_row[0:1, 0:512],
                                      -128.0).then_inc(sem("c_hst"), 1)
            for tcl in range(4):
                C.bump("c_ht")
                if en == "tensor":
                    eng.wait_ge(sem("c_hst"), C["c_hst"])
                    eng.wait_ge(sem("c_devs"), C["c_devs"])
                    eng.wait_ge(sem("c_devv"), C["c_devv"])
                    ins = eng.transpose(ps[:, 3584 + tcl * 128: 3584 + tcl * 128 + 1],
                                        hs_row[0:1, tcl * 128:(tcl + 1) * 128],
                                        onesf1[0:1, 0:1])
                    ins.then_inc(sem("c_ht"), 1)
                C.bump("c_htev")
                if en == "scalar":
                    eng.wait_ge(sem("c_ht"), C["c_ht"])
                    eng.copy(negh[:, tcl:tcl + 1],
                             ps[:, 3584 + tcl * 128: 3584 + tcl * 128 + 1]).then_inc(sem("c_htev"), 1)

        def emit_epi_t(fb, tcl):
            tc = fb * 4 + tcl
            n_sx = C.bump("c_sx", 32) // 32
            if en == "gpsimd":
                if n_sx > 1:
                    eng.wait_ge(sem("c_ep3"), C.get("c_ep3", 0))
                eng.dma_start(out=st_t[:],
                              in_=dr["s_s"].ap()[tc]).then_inc(sem("c_sx"), 16)
                eng.dma_start(out=xt_t[:],
                              in_=dr["x"].ap()[tc * 128:(tc + 1) * 128, :]).then_inc(sem("c_sx"), 16)
            C.bump("c_vv")
            C.bump("c_ep1")
            if en == "vector":
                eng.wait_ge(sem("c_htev"), C["c_htev"])
                fsl = facc[:, tcl * 2048:(tcl + 1) * 2048]
                eng.tensor_scalar_add(fsl, fsl, negh[:, tcl:tcl + 1]).then_inc(sem("c_vv"), 1)
                eng.wait_ge(sem("c_vv"), C["c_vv"])
                eng.tensor_mul(fsl, fsl, dscb[:]).then_inc(sem("c_ep1"), 1)
            C.bump("c_ep2")
            if en == "scalar":
                eng.wait_ge(sem("c_ep1"), C["c_ep1"])
                eng.wait_ge(sem("c_out"), C.get("c_out", 0))
                eng.activation(ep_g[:], facc[:, tcl * 2048:(tcl + 1) * 2048],
                               AF.Silu).then_inc(sem("c_ep2"), 1)
            C.bump("c_vv")
            C.bump("c_ep3")
            if en == "vector":
                eng.wait_ge(sem("c_ep2"), C["c_ep2"])
                eng.wait_ge(sem("c_sx"), C["c_sx"])
                eng.tensor_mul(ep_g[:], st_t[:], ep_g[:]).then_inc(sem("c_vv"), 1)
                eng.wait_ge(sem("c_vv"), C["c_vv"])
                eng.tensor_add(ep_g[:], ep_g[:], xt_t[:]).then_inc(sem("c_ep3"), 1)
            C.bump("c_out", 16)
            if en == "gpsimd":
                eng.wait_ge(sem("c_ep3"), C["c_ep3"])
                eng.dma_start(out=dr["out"].ap()[tc * 128:(tc + 1) * 128, :],
                              in_=ep_g[:]).then_inc(sem("c_out"), 16)

        pend_epi = [None]
        for fb in range(C_NFB):
            C.bump("c_xl", 32)
            if en == "sync":
                if fb >= 1:
                    eng.wait_ge(sem("c_gdone"), 2 * fb)
                eng.dma_start(out=xnt_h[:], in_=dr["xnt_h"].ap()[fb]).then_inc(sem("c_xl"), 16)
                eng.dma_start(out=xnt_l[:], in_=dr["xnt_l"].ap()[fb]).then_inc(sem("c_xl"), 16)
            for ih in range(2):
                for il in range(32):
                    ic = ih * 32 + il
                    wbg = il % 2
                    gwk = "c_gw%d" % wbg
                    C.bump(gwk, 32)
                    if en == "sync":
                        if C[gwk] > 32:
                            eng.wait_ge(sem("c_gmm"), C.get("c_gmm", 0) - 1)
                        eng.dma_start(out=gb[:, wbg * 2048:(wbg + 1) * 2048],
                                      in_=dr["g_b"].ap()[ic]).then_inc(sem(gwk), 16)
                        eng.dma_start(out=ub[:, wbg * 2048:(wbg + 1) * 2048],
                                      in_=dr["u_b"].ap()[ic]).then_inc(sem(gwk), 16)
                    gpar = il % 2
                    C.bump("c_gmm")
                    if en == "tensor":
                        eng.wait_ge(sem(gwk), C[gwk])
                        eng.wait_ge(sem("c_xl"), C["c_xl"])
                        if il == 0:
                            eng.wait_ge(sem("c_devs"), C.get("c_devs", 0))
                            eng.wait_ge(sem("c_devv"), C.get("c_devv", 0))
                        if C["c_gmm"] > 2:
                            eng.wait_ge(sem("c_h"), C["c_gmm"] - 2)
                        dstg = ps[:, gpar * 512:(gpar + 1) * 512]
                        dstu = ps[:, 1024 + gpar * 512: 1024 + (gpar + 1) * 512]
                        for hc in range(NHC):
                            wt_g = gb[:, wbg * 2048 + hc * 128: wbg * 2048 + (hc + 1) * 128]
                            wt_u = ub[:, wbg * 2048 + hc * 128: wbg * 2048 + (hc + 1) * 128]
                            rh = xnt_h[:, hc * 512:(hc + 1) * 512]
                            rl = xnt_l[:, hc * 512:(hc + 1) * 512]
                            eng.matmul(dstg, wt_g, rh, start=(hc == 0), stop=False)
                            eng.matmul(dstg, wt_g, rl, start=False, stop=(hc == NHC - 1))
                            eng.matmul(dstu, wt_u, rh, start=(hc == 0), stop=False)
                            ins = eng.matmul(dstu, wt_u, rl, start=False, stop=(hc == NHC - 1))
                        ins.then_inc(sem("c_gmm"), 1)
                    C.bump("c_sg")
                    if en == "scalar":
                        eng.wait_ge(sem("c_gmm"), C["c_gmm"])
                        if fb == 0 and ic == 0:
                            eng.wait_ge(sem("c_ic"), 48)
                        if C["c_sg"] > 2:
                            eng.wait_ge(sem("c_h"), C["c_sg"] - 2)
                        eng.activation(sg_s[:, gpar * 512:(gpar + 1) * 512],
                                       ps[:, gpar * 512:(gpar + 1) * 512],
                                       AF.Silu, scale=gsc[:, ic:ic + 1]).then_inc(sem("c_sg"), 1)
                    C.bump("c_vv", 2)
                    C.bump("c_h")
                    if en == "vector":
                        eng.wait_ge(sem("c_sg"), C["c_sg"])
                        if fb == 0 and ic == 0:
                            eng.wait_ge(sem("c_ic"), 48)
                        if C.get("c_dmm_ph", 0) > 0 and il == 0:
                            eng.wait_ge(sem("c_dmm"), C["c_dmm_ph"])
                        eng.scalar_tensor_tensor(hf_s[:, gpar * 512:(gpar + 1) * 512],
                                                 ps[:, 1024 + gpar * 512: 1024 + (gpar + 1) * 512],
                                                 usc[:, ic:ic + 1],
                                                 sg_s[:, gpar * 512:(gpar + 1) * 512],
                                                 ALU.mult, ALU.mult).then_inc(sem("c_vv"), 1)
                        eng.wait_ge(sem("c_vv"), C["c_vv"] - 1)
                        dh = h_h[:, il * 512:(il + 1) * 512]
                        dl = h_l[:, il * 512:(il + 1) * 512]
                        eng.tensor_copy(dh, hf_s[:, gpar * 512:(gpar + 1) * 512]).then_inc(sem("c_vv"), 1)
                        eng.wait_ge(sem("c_vv"), C["c_vv"])
                        eng.tensor_sub(dl, hf_s[:, gpar * 512:(gpar + 1) * 512],
                                       dh).then_inc(sem("c_h"), 1)
                    if ih == 0 and pend_epi[0] is not None:
                        if il == 2:
                            emit_epi0(pend_epi[0])
                        elif il in (5, 8, 11, 14):
                            emit_epi_t(pend_epi[0], (il - 5) // 3)
                            if il == 14:
                                pend_epi[0] = None
                    C.bump("c_hs")
                    if en == "tensor":
                        eng.wait_ge(sem("c_h"), C["c_h"])
                        if fb == 0 and ic == 0:
                            eng.wait_ge(sem("c_ic2"), 1)
                        if il == 0:
                            eng.wait_ge(sem("c_hsev"), C.get("c_hsev", 0))
                        dh = h_h[:, il * 512:(il + 1) * 512]
                        dl = h_l[:, il * 512:(il + 1) * 512]
                        eng.matmul(ps[0:1, 2048:2560], onesb[:], dh,
                                   start=(il == 0), stop=False)
                        ins = eng.matmul(ps[0:1, 2048:2560], onesb[:], dl,
                                         start=False, stop=(il == 31))
                        ins.then_inc(sem("c_hs"), 1)
                C.bump("c_hsev")
                if en == "scalar":
                    eng.wait_ge(sem("c_hs"), C["c_hs"])
                    eng.wait_ge(sem("c_hst"), C.get("c_hst", 0))
                    eng.copy(hs_row[0:1, ih * 512:(ih + 1) * 512],
                             ps[0:1, 2048:2560]).then_inc(sem("c_hsev"), 1)
                C.bump("c_gdone")
                if en == "tensor":
                    eng.nop().then_inc(sem("c_gdone"), 1)
                for hhh in range(2):
                    for il in range(32):
                        ic = ih * 32 + il
                        wbd = il % 2
                        dwk = "c_dw%d" % wbd
                        C.bump(dwk, 16)
                        if en == "gpsimd":
                            if C[dwk] > 16:
                                eng.wait_ge(sem("c_dmm"), C.get("c_dmm", 0) - 1)
                            src = dr["d_b"].ap()[ic][:, hhh * 1024:(hhh + 1) * 1024]
                            eng.dma_start(out=db[:, wbd * 1024:(wbd + 1) * 1024],
                                          in_=src).then_inc(sem(dwk), 16)
                        C.bump("c_dmm")
                        if en == "tensor":
                            eng.wait_ge(sem(dwk), C[dwk])
                            eng.wait_ge(sem("c_h"), C["c_h"])
                            if il == 0:
                                eng.wait_ge(sem("c_devs"), C.get("c_devs", 0))
                                eng.wait_ge(sem("c_devv"), C.get("c_devv", 0))
                                eng.wait_ge(sem("c_hsev"), C["c_hsev"])
                                eng.wait_ge(sem("c_htev"), C.get("c_htev", 0))
                            for tcl in range(4):
                                lt_h = h_h[:, il * 512 + tcl * 128: il * 512 + (tcl + 1) * 128]
                                lt_l = h_l[:, il * 512 + tcl * 128: il * 512 + (tcl + 1) * 128]
                                d0 = ps[:, tcl * 1024: tcl * 1024 + 512]
                                d1 = ps[:, tcl * 1024 + 512: (tcl + 1) * 1024]
                                st = il == 0
                                sp = il == 31
                                eng.matmul(d0, lt_h, db[:, wbd * 1024: wbd * 1024 + 512],
                                           start=st, stop=False)
                                eng.matmul(d1, lt_h, db[:, wbd * 1024 + 512:(wbd + 1) * 1024],
                                           start=st, stop=False)
                                eng.matmul(d0, lt_l, db[:, wbd * 1024: wbd * 1024 + 512],
                                           start=False, stop=sp)
                                ins = eng.matmul(d1, lt_l, db[:, wbd * 1024 + 512:(wbd + 1) * 1024],
                                                 start=False, stop=sp)
                            ins.then_inc(sem("c_dmm"), 1)
                    for tcl in range(4):
                        if ih == 0:
                            C.bump("c_devs")
                            if en == "scalar":
                                eng.wait_ge(sem("c_dmm"), C["c_dmm"])
                                if fb > 0 and hhh == 0 and tcl == 0:
                                    eng.wait_ge(sem("c_ep2"), 4 * fb)
                                eng.copy(facc[:, tcl * 2048 + hhh * 1024:
                                              tcl * 2048 + (hhh + 1) * 1024],
                                         ps[:, tcl * 1024:(tcl + 1) * 1024]).then_inc(sem("c_devs"), 1)
                        else:
                            C.bump("c_devv")
                            if en == "vector":
                                eng.wait_ge(sem("c_dmm"), C["c_dmm"])
                                dstf = facc[:, tcl * 2048 + hhh * 1024:
                                            tcl * 2048 + (hhh + 1) * 1024]
                                eng.tensor_add(dstf, ps[:, tcl * 1024:(tcl + 1) * 1024],
                                               dstf).then_inc(sem("c_devv"), 1)
                C["c_dmm_ph"] = C["c_dmm"]
            pend_epi[0] = fb
            if fb == C_NFB - 1:
                emit_epi0(fb)
                for tcl in range(4):
                    emit_epi_t(fb, tcl)
                pend_epi[0] = None
        if en == "gpsimd":
            eng.wait_ge(sem("c_out"), C["c_out"])

    if "p0" in PHASES:
        with contextlib.ExitStack() as es:
            sb = lambda n, s, d: es.enter_context(nc.sbuf_tensor(n, s, d))
            T = {
                "wtmp": sb("p0_wtmp", [128, 4096], F32),
                "whi": sb("p0_whi", [128, 4096], BF16),
                "wlo": sb("p0_wlo", [128, 4096], BF16),
            }
            chunks = []
            for hc in range(NHC):
                src = dr["wq"].ap()[hc * 128:(hc + 1) * 128, :]
                dsth = dr["wq_h"].ap()[:, hc].transpose([1, 0, 2])
                dstl = dr["wq_l"].ap()[:, hc].transpose([1, 0, 2])
                chunks.append(("wq", src, 2048, dsth, dstl))
            for w, cols in (("wk", 512), ("wv", 512), ("wo", 2048)):
                for hc in range(NHC):
                    src = dr[w].ap()[hc * 128:(hc + 1) * 128, :]
                    chunks.append((w, src, cols, dr[w + "_h"].ap()[hc], dr[w + "_l"].ap()[hc]))
            T["chunks"] = chunks
            with nc.Block() as blk:
                mk_emit(blk, lambda en, eng, C: _emit_p0(en, eng, C, T))

    if "a" in PHASES:
        with contextlib.ExitStack() as es:
            sb = lambda n, s, d: es.enter_context(nc.sbuf_tensor(n, s, d))
            pt = lambda n, s: es.enter_context(nc.psum_tensor(n, s, F32))
            T = {
                "xt": sb("a_xt", [128, 4096], F32),
                "xn": sb("a_xn", [128, 2048], F32),
                "xnt_h": sb("a_xnth", [128, 8192], BF16),
                "xnt_l": sb("a_xntl", [128, 8192], BF16),
                "wbh": sb("a_wbh", [128, 8192], BF16),
                "wbl": sb("a_wbl", [128, 8192], BF16),
                "ck": sb("a_ck", [128, 2048], F32),
                "sk": sb("a_sk", [128, 2048], F32),
                "stats": sb("a_stats", [128, 8], F32),
                "kf": sb("a_kf", [128, 512], F32),
                "kc": sb("a_kc", [128, 512], F32),
                "ks": sb("a_ks", [128, 512], F32),
                "ksw": sb("a_ksw", [128, 512], F32),
                "identf": sb("a_identf", [128, 128], F32),
                "tbuf": sb("a_tbuf", [128, 2048], F32),
                "ptr": pt("a_ptr", [128, 2048]),
                "pkv": pt("a_pkv", [128, 1024]),
            }
            with nc.Block() as blk:
                mk_emit(blk, lambda en, eng, C: _emit_a(en, eng, C, T))

    if "b1" in PHASES:
        with contextlib.ExitStack() as es:
            sb = lambda n, s, d: es.enter_context(nc.sbuf_tensor(n, s, d))
            pt = lambda n, s: es.enter_context(nc.psum_tensor(n, s, F32))
            T = {
                "xnt_h": sb("b_xnth", [128, 8192], BF16),
                "xnt_l": sb("b_xntl", [128, 8192], BF16),
                "cq": sb("b_cq", [128, 512], F32),
                "sq": sb("b_sq", [128, 512], F32),
                "wqb_h": sb("b_wqbh", [128, 4096], BF16),
                "wqb_l": sb("b_wqbl", [128, 4096], BF16),
                "qf": sb("b_qf", [128, 512], F32),
                "qc": sb("b_qc", [128, 512], F32),
                "qs": sb("b_qs", [128, 512], F32),
                "qsw": sb("b_qsw", [128, 512], F32),
                "q_h": sb("b_qh", [128, 1024], BF16),
                "q_l": sb("b_ql", [128, 1024], BF16),
                "p_f": sb("b_pf", [128, 2048], F32),
                "p_b": sb("b_pb", [128, 2048], BF16),
                "p_r": sb("b_pr", [128, 2048], BF16),
                "ao_h": sb("b_aoh", [128, HQ * 512], BF16),
                "ao_l": sb("b_aol", [128, HQ * 512], BF16),
                "ao_f": sb("b_aof", [128, 512], F32),
                "rr_b": sb("b_rrb", [128, 512], F32),
                "rs_row": sb("b_rsrow", [1, 512], F32),
                "rr_row": sb("b_rrrow", [1, 512], F32),
                "negc": sb("b_negc", [128, 1], F32),
                "ones_b": sb("b_onesb", [128, 1], BF16),
                "ones_rf": sb("b_onesrf", [1, 128], F32),
                "cmask": sb("b_cmask", [128, 2048], BF16),
                "identb": sb("b_identb", [128, 128], BF16),
                "pARE": pt("b_pARE", [128, 4096]),
                "wob_h": sb("b_wobh", [128, 2048], BF16),
                "wob_l": sb("b_wobl", [128, 2048], BF16),
                "s_half": sb("b_shalf", [128, 2048], F32),
            }
            with nc.Block() as blk:
                mk_emit(blk, lambda en, eng, C: _emit_b1(en, eng, C, T))

    if "devkv" in PHASES:
        dkh = nc.dram_tensor("dev_kh", [128, NKV * S], BF16, kind="ExternalOutput")
        dkl = nc.dram_tensor("dev_kl", [128, NKV * S], BF16, kind="ExternalOutput")
        dvf = nc.dram_tensor("dev_vf", [128, NTC * 512], F32, kind="ExternalOutput")
        with nc.Block() as blk:
            @blk.gpsimd
            def _(eng):
                eng.dma_start(out=dkh.ap()[:, :], in_=g_k_h[:]).then_inc(sem("devkv"), 16)
                eng.dma_start(out=dkl.ap()[:, :], in_=g_k_l[:]).then_inc(sem("devkv"), 16)
                eng.dma_start(out=dvf.ap()[:, :], in_=g_v_f[:]).then_inc(sem("devkv"), 16)
                eng.wait_ge(sem("devkv"), 48)

    pstack.close()

    if "c" in PHASES:
        with contextlib.ExitStack() as es:
            sb = lambda n, s, d: es.enter_context(nc.sbuf_tensor(n, s, d))
            pt = lambda n, s: es.enter_context(nc.psum_tensor(n, s, F32))
            T = {
                "xnt_h": sb("c_xnth", [128, 8192], BF16),
                "xnt_l": sb("c_xntl", [128, 8192], BF16),
                "gb": sb("c_gb", [128, 4096], BF16),
                "ub": sb("c_ub", [128, 4096], BF16),
                "db": sb("c_db", [128, 2048], BF16),
                "h_h": sb("c_hh", [128, 32 * 512], BF16),
                "h_l": sb("c_hl", [128, 32 * 512], BF16),
                "sg_s": sb("c_sgs", [128, 1024], F32),
                "hf_s": sb("c_hfs", [128, 1024], F32),
                "facc": sb("c_facc", [128, 4 * 2048], F32),
                "gsc": sb("c_gsc", [128, NIC], F32),
                "usc": sb("c_usc", [128, NIC], F32),
                "dscb": sb("c_dscb", [128, E], F32),
                "hs_row": sb("c_hsrow", [1, 1024], F32),
                "negh": sb("c_negh", [128, 4], F32),
                "onesb": sb("c_onesb", [128, 1], BF16),
                "onesf1": sb("c_onesf1", [1, 1], F32),
                "st_t": sb("c_stt", [128, 2048], F32),
                "xt_t": sb("c_xtt", [128, 2048], F32),
                "ep_g": sb("c_epg", [128, 2048], F32),
                "ps": pt("c_ps", [128, 4096]),
            }
            with nc.Block() as blk:
                mk_emit(blk, lambda en, eng, C: _emit_c(en, eng, C, T))

    return nc


def kernel(**inputs):
    in_maps = _host_prep(**inputs)
    nc = build_nc()
    res = run_bass_kernel_spmd(nc, in_maps, core_ids=list(range(8)))
    return np.stack([np.asarray(res.results[c]["out"]) for c in range(8)], axis=0)


# version 13
# speedup vs baseline: 1.1615x; 1.1063x over previous
import contextlib
import numpy as np
import ml_dtypes
import concourse.bass as bass
import concourse.mybir as mybir
from concourse.bass_utils import run_bass_kernel_spmd

F32 = mybir.dt.float32
BF16 = mybir.dt.bfloat16
I32 = mybir.dt.int32
AF = mybir.ActivationFunctionType
ALU = mybir.AluOpType
AX = mybir.AxisListType.X
bfdt = ml_dtypes.bfloat16

S = 2048
E = 2048
HQ = 16
NKV = 4
D = 128
II = 8192
NTC = 16
NQB = 4
NHC = 16
NIC = 64
CSHIFT = 8.0
EPS = 1e-5
RSQD = float(1.0 / np.sqrt(128.0))

ENGINES = ("sync", "tensor", "scalar", "vector", "gpsimd")
PHASES = ("p0", "a", "b1", "b2", "c")
C_NFB = NQB
DEV_OUT = ()



_TAB_CACHE = {}


def _rope_tables_f32():
    if "cs" not in _TAB_CACHE:
        import subprocess
        import sys
        import tempfile
        code = (
            "import jax\n"
            "jax.config.update('jax_platforms', 'cpu')\n"
            "jax.config.update('jax_enable_x64', True)\n"
            "import numpy as np, jax.numpy as jnp, sys\n"
            "inv = 1.0 / 10000.0 ** (jnp.arange(0, 128, 2, dtype=jnp.float32) / 128)\n"
            "t = jnp.arange(2048, dtype=jnp.float32)\n"
            "fr = jnp.outer(t, inv)\n"
            "np.save(sys.argv[1], np.stack([np.asarray(jnp.cos(fr)), np.asarray(jnp.sin(fr))]))\n"
        )
        with tempfile.TemporaryDirectory() as td:
            p = td + "/t.npy"
            subprocess.run([sys.executable, "-c", code, p], check=True,
                           capture_output=True)
            cs = np.load(p)
        _TAB_CACHE["cs"] = (cs[0].astype(np.float32), cs[1].astype(np.float32))
    return _TAB_CACHE["cs"]


def _host_prep(L0_attn_x, L0_attn_Wq, L0_attn_Wk, L0_attn_Wv, L0_attn_Wo,
               L0_ffn_gate_w, L0_ffn_gate_scale, L0_ffn_up_w, L0_ffn_up_scale,
               L0_ffn_down_w, L0_ffn_down_scale):
    cm = np.ascontiguousarray
    x = np.asarray(L0_attn_x, dtype=np.float32)
    shared = {
        "wq": cm(np.asarray(L0_attn_Wq, np.float32)),
        "wk": cm(np.asarray(L0_attn_Wk, np.float32)),
        "wv": cm(np.asarray(L0_attn_Wv, np.float32)),
        "wo": cm(np.asarray(L0_attn_Wo, np.float32)),
        "gwt": cm(np.asarray(L0_ffn_gate_w, np.int32).T),
        "uwt": cm(np.asarray(L0_ffn_up_w, np.int32).T),
        "dwt": cm(np.asarray(L0_ffn_down_w, np.int32).T),
        "gsc": cm(np.asarray(L0_ffn_gate_scale, np.float32).reshape(NIC, 128).T),
        "usc": cm(np.asarray(L0_ffn_up_scale, np.float32).reshape(NIC, 128).T),
        "dscb": cm(np.broadcast_to(
            np.asarray(L0_ffn_down_scale, np.float32)[None, :], (128, E))),
    }
    cos32, sin32 = _rope_tables_f32()
    cos = np.concatenate([cos32, cos32], axis=1).astype(np.float64)
    sinn = np.concatenate([-sin32, sin32], axis=1).astype(np.float64)
    for nm, tab in (("cosk", cos.T), ("sink", sinn.T),
                    ("cosq", cos.T * RSQD), ("sinq", sinn.T * RSQD)):
        hi = tab.astype(np.float32)
        shared[nm] = cm(hi)
        shared[nm + "_lo"] = cm((tab - hi.astype(np.float64)).astype(np.float32))
    cmask = np.zeros((128, 4 * 512), dtype=np.float32)
    tk = np.arange(128)[:, None]
    tq = np.arange(512)[None, :]
    for p in range(4):
        cmask[:, p * 512:(p + 1) * 512] = np.where(tq >= p * 128 + tk, 0.0, -1e30)
    shared["cmask"] = cm(cmask.astype(bfdt))
    shared["identb"] = cm(np.eye(128, dtype=bfdt))
    shared["identf"] = cm(np.eye(128, dtype=np.float32))
    return [dict(shared, x=cm(x[c])) for c in range(8)]


class Ctr(dict):
    def bump(self, k, n=1):
        self[k] = self.get(k, 0) + n
        return self[k]


def build_nc():
    nc = bass.Bass("TRN2", target_bir_lowering=False)

    dr = {}
    dr["x"] = nc.dram_tensor("x", [S, E], F32, kind="ExternalInput")
    dr["wq"] = nc.dram_tensor("wq", [E, E], F32, kind="ExternalInput")
    dr["wk"] = nc.dram_tensor("wk", [E, 512], F32, kind="ExternalInput")
    dr["wv"] = nc.dram_tensor("wv", [E, 512], F32, kind="ExternalInput")
    dr["wo"] = nc.dram_tensor("wo", [E, E], F32, kind="ExternalInput")
    dr["gwt"] = nc.dram_tensor("gwt", [E, II], I32, kind="ExternalInput")
    dr["uwt"] = nc.dram_tensor("uwt", [E, II], I32, kind="ExternalInput")
    dr["dwt"] = nc.dram_tensor("dwt", [II, E], I32, kind="ExternalInput")
    dr["gsc"] = nc.dram_tensor("gsc", [128, NIC], F32, kind="ExternalInput")
    dr["usc"] = nc.dram_tensor("usc", [128, NIC], F32, kind="ExternalInput")
    dr["dscb"] = nc.dram_tensor("dscb", [128, E], F32, kind="ExternalInput")
    for t in ("cosk", "sink", "cosq", "sinq"):
        dr[t] = nc.dram_tensor(t, [128, S], F32, kind="ExternalInput")
        dr[t + "_lo"] = nc.dram_tensor(t + "_lo", [128, S], F32, kind="ExternalInput")
    dr["cmask"] = nc.dram_tensor("cmask", [128, 4 * 512], BF16, kind="ExternalInput")
    dr["identb"] = nc.dram_tensor("identb", [128, 128], BF16, kind="ExternalInput")
    dr["identf"] = nc.dram_tensor("identf", [128, 128], F32, kind="ExternalInput")
    dr["out"] = nc.dram_tensor("out", [S, E], F32, kind="ExternalOutput")

    def scratch(name, shape, dt):
        kind = "ExternalOutput" if name in DEV_OUT else "Internal"
        return nc.dram_tensor(name, shape, dt, kind=kind)
    dr["wq_h"] = scratch("wq_h", [HQ, NHC, 128, 128], BF16)
    dr["wq_l"] = scratch("wq_l", [HQ, NHC, 128, 128], BF16)
    dr["wk_h"] = scratch("wk_h", [NHC, 128, 512], BF16)
    dr["wk_l"] = scratch("wk_l", [NHC, 128, 512], BF16)
    dr["wv_h"] = scratch("wv_h", [NHC, 128, 512], BF16)
    dr["wv_l"] = scratch("wv_l", [NHC, 128, 512], BF16)
    dr["wo_h"] = scratch("wo_h", [NHC, 128, E], BF16)
    dr["wo_l"] = scratch("wo_l", [NHC, 128, E], BF16)
    dr["xnt_h"] = scratch("xnt_h", [NQB, 128, 8192], BF16)
    dr["xnt_l"] = scratch("xnt_l", [NQB, 128, 8192], BF16)
    dr["s_s"] = scratch("s_s", [NTC, 128, E], F32)
    dr["g_b"] = scratch("g_b", [NIC, 128, 2048], BF16)
    dr["u_b"] = scratch("u_b", [NIC, 128, 2048], BF16)
    dr["d_b"] = scratch("d_b", [NIC, 128, 2048], BF16)

    sems = {}

    def sem(name):
        if name not in sems:
            sems[name] = nc.alloc_semaphore(name)
        return sems[name]

    ctrs = {}

    def emit_conv_jobs(en, eng, C, jobs, cbuf=None):
        for kind, ic in jobs:
            C.bump("cv_o", 16)
            if en == "gpsimd":
                if kind == "d":
                    srca = dr["dwt"].ap()[ic * 128:(ic + 1) * 128, :]
                    dsts = dr["d_b"].ap()[ic]
                else:
                    w = dr["gwt"] if kind == "g" else dr["uwt"]
                    srca = w.ap()[:, ic * 128:(ic + 1) * 128].rearrange(
                        "(hh p) j -> p hh j", p=128)
                    dsts = dr[{"g": "g_b", "u": "u_b"}[kind]].ap()[ic].rearrange(
                        "p (hh j) -> p hh j", j=128)
                eng.dma_start(out=dsts, in_=srca).then_inc(sem("cv_o"), 16)

    def mk_emit(block, fn):
        for en in ENGINES:
            def closure(eng, en=en):
                fn(en, eng, ctrs.setdefault(en, Ctr()))
            getattr(block, en)(closure)

    pstack = contextlib.ExitStack()
    g_k_h = pstack.enter_context(nc.sbuf_tensor("g_k_h", [128, NKV * S], BF16))
    g_k_l = pstack.enter_context(nc.sbuf_tensor("g_k_l", [128, NKV * S], BF16))
    g_v_f = pstack.enter_context(nc.sbuf_tensor("g_v_f", [128, NTC * 512], F32))

    def _emit_p0(en, eng, C, T):
        wtmp, whi, wlo, chunks = T["wtmp"], T["whi"], T["wlo"], T["chunks"]
        for i, (nm, src, cols, dsth, dstl) in enumerate(chunks):
            b = i % 2
            ldk = "p0_ld%d" % b
            stk = "p0_st%d" % b
            C.bump(ldk, 16)
            if en == "sync":
                if i >= 2:
                    eng.wait_ge(sem(stk), C[stk])
                eng.dma_start(out=wtmp[:, b * 2048: b * 2048 + cols], in_=src).then_inc(sem(ldk), 16)
            C.bump("p0_hi")
            if en == "scalar":
                eng.wait_ge(sem(ldk), C[ldk])
                if i >= 2:
                    eng.wait_ge(sem(stk), C[stk])
                eng.copy(whi[:, b * 2048: b * 2048 + cols],
                         wtmp[:, b * 2048: b * 2048 + cols]).then_inc(sem("p0_hi"), 1)
            C.bump("p0_lo")
            if en == "vector":
                eng.wait_ge(sem("p0_hi"), C["p0_hi"])
                eng.tensor_sub(wlo[:, b * 2048: b * 2048 + cols],
                               wtmp[:, b * 2048: b * 2048 + cols],
                               whi[:, b * 2048: b * 2048 + cols]).then_inc(sem("p0_lo"), 1)
            C.bump(stk, 32)
            if en == "gpsimd":
                eng.wait_ge(sem("p0_hi"), C["p0_hi"])
                srch = whi[:, b * 2048: b * 2048 + cols]
                srcl = wlo[:, b * 2048: b * 2048 + cols]
                if nm == "wq":
                    srch = srch.rearrange("p (a j) -> p a j", j=128)
                    srcl = srcl.rearrange("p (a j) -> p a j", j=128)
                eng.dma_start(out=dsth, in_=srch).then_inc(sem(stk), 16)
                eng.wait_ge(sem("p0_lo"), C["p0_lo"])
                eng.dma_start(out=dstl, in_=srcl).then_inc(sem(stk), 16)
        if en == "gpsimd":
            eng.wait_ge(sem("p0_st0"), C["p0_st0"])
            eng.wait_ge(sem("p0_st1"), C["p0_st1"])

    def _emit_a(en, eng, C, T):
        xt, xn = T["xt"], T["xn"]
        xnt_h, xnt_l = T["xnt_h"], T["xnt_l"]
        wbh, wbl = T["wbh"], T["wbl"]
        ck, sk, stats = T["ck"], T["sk"], T["stats"]
        kf, kc, ks, ksw = T["kf"], T["kc"], T["ks"], T["ksw"]
        ptr, pkv = T["ptr"], T["pkv"]
        identf = T["identf"]

        tbuf = T["tbuf"]
        C.bump("a_id", 16)
        C.bump("a_if", 16 * 2)
        C.bump("a_tl0", 16)
        if en == "sync":
            eng.dma_start(out=identf[:], in_=dr["identf"].ap()[:, :]).then_inc(sem("a_id"), 16)
            eng.dma_start(out=ck[:], in_=dr["cosk"].ap()[:, :]).then_inc(sem("a_if"), 16)
            eng.dma_start(out=sk[:], in_=dr["sink"].ap()[:, :]).then_inc(sem("a_if"), 16)
            eng.dma_start(out=tbuf[:], in_=dr["cosk_lo"].ap()[:, :]).then_inc(sem("a_tl0"), 16)
        C.bump("a_tl1")
        if en == "vector":
            eng.wait_ge(sem("a_if"), 32)
            eng.wait_ge(sem("a_tl0"), C["a_tl0"])
            eng.tensor_add(ck[:], ck[:], tbuf[:]).then_inc(sem("a_tl1"), 1)
        C.bump("a_tl0", 16)
        if en == "sync":
            eng.wait_ge(sem("a_tl1"), C["a_tl1"])
            eng.dma_start(out=tbuf[:], in_=dr["sink_lo"].ap()[:, :]).then_inc(sem("a_tl0"), 16)
        C.bump("a_tl1")
        if en == "vector":
            eng.wait_ge(sem("a_tl0"), C["a_tl0"])
            eng.tensor_add(sk[:], sk[:], tbuf[:]).then_inc(sem("a_tl1"), 1)
        C.bump("a_eps")
        if en == "gpsimd":
            eng.memset(stats[:, 7:8], EPS).then_inc(sem("a_eps"), 1)

        for qb in range(NQB):
            for j in range(4):
                tc = qb * 4 + j
                b = tc % 2
                axk = "a_x%d" % b
                C.bump(axk, 16)
                if en == "sync":
                    if tc >= 2:
                        eng.wait_ge(sem("a_sq"), tc - 1)
                        eng.wait_ge(sem("a_xn"), 2 * tc - 2)
                    eng.dma_start(out=xt[:, b * 2048:(b + 1) * 2048],
                                  in_=dr["x"].ap()[tc * 128:(tc + 1) * 128, :]).then_inc(sem(axk), 16)
                C.bump("a_sq")
                if en == "scalar":
                    eng.wait_ge(sem(axk), C[axk])
                    eng.wait_ge(sem("a_tr"), 2 * tc)
                    eng.activation(xn[:], xt[:, b * 2048:(b + 1) * 2048], AF.Square,
                                   accum_out=stats[:, 3:4]).then_inc(sem("a_sq"), 1)
                C.bump("a_vr", 4)
                if en == "vector":
                    eng.wait_ge(sem(axk), C[axk])
                    eng.reduce_sum(stats[:, 0:1], xt[:, b * 2048:(b + 1) * 2048], AX).then_inc(sem("a_vr"), 1)
                    eng.wait_ge(sem("a_vr"), C["a_vr"] - 3)
                    eng.tensor_scalar_mul(stats[:, 1:2], stats[:, 0:1], 1.0 / E).then_inc(sem("a_vr"), 1)
                    eng.wait_ge(sem("a_vr"), C["a_vr"] - 2)
                    eng.tensor_mul(stats[:, 2:3], stats[:, 1:2], stats[:, 1:2]).then_inc(sem("a_vr"), 1)
                    eng.wait_ge(sem("a_sq"), C["a_sq"])
                    if tc == 0:
                        eng.wait_ge(sem("a_eps"), 1)
                    eng.tensor_scalar(stats[:, 4:5], stats[:, 3:4], 1.0 / E, None, ALU.mult).then_inc(sem("a_vr"), 1)
                    eng.wait_ge(sem("a_vr"), C["a_vr"])
                C.bump("a_vs")
                if en == "vector":
                    eng.tensor_sub(stats[:, 4:5], stats[:, 4:5], stats[:, 2:3]).then_inc(sem("a_vs"), 1)
                C.bump("a_sd")
                if en == "scalar":
                    eng.wait_ge(sem("a_vs"), C["a_vs"])
                    eng.activation(stats[:, 5:6], stats[:, 4:5], AF.Sqrt,
                                   bias=stats[:, 7:8]).then_inc(sem("a_sd"), 1)
                C.bump("a_xn")
                if en == "vector":
                    eng.wait_ge(sem("a_sd"), C["a_sd"])
                    eng.reciprocal(stats[:, 6:7], stats[:, 5:6]).then_inc(sem("a_xn"), 1)
                    eng.wait_ge(sem("a_xn"), C["a_xn"])
                C.bump("a_xn")
                if en == "vector":
                    eng.wait_ge(sem("a_tr"), 2 * tc)
                    eng.tensor_scalar(xn[:], xt[:, b * 2048:(b + 1) * 2048],
                                      stats[:, 1:2], stats[:, 6:7],
                                      ALU.subtract, ALU.mult).then_inc(sem("a_xn"), 1)
                for g in range(2):
                    gidx = C.bump("a_tr")
                    if en == "tensor":
                        if tc == 0 and g == 0:
                            eng.wait_ge(sem("a_id"), 16)
                        eng.wait_ge(sem("a_xn"), C["a_xn"])
                        if gidx > 2:
                            eng.wait_ge(sem("a_evh"), gidx - 2)
                            eng.wait_ge(sem("a_evl"), gidx - 2)
                        if j == 0 and g == 0 and qb > 0:
                            eng.wait_ge(sem("a_vev"), C["a_vev"])
                        for hl in range(8):
                            hc = g * 8 + hl
                            ins = eng.transpose(
                                ptr[:, g * 1024 + hl * 128: g * 1024 + (hl + 1) * 128],
                                xn[:, hc * 128:(hc + 1) * 128], identf[:])
                        ins.then_inc(sem("a_tr"), 1)
                    dsth = xnt_h[:].rearrange("p (h c) -> p h c", c=512)[
                        :, g * 8:(g + 1) * 8, j * 128:(j + 1) * 128]
                    dstl = xnt_l[:].rearrange("p (h c) -> p h c", c=512)[
                        :, g * 8:(g + 1) * 8, j * 128:(j + 1) * 128]
                    srcp = ptr[:, g * 1024:(g + 1) * 1024].rearrange("p (h c) -> p h c", c=128)
                    C.bump("a_evh")
                    if en == "scalar":
                        eng.wait_ge(sem("a_tr"), C["a_tr"])
                        eng.copy(dsth, srcp).then_inc(sem("a_evh"), 1)
                    C.bump("a_evl")
                    if en == "vector":
                        eng.wait_ge(sem("a_evh"), C["a_evh"])
                        eng.tensor_sub(dstl, srcp, dsth).then_inc(sem("a_evl"), 1)

            C.bump("a_wk", 32)
            if en == "sync":
                if qb > 0:
                    eng.wait_ge(sem("a_vm"), C["a_vm"])
                eng.dma_start(out=wbh[:].rearrange("p (h j) -> p h j", j=512),
                              in_=dr["wk_h"].ap().transpose([1, 0, 2])).then_inc(sem("a_wk"), 16)
                eng.dma_start(out=wbl[:].rearrange("p (h j) -> p h j", j=512),
                              in_=dr["wk_l"].ap().transpose([1, 0, 2])).then_inc(sem("a_wk"), 16)
            for kvh in range(NKV):
                pb = kvh % 2
                C.bump("a_km")
                if en == "tensor":
                    eng.wait_ge(sem("a_evh"), C["a_evh"])
                    eng.wait_ge(sem("a_evl"), C["a_evl"])
                    eng.wait_ge(sem("a_wk"), C["a_wk"])
                    if C["a_km"] > 2:
                        eng.wait_ge(sem("a_kev"), C["a_km"] - 2)
                    first = True
                    for hc in range(NHC):
                        wt_h = wbh[:, hc * 512 + kvh * 128: hc * 512 + (kvh + 1) * 128]
                        wt_l = wbl[:, hc * 512 + kvh * 128: hc * 512 + (kvh + 1) * 128]
                        rh = xnt_h[:, hc * 512:(hc + 1) * 512]
                        rl = xnt_l[:, hc * 512:(hc + 1) * 512]
                        dst = pkv[:, pb * 512:(pb + 1) * 512]
                        eng.matmul(dst, wt_h, rh, start=first, stop=False)
                        first = False
                        eng.matmul(dst, wt_h, rl, start=False, stop=False)
                        ins = eng.matmul(dst, wt_l, rh, start=False, stop=(hc == NHC - 1))
                    ins.then_inc(sem("a_km"), 1)
                C.bump("a_kev")
                if en == "scalar":
                    eng.wait_ge(sem("a_km"), C["a_km"])
                    if C["a_kev"] > 1:
                        eng.wait_ge(sem("a_krs"), C["a_kev"] - 1)
                    eng.copy(kf[:], pkv[:, pb * 512:(pb + 1) * 512])
                    eng.copy(ksw[0:64, :], pkv[64:128, pb * 512:(pb + 1) * 512])
                    eng.copy(ksw[64:128, :], pkv[0:64, pb * 512:(pb + 1) * 512]).then_inc(sem("a_kev"), 1)
                C.bump("a_vv", 3)
                C.bump("a_krs")
                if en == "vector":
                    eng.wait_ge(sem("a_kev"), C["a_kev"])
                    if qb == 0 and kvh == 0:
                        eng.wait_ge(sem("a_tl1"), 2)
                    cs = ck[:, qb * 512:(qb + 1) * 512]
                    ss = sk[:, qb * 512:(qb + 1) * 512]
                    eng.tensor_mul(kc[:], kf[:], cs)
                    eng.tensor_mul(ks[:], ksw[:], ss).then_inc(sem("a_vv"), 1)
                    eng.wait_ge(sem("a_vv"), C["a_vv"] - 2)
                    eng.tensor_add(kc[:], kc[:], ks[:]).then_inc(sem("a_vv"), 1)
                    eng.wait_ge(sem("a_vv"), C["a_vv"] - 1)
                    dh = g_k_h[:, kvh * S + qb * 512: kvh * S + (qb + 1) * 512]
                    dl = g_k_l[:, kvh * S + qb * 512: kvh * S + (qb + 1) * 512]
                    eng.tensor_copy(dh, kc[:]).then_inc(sem("a_vv"), 1)
                    eng.wait_ge(sem("a_vv"), C["a_vv"])
                    eng.tensor_sub(dl, kc[:], dh).then_inc(sem("a_krs"), 1)

            C.bump("a_wv", 32)
            if en == "sync":
                eng.wait_ge(sem("a_km"), C["a_km"])
                eng.dma_start(out=wbh[:].rearrange("p (h j) -> p h j", j=512),
                              in_=dr["wv_h"].ap().transpose([1, 0, 2])).then_inc(sem("a_wv"), 16)
                eng.dma_start(out=wbl[:].rearrange("p (h j) -> p h j", j=512),
                              in_=dr["wv_l"].ap().transpose([1, 0, 2])).then_inc(sem("a_wv"), 16)
            for j in range(4):
                tc = qb * 4 + j
                pb = j % 2
                C.bump("a_vm")
                if en == "tensor":
                    eng.wait_ge(sem("a_wv"), C["a_wv"])
                    if C["a_vm"] > 2:
                        eng.wait_ge(sem("a_vev"), C["a_vm"] - 2)
                    first = True
                    for hc in range(NHC):
                        lt_h = xnt_h[:, hc * 512 + j * 128: hc * 512 + (j + 1) * 128]
                        lt_l = xnt_l[:, hc * 512 + j * 128: hc * 512 + (j + 1) * 128]
                        rh = wbh[:, hc * 512:(hc + 1) * 512]
                        rl = wbl[:, hc * 512:(hc + 1) * 512]
                        dst = ptr[:, pb * 512:(pb + 1) * 512]
                        eng.matmul(dst, lt_h, rh, start=first, stop=False)
                        first = False
                        eng.matmul(dst, lt_h, rl, start=False, stop=False)
                        ins = eng.matmul(dst, lt_l, rh, start=False, stop=(hc == NHC - 1))
                    ins.then_inc(sem("a_vm"), 1)
                C.bump("a_vev")
                if en == "scalar":
                    eng.wait_ge(sem("a_vm"), C["a_vm"])
                    eng.copy(g_v_f[:, tc * 512:(tc + 1) * 512],
                             ptr[:, pb * 512:(pb + 1) * 512]).then_inc(sem("a_vev"), 1)

            C.bump("a_sp", 32)
            if en == "gpsimd":
                eng.wait_ge(sem("a_evh"), C["a_evh"])
                eng.wait_ge(sem("a_evl"), C["a_evl"])
                eng.dma_start(out=dr["xnt_h"].ap()[qb], in_=xnt_h[:]).then_inc(sem("a_sp"), 16)
                eng.dma_start(out=dr["xnt_l"].ap()[qb], in_=xnt_l[:]).then_inc(sem("a_sp"), 16)
            emit_conv_jobs(en, eng, C, [("g", qb * 16 + i) for i in range(16)])
            if qb < NQB - 1:
                if en in ("scalar", "vector"):
                    eng.wait_ge(sem("a_sp"), C["a_sp"])
                    eng.wait_ge(sem("a_vm"), C["a_vm"])
        if en == "gpsimd":
            eng.wait_ge(sem("a_sp"), C["a_sp"])

    def _emit_b1(en, eng, C, T):
        xnt_h, xnt_l = T["xnt_h"], T["xnt_l"]
        cq, sq = T["cq"], T["sq"]
        wqb_h, wqb_l = T["wqb_h"], T["wqb_l"]
        qf, qc, qs, qsw = T["qf"], T["qc"], T["qs"], T["qsw"]
        q_h, q_l = T["q_h"], T["q_l"]
        p_f, p_b, p_r = T["p_f"], T["p_b"], T["p_r"]
        ao_h, ao_l, ao_f = T["ao_h"], T["ao_l"], T["ao_f"]
        rr_b, rs_row, rr_row = T["rr_b"], T["rs_row"], T["rr_row"]
        negc, ones_b, ones_rf = T["negc"], T["ones_b"], T["ones_rf"]
        cmask, identb = T["cmask"], T["identb"]
        pARE = T["pARE"]
        pS = pARE[:, 0:2048]
        psQ = pARE[:, 2048:2560]
        pAO = pARE[:, 2560:3072]
        pRS = pARE[:, 3072:3584]
        wob_h, wob_l, s_half = T["wob_h"], T["wob_l"], T["s_half"]

        C.bump("b_cl", 32)
        if en == "sync":
            eng.dma_start(out=cmask[:], in_=dr["cmask"].ap()[:, :]).then_inc(sem("b_cl"), 16)
            eng.dma_start(out=identb[:], in_=dr["identb"].ap()[:, :]).then_inc(sem("b_cl"), 16)
        C.bump("b_ic")
        if en == "gpsimd":
            eng.memset(negc[:], -CSHIFT)
            eng.memset(ones_b[:], 1.0)
            eng.memset(ones_rf[:], 1.0).then_inc(sem("b_ic"), 1)

        gg = 0
        for qb in range(NQB):
            C.bump("b_ld", 16 * 4)
            if en == "sync":
                if qb >= 1:
                    eng.wait_ge(sem("b_qbdone"), qb)
                eng.dma_start(out=xnt_h[:], in_=dr["xnt_h"].ap()[qb]).then_inc(sem("b_ld"), 16)
                eng.dma_start(out=xnt_l[:], in_=dr["xnt_l"].ap()[qb]).then_inc(sem("b_ld"), 16)
                eng.dma_start(out=cq[:], in_=dr["cosq"].ap()[:, qb * 512:(qb + 1) * 512]).then_inc(sem("b_ld"), 16)
                eng.dma_start(out=sq[:], in_=dr["sinq"].ap()[:, qb * 512:(qb + 1) * 512]).then_inc(sem("b_ld"), 16)
            C.bump("b_tl0", 32)
            if en == "sync":
                if qb > 0:
                    eng.wait_ge(sem("b_qrs"), C["b_qrs"])
                eng.dma_start(out=qc[:], in_=dr["cosq_lo"].ap()[:, qb * 512:(qb + 1) * 512]).then_inc(sem("b_tl0"), 16)
                eng.dma_start(out=qs[:], in_=dr["sinq_lo"].ap()[:, qb * 512:(qb + 1) * 512]).then_inc(sem("b_tl0"), 16)
            C.bump("b_tl1")
            if en == "vector":
                eng.wait_ge(sem("b_ld"), C["b_ld"])
                eng.wait_ge(sem("b_tl0"), C["b_tl0"])
                eng.tensor_add(cq[:], cq[:], qc[:]).then_inc(sem("b_tl1"), 1)
                eng.wait_ge(sem("b_tl1"), C["b_tl1"])
            C.bump("b_tl1")
            if en == "vector":
                eng.tensor_add(sq[:], sq[:], qs[:]).then_inc(sem("b_tl1"), 1)
            L = 4 * qb + 4
            groups = [list(range(L))[i:i + 2] for i in range(0, L, 2)]

            def emit_qproj(qh):
                kvh = qh // 4
                qpar = qh % 2
                wqk = "b_wq%d" % (qh % 2)
                C.bump(wqk, 32)
                if en == "sync":
                    if C[wqk] > 32:
                        eng.wait_ge(sem("b_qm"), C["b_qm"] - 1)
                    eng.dma_start(out=wqb_h[:, (qh % 2) * 2048:(qh % 2 + 1) * 2048
                                              ].rearrange("p (h j) -> p h j", j=128),
                                  in_=dr["wq_h"].ap()[qh].transpose([1, 0, 2])
                                  ).then_inc(sem(wqk), 16)
                    eng.dma_start(out=wqb_l[:, (qh % 2) * 2048:(qh % 2 + 1) * 2048
                                              ].rearrange("p (h j) -> p h j", j=128),
                                  in_=dr["wq_l"].ap()[qh].transpose([1, 0, 2])
                                  ).then_inc(sem(wqk), 16)
                C.bump("b_qm")
                if en == "tensor":
                    eng.wait_ge(sem("b_ld"), C["b_ld"])
                    eng.wait_ge(sem(wqk), C[wqk])
                    if qh == 0:
                        eng.wait_ge(sem("w_ev"), C.get("w_ev", 0))
                    if C["b_qm"] > 1:
                        eng.wait_ge(sem("b_qev"), C["b_qm"] - 1)
                    first = True
                    for hc in range(NHC):
                        off = (qh % 2) * 2048 + hc * 128
                        wt_h = wqb_h[:, off: off + 128]
                        wt_l = wqb_l[:, off: off + 128]
                        rh = xnt_h[:, hc * 512:(hc + 1) * 512]
                        rl = xnt_l[:, hc * 512:(hc + 1) * 512]
                        eng.matmul(psQ, wt_h, rh, start=first, stop=False)
                        first = False
                        eng.matmul(psQ, wt_h, rl, start=False, stop=False)
                        ins = eng.matmul(psQ, wt_l, rh, start=False, stop=(hc == NHC - 1))
                    ins.then_inc(sem("b_qm"), 1)
                C.bump("b_qev")
                if en == "scalar":
                    eng.wait_ge(sem("b_qm"), C["b_qm"])
                    if C["b_qev"] > 1:
                        eng.wait_ge(sem("b_qrs"), C["b_qev"] - 1)
                    eng.copy(qf[:], psQ)
                    eng.copy(qsw[0:64, :], psQ[64:128, :])
                    eng.copy(qsw[64:128, :], psQ[0:64, :]).then_inc(sem("b_qev"), 1)
                C.bump("b_vv", 3)
                C.bump("b_qrs")
                if en == "vector":
                    if qh == 0:
                        eng.wait_ge(sem("b_tl1"), C["b_tl1"])
                    eng.wait_ge(sem("b_qev"), C["b_qev"])
                    eng.tensor_mul(qc[:], qf[:], cq[:])
                    eng.tensor_mul(qs[:], qsw[:], sq[:]).then_inc(sem("b_vv"), 1)
                    eng.wait_ge(sem("b_vv"), C["b_vv"] - 2)
                    eng.tensor_add(qc[:], qc[:], qs[:]).then_inc(sem("b_vv"), 1)
                    eng.wait_ge(sem("b_vv"), C["b_vv"] - 1)
                    qhsl = q_h[:, qpar * 512:(qpar + 1) * 512]
                    qlsl = q_l[:, qpar * 512:(qpar + 1) * 512]
                    eng.tensor_copy(qhsl, qc[:]).then_inc(sem("b_vv"), 1)
                    eng.wait_ge(sem("b_vv"), C["b_vv"])
                    eng.tensor_sub(qlsl, qc[:], qhsl).then_inc(sem("b_qrs"), 1)

            def emit_S(qh, gi, grp, gg):
                kvh = qh // 4
                qpar = qh % 2
                slot = gg % 2
                C.bump("b_S")
                if en == "tensor":
                    if gi == 0:
                        eng.wait_ge(sem("b_qrs"), C["b_qrs"])
                        if qb == 0 and qh == 0:
                            eng.wait_ge(sem("b_ic"), 1)
                            eng.wait_ge(sem("b_cl"), 32)
                    if gg > 2:
                        eng.wait_ge(sem("b_exp"), gg - 2)
                    for ti, tkc in enumerate(grp):
                        dst = pS[:, slot * 1024 + ti * 512: slot * 1024 + (ti + 1) * 512]
                        st = True
                        if tkc >= qb * 4:
                            o = tkc - qb * 4
                            eng.matmul(dst, identb[:], cmask[:, o * 512:(o + 1) * 512],
                                       start=True, stop=False)
                            st = False
                        kh = g_k_h[:, kvh * S + tkc * 128: kvh * S + (tkc + 1) * 128]
                        kl = g_k_l[:, kvh * S + tkc * 128: kvh * S + (tkc + 1) * 128]
                        qhsl = q_h[:, qpar * 512:(qpar + 1) * 512]
                        qlsl = q_l[:, qpar * 512:(qpar + 1) * 512]
                        eng.matmul(dst, kh, qhsl, start=st, stop=False)
                        eng.matmul(dst, kh, qlsl, start=False, stop=False)
                        ins = eng.matmul(dst, kl, qhsl, start=False, stop=True)
                    ins.then_inc(sem("b_S"), 1)
                C.bump("b_exp")
                if en == "scalar":
                    eng.wait_ge(sem("b_S"), C["b_S"])
                    if gg > 2:
                        eng.wait_ge(sem("b_pv"), gg - 2)
                        eng.wait_ge(sem("b_cast"), gg - 2)
                    n = len(grp)
                    eng.activation(p_f[:, slot * 1024: slot * 1024 + n * 512],
                                   pS[:, slot * 1024: slot * 1024 + n * 512],
                                   AF.Exp, bias=negc[:, 0:1]).then_inc(sem("b_exp"), 1)
                C.bump("b_vv")
                C.bump("b_cast")
                if en == "vector":
                    eng.wait_ge(sem("b_exp"), C["b_exp"])
                    n = len(grp)
                    eng.tensor_copy(p_b[:, slot * 1024: slot * 1024 + n * 512],
                                    p_f[:, slot * 1024: slot * 1024 + n * 512]
                                    ).then_inc(sem("b_vv"), 1)
                    eng.wait_ge(sem("b_vv"), C["b_vv"])
                    eng.tensor_sub(p_r[:, slot * 1024: slot * 1024 + n * 512],
                                   p_f[:, slot * 1024: slot * 1024 + n * 512],
                                   p_b[:, slot * 1024: slot * 1024 + n * 512]
                                   ).then_inc(sem("b_cast"), 1)

            def emit_PV(qh, gi, grp, gg, ng):
                kvh = qh // 4
                C.bump("b_pv")
                if en == "tensor":
                    eng.wait_ge(sem("b_exp"), C["b_exp_at%d" % gg])
                    if gi == 0:
                        eng.wait_ge(sem("b_aoev"), C.get("b_aoev", 0))
                    slot = gg % 2
                    for ti, tkc in enumerate(grp):
                        vt = g_v_f[:, tkc * 512 + kvh * 128: tkc * 512 + (kvh + 1) * 128]
                        eng.matmul(pAO[:, 0:512], vt,
                                   p_f[:, slot * 1024 + ti * 512: slot * 1024 + (ti + 1) * 512],
                                   start=(gi == 0 and ti == 0),
                                   stop=(gi == ng - 1 and ti == len(grp) - 1))
                    eng.wait_ge(sem("b_cast"), C["b_cast_at%d" % gg])
                    if gi == 0:
                        eng.wait_ge(sem("b_rrev"), C.get("b_rrev", 0))
                    for ti, tkc in enumerate(grp):
                        eng.matmul(pRS[0:1, 0:512], ones_b[:],
                                   p_b[:, slot * 1024 + ti * 512: slot * 1024 + (ti + 1) * 512],
                                   start=(gi == 0 and ti == 0), stop=False)
                        ins = eng.matmul(pRS[0:1, 0:512], ones_b[:],
                                         p_r[:, slot * 1024 + ti * 512: slot * 1024 + (ti + 1) * 512],
                                         start=False,
                                         stop=(gi == ng - 1 and ti == len(grp) - 1))
                    ins.then_inc(sem("b_pv"), 1)

            emit_qproj(0)
            for qh in range(HQ):
                ng = len(groups)
                pend = None
                for gi, grp in enumerate(groups):
                    gg = C.bump("b_gg")
                    C["b_exp_at%d" % gg] = 0
                    emit_S(qh, gi, grp, gg)
                    C["b_exp_at%d" % gg] = C["b_exp"]
                    C["b_cast_at%d" % gg] = C["b_cast"]
                    if pend is not None:
                        emit_PV(qh, pend[0], pend[1], pend[2], ng)
                    pend = (gi, grp, gg)
                if qh + 1 < HQ:
                    emit_qproj(qh + 1)
                emit_PV(qh, pend[0], pend[1], pend[2], ng)
                C.bump("b_rsev")
                if en == "scalar":
                    eng.wait_ge(sem("b_pv"), C["b_pv"])
                    eng.copy(rs_row[:], pRS[0:1, 0:512]).then_inc(sem("b_rsev"), 1)
                C.bump("b_rcp")
                if en == "vector":
                    eng.wait_ge(sem("b_rsev"), C["b_rsev"])
                    eng.reciprocal(rr_row[:], rs_row[:]).then_inc(sem("b_rcp"), 1)
                C.bump("b_bc")
                if en == "tensor":
                    eng.wait_ge(sem("b_rcp"), C["b_rcp"])
                    ins = eng.matmul(pRS[:, 0:512], ones_rf[0:1, :], rr_row[:],
                                     start=True, stop=True)
                    ins.then_inc(sem("b_bc"), 1)
                C.bump("b_rrev")
                if en == "scalar":
                    eng.wait_ge(sem("b_bc"), C["b_bc"])
                    eng.copy(rr_b[:], pRS[:, 0:512]).then_inc(sem("b_rrev"), 1)
                C.bump("b_aoev")
                if en == "scalar":
                    eng.wait_ge(sem("b_pv"), C["b_pv"])
                    if C["b_aoev"] > 1:
                        eng.wait_ge(sem("b_ao"), C["b_aoev"] - 1)
                    eng.copy(ao_f[:], pAO[:, 0:512]).then_inc(sem("b_aoev"), 1)
                C.bump("b_vv", 2)
                C.bump("b_ao")
                if en == "vector":
                    if qh == 0 and qb > 0:
                        eng.wait_ge(sem("w_mm"), C.get("w_mm", 0))
                    eng.wait_ge(sem("b_rrev"), C["b_rrev"])
                    eng.wait_ge(sem("b_aoev"), C["b_aoev"])
                    eng.tensor_mul(ao_f[:], ao_f[:], rr_b[:]).then_inc(sem("b_vv"), 1)
                    eng.wait_ge(sem("b_vv"), C["b_vv"] - 1)
                    dh = ao_h[:, qh * 512:(qh + 1) * 512]
                    dl = ao_l[:, qh * 512:(qh + 1) * 512]
                    eng.tensor_copy(dh, ao_f[:]).then_inc(sem("b_vv"), 1)
                    eng.wait_ge(sem("b_vv"), C["b_vv"])
                    eng.tensor_sub(dl, ao_f[:], dh).then_inc(sem("b_ao"), 1)
            C.bump("b_qbdone")
            if en == "gpsimd":
                eng.wait_ge(sem("b_qm"), C["b_qm"])
                eng.sem_inc(sem("b_qbdone"), 1)
            for eh in range(2):
                for dc in range(NHC):
                    wlk = "w_wl%d" % (dc % 2)
                    C.bump(wlk, 32)
                    if en == "sync":
                        if C[wlk] > 32:
                            eng.wait_ge(sem("w_mm"), C.get("w_mm", 0) - 1)
                        eng.dma_start(out=wob_h[:, (dc % 2) * 1024:(dc % 2 + 1) * 1024],
                                      in_=dr["wo_h"].ap()[dc][:, eh * 1024:(eh + 1) * 1024]
                                      ).then_inc(sem(wlk), 16)
                        eng.dma_start(out=wob_l[:, (dc % 2) * 1024:(dc % 2 + 1) * 1024],
                                      in_=dr["wo_l"].ap()[dc][:, eh * 1024:(eh + 1) * 1024]
                                      ).then_inc(sem(wlk), 16)
                    C.bump("w_mm")
                    if en == "tensor":
                        eng.wait_ge(sem(wlk), C[wlk])
                        if eh == 0 and dc == 0:
                            eng.wait_ge(sem("b_ao"), C["b_ao"])
                        if dc == 0:
                            eng.wait_ge(sem("w_ev"), C.get("w_ev", 0))
                        wb = dc % 2
                        for tqc in range(4):
                            ah = ao_h[:, dc * 512 + tqc * 128: dc * 512 + (tqc + 1) * 128]
                            al = ao_l[:, dc * 512 + tqc * 128: dc * 512 + (tqc + 1) * 128]
                            d0 = pARE[:, tqc * 1024: tqc * 1024 + 512]
                            d1 = pARE[:, tqc * 1024 + 512: (tqc + 1) * 1024]
                            wh0 = wob_h[:, wb * 1024: wb * 1024 + 512]
                            wh1 = wob_h[:, wb * 1024 + 512: (wb + 1) * 1024]
                            wl0 = wob_l[:, wb * 1024: wb * 1024 + 512]
                            wl1 = wob_l[:, wb * 1024 + 512: (wb + 1) * 1024]
                            st = dc == 0
                            sp = dc == NHC - 1
                            eng.matmul(d0, ah, wh0, start=st, stop=False)
                            eng.matmul(d1, ah, wh1, start=st, stop=False)
                            eng.matmul(d0, ah, wl0, start=False, stop=False)
                            eng.matmul(d1, ah, wl1, start=False, stop=False)
                            eng.matmul(d0, al, wh0, start=False, stop=sp)
                            ins = eng.matmul(d1, al, wh1, start=False, stop=sp)
                        ins.then_inc(sem("w_mm"), 1)
                for tqc in range(4):
                    spar = tqc % 2
                    sok = "w_sout%d" % spar
                    C.bump("w_ev")
                    if en == "scalar":
                        eng.wait_ge(sem("w_mm"), C["w_mm"])
                        eng.wait_ge(sem(sok), C.get(sok, 0))
                        eng.copy(s_half[:, spar * 1024:(spar + 1) * 1024],
                                 pARE[:, tqc * 1024:(tqc + 1) * 1024]).then_inc(sem("w_ev"), 1)
                    C.bump(sok, 16)
                    if en == "gpsimd":
                        eng.wait_ge(sem("w_ev"), C["w_ev"])
                        tc = qb * 4 + tqc
                        eng.dma_start(out=dr["s_s"].ap()[tc][:, eh * 1024:(eh + 1) * 1024],
                                      in_=s_half[:, spar * 1024:(spar + 1) * 1024]
                                      ).then_inc(sem(sok), 16)
            emit_conv_jobs(en, eng, C,
                           [("u", qb * 16 + i) for i in range(16)]
                           + [("d", qb * 16 + i) for i in range(16)])


        if en == "gpsimd":
            eng.wait_ge(sem("cv_o"), C.get("cv_o", 0))
            eng.wait_ge(sem("w_sout0"), C.get("w_sout0", 0))
            eng.wait_ge(sem("w_sout1"), C.get("w_sout1", 0))

    def _emit_c(en, eng, C, T):
        xnt_h, xnt_l = T["xnt_h"], T["xnt_l"]
        gb, ub, db = T["gb"], T["ub"], T["db"]
        h_h, h_l = T["h_h"], T["h_l"]
        sg_s, hf_s, facc = T["sg_s"], T["hf_s"], T["facc"]
        gsc, usc, dscb = T["gsc"], T["usc"], T["dscb"]
        hs_row, negh = T["hs_row"], T["negh"]
        onesb, onesf1 = T["onesb"], T["onesf1"]
        st_t, xt_t, ep_g = T["st_t"], T["xt_t"], T["ep_g"]
        ps = T["ps"]

        C.bump("c_ic", 16 * 3)
        if en == "sync":
            eng.dma_start(out=gsc[:], in_=dr["gsc"].ap()[:, :]).then_inc(sem("c_ic"), 16)
            eng.dma_start(out=usc[:], in_=dr["usc"].ap()[:, :]).then_inc(sem("c_ic"), 16)
            eng.dma_start(out=dscb[:], in_=dr["dscb"].ap()[:, :]).then_inc(sem("c_ic"), 16)
        C.bump("c_ic2")
        if en == "gpsimd":
            eng.memset(onesb[:], 1.0)
            eng.memset(onesf1[:], 1.0).then_inc(sem("c_ic2"), 1)

        def emit_epi0(fb):
            C.bump("c_hst", 2)
            if en == "vector":
                eng.wait_ge(sem("c_hsev"), C["c_hsev"])
                eng.tensor_add(hs_row[0:1, 0:512], hs_row[0:1, 0:512],
                               hs_row[0:1, 512:1024]).then_inc(sem("c_hst"), 1)
                eng.wait_ge(sem("c_hst"), C["c_hst"] - 1)
                eng.tensor_scalar_mul(hs_row[0:1, 0:512], hs_row[0:1, 0:512],
                                      -128.0).then_inc(sem("c_hst"), 1)
            for tcl in range(4):
                C.bump("c_ht")
                if en == "tensor":
                    eng.wait_ge(sem("c_hst"), C["c_hst"])
                    eng.wait_ge(sem("c_devs"), C["c_devs"])
                    eng.wait_ge(sem("c_devv"), C["c_devv"])
                    ins = eng.transpose(ps[:, 3584 + tcl * 128: 3584 + tcl * 128 + 1],
                                        hs_row[0:1, tcl * 128:(tcl + 1) * 128],
                                        onesf1[0:1, 0:1])
                    ins.then_inc(sem("c_ht"), 1)
                C.bump("c_htev")
                if en == "scalar":
                    eng.wait_ge(sem("c_ht"), C["c_ht"])
                    eng.copy(negh[:, tcl:tcl + 1],
                             ps[:, 3584 + tcl * 128: 3584 + tcl * 128 + 1]).then_inc(sem("c_htev"), 1)

        def emit_epi_t(fb, tcl):
            tc = fb * 4 + tcl
            n_sx = C.bump("c_sx", 32) // 32
            if en == "gpsimd":
                if n_sx > 1:
                    eng.wait_ge(sem("c_ep3"), C.get("c_ep3", 0))
                eng.dma_start(out=st_t[:],
                              in_=dr["s_s"].ap()[tc]).then_inc(sem("c_sx"), 16)
                eng.dma_start(out=xt_t[:],
                              in_=dr["x"].ap()[tc * 128:(tc + 1) * 128, :]).then_inc(sem("c_sx"), 16)
            C.bump("c_vv")
            C.bump("c_ep1")
            if en == "vector":
                eng.wait_ge(sem("c_htev"), C["c_htev"])
                fsl = facc[:, tcl * 2048:(tcl + 1) * 2048]
                eng.tensor_scalar_add(fsl, fsl, negh[:, tcl:tcl + 1]).then_inc(sem("c_vv"), 1)
                eng.wait_ge(sem("c_vv"), C["c_vv"])
                eng.tensor_mul(fsl, fsl, dscb[:]).then_inc(sem("c_ep1"), 1)
            C.bump("c_ep2")
            if en == "scalar":
                eng.wait_ge(sem("c_ep1"), C["c_ep1"])
                eng.wait_ge(sem("c_out"), C.get("c_out", 0))
                eng.activation(ep_g[:], facc[:, tcl * 2048:(tcl + 1) * 2048],
                               AF.Silu).then_inc(sem("c_ep2"), 1)
            C.bump("c_vv")
            C.bump("c_ep3")
            if en == "vector":
                eng.wait_ge(sem("c_ep2"), C["c_ep2"])
                eng.wait_ge(sem("c_sx"), C["c_sx"])
                eng.tensor_mul(ep_g[:], st_t[:], ep_g[:]).then_inc(sem("c_vv"), 1)
                eng.wait_ge(sem("c_vv"), C["c_vv"])
                eng.tensor_add(ep_g[:], ep_g[:], xt_t[:]).then_inc(sem("c_ep3"), 1)
            C.bump("c_out", 16)
            if en == "gpsimd":
                eng.wait_ge(sem("c_ep3"), C["c_ep3"])
                eng.dma_start(out=dr["out"].ap()[tc * 128:(tc + 1) * 128, :],
                              in_=ep_g[:]).then_inc(sem("c_out"), 16)

        pend_epi = [None]
        for fb in range(C_NFB):
            C.bump("c_xl", 32)
            if en == "sync":
                if fb >= 1:
                    eng.wait_ge(sem("c_gdone"), 2 * fb)
                eng.dma_start(out=xnt_h[:], in_=dr["xnt_h"].ap()[fb]).then_inc(sem("c_xl"), 16)
                eng.dma_start(out=xnt_l[:], in_=dr["xnt_l"].ap()[fb]).then_inc(sem("c_xl"), 16)
            for ih in range(2):
                for il in range(32):
                    ic = ih * 32 + il
                    wbg = il % 2
                    gwk = "c_gw%d" % wbg
                    C.bump(gwk, 32)
                    if en == "sync":
                        if C[gwk] > 32:
                            eng.wait_ge(sem("c_gmm"), C.get("c_gmm", 0) - 1)
                        eng.dma_start(out=gb[:, wbg * 2048:(wbg + 1) * 2048],
                                      in_=dr["g_b"].ap()[ic]).then_inc(sem(gwk), 16)
                        eng.dma_start(out=ub[:, wbg * 2048:(wbg + 1) * 2048],
                                      in_=dr["u_b"].ap()[ic]).then_inc(sem(gwk), 16)
                    gpar = il % 2
                    C.bump("c_gmm")
                    if en == "tensor":
                        eng.wait_ge(sem(gwk), C[gwk])
                        eng.wait_ge(sem("c_xl"), C["c_xl"])
                        if il == 0:
                            eng.wait_ge(sem("c_devs"), C.get("c_devs", 0))
                            eng.wait_ge(sem("c_devv"), C.get("c_devv", 0))
                        if C["c_gmm"] > 2:
                            eng.wait_ge(sem("c_h"), C["c_gmm"] - 2)
                        dstg = ps[:, gpar * 512:(gpar + 1) * 512]
                        dstu = ps[:, 1024 + gpar * 512: 1024 + (gpar + 1) * 512]
                        for hc in range(NHC):
                            wt_g = gb[:, wbg * 2048 + hc * 128: wbg * 2048 + (hc + 1) * 128]
                            wt_u = ub[:, wbg * 2048 + hc * 128: wbg * 2048 + (hc + 1) * 128]
                            rh = xnt_h[:, hc * 512:(hc + 1) * 512]
                            rl = xnt_l[:, hc * 512:(hc + 1) * 512]
                            eng.matmul(dstg, wt_g, rh, start=(hc == 0), stop=False)
                            eng.matmul(dstg, wt_g, rl, start=False, stop=(hc == NHC - 1))
                            eng.matmul(dstu, wt_u, rh, start=(hc == 0), stop=False)
                            ins = eng.matmul(dstu, wt_u, rl, start=False, stop=(hc == NHC - 1))
                        ins.then_inc(sem("c_gmm"), 1)
                    C.bump("c_sg")
                    if en == "scalar":
                        eng.wait_ge(sem("c_gmm"), C["c_gmm"])
                        if fb == 0 and ic == 0:
                            eng.wait_ge(sem("c_ic"), 48)
                        if C["c_sg"] > 2:
                            eng.wait_ge(sem("c_h"), C["c_sg"] - 2)
                        eng.activation(sg_s[:, gpar * 512:(gpar + 1) * 512],
                                       ps[:, gpar * 512:(gpar + 1) * 512],
                                       AF.Silu, scale=gsc[:, ic:ic + 1]).then_inc(sem("c_sg"), 1)
                    C.bump("c_vv", 2)
                    C.bump("c_h")
                    if en == "vector":
                        eng.wait_ge(sem("c_sg"), C["c_sg"])
                        if fb == 0 and ic == 0:
                            eng.wait_ge(sem("c_ic"), 48)
                        if C.get("c_dmm_ph", 0) > 0 and il == 0:
                            eng.wait_ge(sem("c_dmm"), C["c_dmm_ph"])
                        eng.scalar_tensor_tensor(hf_s[:, gpar * 512:(gpar + 1) * 512],
                                                 ps[:, 1024 + gpar * 512: 1024 + (gpar + 1) * 512],
                                                 usc[:, ic:ic + 1],
                                                 sg_s[:, gpar * 512:(gpar + 1) * 512],
                                                 ALU.mult, ALU.mult).then_inc(sem("c_vv"), 1)
                        eng.wait_ge(sem("c_vv"), C["c_vv"] - 1)
                        dh = h_h[:, il * 512:(il + 1) * 512]
                        dl = h_l[:, il * 512:(il + 1) * 512]
                        eng.tensor_copy(dh, hf_s[:, gpar * 512:(gpar + 1) * 512]).then_inc(sem("c_vv"), 1)
                        eng.wait_ge(sem("c_vv"), C["c_vv"])
                        eng.tensor_sub(dl, hf_s[:, gpar * 512:(gpar + 1) * 512],
                                       dh).then_inc(sem("c_h"), 1)
                    if ih == 0 and pend_epi[0] is not None:
                        if il == 2:
                            emit_epi0(pend_epi[0])
                        elif il in (5, 8, 11, 14):
                            emit_epi_t(pend_epi[0], (il - 5) // 3)
                            if il == 14:
                                pend_epi[0] = None
                    for ild in ([il - 2] if il >= 2 else []) + ([30, 31] if il == 31 else []):
                        C.bump("c_hs")
                        if en == "tensor":
                            eng.wait_ge(sem("c_h"), C["c_h"] - (il - ild))
                            if fb == 0 and ic == 0:
                                eng.wait_ge(sem("c_ic2"), 1)
                            if ild == 0:
                                eng.wait_ge(sem("c_hsev"), C.get("c_hsev", 0))
                            dh = h_h[:, ild * 512:(ild + 1) * 512]
                            dl = h_l[:, ild * 512:(ild + 1) * 512]
                            eng.matmul(ps[0:1, 2048:2560], onesb[:], dh,
                                       start=(ild == 0), stop=False)
                            ins = eng.matmul(ps[0:1, 2048:2560], onesb[:], dl,
                                             start=False, stop=(ild == 31))
                            ins.then_inc(sem("c_hs"), 1)
                C.bump("c_hsev")
                if en == "scalar":
                    eng.wait_ge(sem("c_hs"), C["c_hs"])
                    eng.wait_ge(sem("c_hst"), C.get("c_hst", 0))
                    eng.copy(hs_row[0:1, ih * 512:(ih + 1) * 512],
                             ps[0:1, 2048:2560]).then_inc(sem("c_hsev"), 1)
                C.bump("c_gdone")
                if en == "tensor":
                    eng.nop().then_inc(sem("c_gdone"), 1)
                for hhh in range(2):
                    for il in range(32):
                        ic = ih * 32 + il
                        wbd = il % 2
                        dwk = "c_dw%d" % wbd
                        C.bump(dwk, 16)
                        if en == "gpsimd":
                            if C[dwk] > 16:
                                eng.wait_ge(sem("c_dmm"), C.get("c_dmm", 0) - 1)
                            src = dr["d_b"].ap()[ic][:, hhh * 1024:(hhh + 1) * 1024]
                            eng.dma_start(out=db[:, wbd * 1024:(wbd + 1) * 1024],
                                          in_=src).then_inc(sem(dwk), 16)
                        C.bump("c_dmm")
                        if en == "tensor":
                            eng.wait_ge(sem(dwk), C[dwk])
                            eng.wait_ge(sem("c_h"), C["c_h"])
                            if il == 0:
                                eng.wait_ge(sem("c_devs"), C.get("c_devs", 0))
                                eng.wait_ge(sem("c_devv"), C.get("c_devv", 0))
                                eng.wait_ge(sem("c_hsev"), C["c_hsev"])
                                eng.wait_ge(sem("c_htev"), C.get("c_htev", 0))
                            for tcl in range(4):
                                lt_h = h_h[:, il * 512 + tcl * 128: il * 512 + (tcl + 1) * 128]
                                lt_l = h_l[:, il * 512 + tcl * 128: il * 512 + (tcl + 1) * 128]
                                d0 = ps[:, tcl * 1024: tcl * 1024 + 512]
                                d1 = ps[:, tcl * 1024 + 512: (tcl + 1) * 1024]
                                st = il == 0
                                sp = il == 31
                                eng.matmul(d0, lt_h, db[:, wbd * 1024: wbd * 1024 + 512],
                                           start=st, stop=False)
                                eng.matmul(d1, lt_h, db[:, wbd * 1024 + 512:(wbd + 1) * 1024],
                                           start=st, stop=False)
                                eng.matmul(d0, lt_l, db[:, wbd * 1024: wbd * 1024 + 512],
                                           start=False, stop=sp)
                                ins = eng.matmul(d1, lt_l, db[:, wbd * 1024 + 512:(wbd + 1) * 1024],
                                                 start=False, stop=sp)
                            ins.then_inc(sem("c_dmm"), 1)
                    for tcl in range(4):
                        if ih == 0:
                            C.bump("c_devs")
                            if en == "scalar":
                                eng.wait_ge(sem("c_dmm"), C["c_dmm"])
                                if fb > 0 and hhh == 0 and tcl == 0:
                                    eng.wait_ge(sem("c_ep2"), 4 * fb)
                                eng.copy(facc[:, tcl * 2048 + hhh * 1024:
                                              tcl * 2048 + (hhh + 1) * 1024],
                                         ps[:, tcl * 1024:(tcl + 1) * 1024]).then_inc(sem("c_devs"), 1)
                        else:
                            C.bump("c_devv")
                            if en == "vector":
                                eng.wait_ge(sem("c_dmm"), C["c_dmm"])
                                dstf = facc[:, tcl * 2048 + hhh * 1024:
                                            tcl * 2048 + (hhh + 1) * 1024]
                                eng.tensor_add(dstf, ps[:, tcl * 1024:(tcl + 1) * 1024],
                                               dstf).then_inc(sem("c_devv"), 1)
                C["c_dmm_ph"] = C["c_dmm"]
            pend_epi[0] = fb
            if fb == C_NFB - 1:
                emit_epi0(fb)
                for tcl in range(4):
                    emit_epi_t(fb, tcl)
                pend_epi[0] = None
        if en == "gpsimd":
            eng.wait_ge(sem("c_out"), C["c_out"])

    if "p0" in PHASES:
        with contextlib.ExitStack() as es:
            sb = lambda n, s, d: es.enter_context(nc.sbuf_tensor(n, s, d))
            T = {
                "wtmp": sb("p0_wtmp", [128, 4096], F32),
                "whi": sb("p0_whi", [128, 4096], BF16),
                "wlo": sb("p0_wlo", [128, 4096], BF16),
            }
            chunks = []
            for hc in range(NHC):
                src = dr["wq"].ap()[hc * 128:(hc + 1) * 128, :]
                dsth = dr["wq_h"].ap()[:, hc].transpose([1, 0, 2])
                dstl = dr["wq_l"].ap()[:, hc].transpose([1, 0, 2])
                chunks.append(("wq", src, 2048, dsth, dstl))
            for w, cols in (("wk", 512), ("wv", 512), ("wo", 2048)):
                for hc in range(NHC):
                    src = dr[w].ap()[hc * 128:(hc + 1) * 128, :]
                    chunks.append((w, src, cols, dr[w + "_h"].ap()[hc], dr[w + "_l"].ap()[hc]))
            T["chunks"] = chunks
            with nc.Block() as blk:
                mk_emit(blk, lambda en, eng, C: _emit_p0(en, eng, C, T))

    if "a" in PHASES:
        with contextlib.ExitStack() as es:
            sb = lambda n, s, d: es.enter_context(nc.sbuf_tensor(n, s, d))
            pt = lambda n, s: es.enter_context(nc.psum_tensor(n, s, F32))
            T = {
                "xt": sb("a_xt", [128, 4096], F32),
                "xn": sb("a_xn", [128, 2048], F32),
                "xnt_h": sb("a_xnth", [128, 8192], BF16),
                "xnt_l": sb("a_xntl", [128, 8192], BF16),
                "wbh": sb("a_wbh", [128, 8192], BF16),
                "wbl": sb("a_wbl", [128, 8192], BF16),
                "ck": sb("a_ck", [128, 2048], F32),
                "sk": sb("a_sk", [128, 2048], F32),
                "stats": sb("a_stats", [128, 8], F32),
                "kf": sb("a_kf", [128, 512], F32),
                "kc": sb("a_kc", [128, 512], F32),
                "ks": sb("a_ks", [128, 512], F32),
                "ksw": sb("a_ksw", [128, 512], F32),
                "identf": sb("a_identf", [128, 128], F32),
                "tbuf": sb("a_tbuf", [128, 2048], F32),
                "ptr": pt("a_ptr", [128, 2048]),
                "pkv": pt("a_pkv", [128, 1024]),
            }
            with nc.Block() as blk:
                mk_emit(blk, lambda en, eng, C: _emit_a(en, eng, C, T))

    if "b1" in PHASES:
        with contextlib.ExitStack() as es:
            sb = lambda n, s, d: es.enter_context(nc.sbuf_tensor(n, s, d))
            pt = lambda n, s: es.enter_context(nc.psum_tensor(n, s, F32))
            T = {
                "xnt_h": sb("b_xnth", [128, 8192], BF16),
                "xnt_l": sb("b_xntl", [128, 8192], BF16),
                "cq": sb("b_cq", [128, 512], F32),
                "sq": sb("b_sq", [128, 512], F32),
                "wqb_h": sb("b_wqbh", [128, 4096], BF16),
                "wqb_l": sb("b_wqbl", [128, 4096], BF16),
                "qf": sb("b_qf", [128, 512], F32),
                "qc": sb("b_qc", [128, 512], F32),
                "qs": sb("b_qs", [128, 512], F32),
                "qsw": sb("b_qsw", [128, 512], F32),
                "q_h": sb("b_qh", [128, 1024], BF16),
                "q_l": sb("b_ql", [128, 1024], BF16),
                "p_f": sb("b_pf", [128, 2048], F32),
                "p_b": sb("b_pb", [128, 2048], BF16),
                "p_r": sb("b_pr", [128, 2048], BF16),
                "ao_h": sb("b_aoh", [128, HQ * 512], BF16),
                "ao_l": sb("b_aol", [128, HQ * 512], BF16),
                "ao_f": sb("b_aof", [128, 512], F32),
                "rr_b": sb("b_rrb", [128, 512], F32),
                "rs_row": sb("b_rsrow", [1, 512], F32),
                "rr_row": sb("b_rrrow", [1, 512], F32),
                "negc": sb("b_negc", [128, 1], F32),
                "ones_b": sb("b_onesb", [128, 1], BF16),
                "ones_rf": sb("b_onesrf", [1, 128], F32),
                "cmask": sb("b_cmask", [128, 2048], BF16),
                "identb": sb("b_identb", [128, 128], BF16),
                "pARE": pt("b_pARE", [128, 4096]),
                "wob_h": sb("b_wobh", [128, 2048], BF16),
                "wob_l": sb("b_wobl", [128, 2048], BF16),
                "s_half": sb("b_shalf", [128, 2048], F32),
            }
            with nc.Block() as blk:
                mk_emit(blk, lambda en, eng, C: _emit_b1(en, eng, C, T))

    if "devkv" in PHASES:
        dkh = nc.dram_tensor("dev_kh", [128, NKV * S], BF16, kind="ExternalOutput")
        dkl = nc.dram_tensor("dev_kl", [128, NKV * S], BF16, kind="ExternalOutput")
        dvf = nc.dram_tensor("dev_vf", [128, NTC * 512], F32, kind="ExternalOutput")
        with nc.Block() as blk:
            @blk.gpsimd
            def _(eng):
                eng.dma_start(out=dkh.ap()[:, :], in_=g_k_h[:]).then_inc(sem("devkv"), 16)
                eng.dma_start(out=dkl.ap()[:, :], in_=g_k_l[:]).then_inc(sem("devkv"), 16)
                eng.dma_start(out=dvf.ap()[:, :], in_=g_v_f[:]).then_inc(sem("devkv"), 16)
                eng.wait_ge(sem("devkv"), 48)

    pstack.close()

    if "c" in PHASES:
        with contextlib.ExitStack() as es:
            sb = lambda n, s, d: es.enter_context(nc.sbuf_tensor(n, s, d))
            pt = lambda n, s: es.enter_context(nc.psum_tensor(n, s, F32))
            T = {
                "xnt_h": sb("c_xnth", [128, 8192], BF16),
                "xnt_l": sb("c_xntl", [128, 8192], BF16),
                "gb": sb("c_gb", [128, 4096], BF16),
                "ub": sb("c_ub", [128, 4096], BF16),
                "db": sb("c_db", [128, 2048], BF16),
                "h_h": sb("c_hh", [128, 32 * 512], BF16),
                "h_l": sb("c_hl", [128, 32 * 512], BF16),
                "sg_s": sb("c_sgs", [128, 1024], F32),
                "hf_s": sb("c_hfs", [128, 1024], F32),
                "facc": sb("c_facc", [128, 4 * 2048], F32),
                "gsc": sb("c_gsc", [128, NIC], F32),
                "usc": sb("c_usc", [128, NIC], F32),
                "dscb": sb("c_dscb", [128, E], F32),
                "hs_row": sb("c_hsrow", [1, 1024], F32),
                "negh": sb("c_negh", [128, 4], F32),
                "onesb": sb("c_onesb", [128, 1], BF16),
                "onesf1": sb("c_onesf1", [1, 1], F32),
                "st_t": sb("c_stt", [128, 2048], F32),
                "xt_t": sb("c_xtt", [128, 2048], F32),
                "ep_g": sb("c_epg", [128, 2048], F32),
                "ps": pt("c_ps", [128, 4096]),
            }
            with nc.Block() as blk:
                mk_emit(blk, lambda en, eng, C: _emit_c(en, eng, C, T))

    return nc


def kernel(**inputs):
    in_maps = _host_prep(**inputs)
    nc = build_nc()
    res = run_bass_kernel_spmd(nc, in_maps, core_ids=list(range(8)))
    return np.stack([np.asarray(res.results[c]["out"]) for c in range(8)], axis=0)


# version 14
# speedup vs baseline: 1.1953x; 1.0291x over previous
import contextlib
import numpy as np
import ml_dtypes
import concourse.bass as bass
import concourse.mybir as mybir
from concourse.bass_utils import run_bass_kernel_spmd

F32 = mybir.dt.float32
BF16 = mybir.dt.bfloat16
I32 = mybir.dt.int32
AF = mybir.ActivationFunctionType
ALU = mybir.AluOpType
AX = mybir.AxisListType.X
bfdt = ml_dtypes.bfloat16

S = 2048
E = 2048
HQ = 16
NKV = 4
D = 128
II = 8192
NTC = 16
NQB = 4
NHC = 16
NIC = 64
CSHIFT = 8.0
EPS = 1e-5
RSQD = float(1.0 / np.sqrt(128.0))

ENGINES = ("sync", "tensor", "scalar", "vector", "gpsimd")
PHASES = ("p0", "a", "b1", "b2", "c")
C_NFB = NQB
DEV_OUT = ()



_TAB_CACHE = {}


def _rope_tables_f32():
    if "cs" not in _TAB_CACHE:
        import subprocess
        import sys
        import tempfile
        code = (
            "import jax\n"
            "jax.config.update('jax_platforms', 'cpu')\n"
            "jax.config.update('jax_enable_x64', True)\n"
            "import numpy as np, jax.numpy as jnp, sys\n"
            "inv = 1.0 / 10000.0 ** (jnp.arange(0, 128, 2, dtype=jnp.float32) / 128)\n"
            "t = jnp.arange(2048, dtype=jnp.float32)\n"
            "fr = jnp.outer(t, inv)\n"
            "np.save(sys.argv[1], np.stack([np.asarray(jnp.cos(fr)), np.asarray(jnp.sin(fr))]))\n"
        )
        with tempfile.TemporaryDirectory() as td:
            p = td + "/t.npy"
            subprocess.run([sys.executable, "-c", code, p], check=True,
                           capture_output=True)
            cs = np.load(p)
        _TAB_CACHE["cs"] = (cs[0].astype(np.float32), cs[1].astype(np.float32))
    return _TAB_CACHE["cs"]


def _host_prep(L0_attn_x, L0_attn_Wq, L0_attn_Wk, L0_attn_Wv, L0_attn_Wo,
               L0_ffn_gate_w, L0_ffn_gate_scale, L0_ffn_up_w, L0_ffn_up_scale,
               L0_ffn_down_w, L0_ffn_down_scale):
    cm = np.ascontiguousarray
    x = np.asarray(L0_attn_x, dtype=np.float32)
    shared = {
        "wq": cm(np.asarray(L0_attn_Wq, np.float32)),
        "wk": cm(np.asarray(L0_attn_Wk, np.float32)),
        "wv": cm(np.asarray(L0_attn_Wv, np.float32)),
        "wo": cm(np.asarray(L0_attn_Wo, np.float32)),
        "gwt": cm(np.asarray(L0_ffn_gate_w, np.int32).T),
        "uwt": cm(np.asarray(L0_ffn_up_w, np.int32).T),
        "dwt": cm(np.asarray(L0_ffn_down_w, np.int32).T),
        "gsc": cm(np.asarray(L0_ffn_gate_scale, np.float32).reshape(NIC, 128).T),
        "usc": cm(np.asarray(L0_ffn_up_scale, np.float32).reshape(NIC, 128).T),
        "dscb": cm(np.broadcast_to(
            np.asarray(L0_ffn_down_scale, np.float32)[None, :], (128, E))),
    }
    cos32, sin32 = _rope_tables_f32()
    cos = np.concatenate([cos32, cos32], axis=1).astype(np.float64)
    sinn = np.concatenate([-sin32, sin32], axis=1).astype(np.float64)
    for nm, tab in (("cosk", cos.T), ("sink", sinn.T),
                    ("cosq", cos.T * RSQD), ("sinq", sinn.T * RSQD)):
        hi = tab.astype(np.float32)
        shared[nm] = cm(hi)
        shared[nm + "_lo"] = cm((tab - hi.astype(np.float64)).astype(np.float32))
    cmask = np.zeros((128, 4 * 512), dtype=np.float32)
    tk = np.arange(128)[:, None]
    tq = np.arange(512)[None, :]
    for p in range(4):
        cmask[:, p * 512:(p + 1) * 512] = np.where(tq >= p * 128 + tk, 0.0, -1e30)
    shared["cmask"] = cm(cmask.astype(bfdt))
    shared["identb"] = cm(np.eye(128, dtype=bfdt))
    shared["identf"] = cm(np.eye(128, dtype=np.float32))
    return [dict(shared, x=cm(x[c])) for c in range(8)]


class Ctr(dict):
    def bump(self, k, n=1):
        self[k] = self.get(k, 0) + n
        return self[k]


def build_nc():
    nc = bass.Bass("TRN2", target_bir_lowering=False)

    dr = {}
    dr["x"] = nc.dram_tensor("x", [S, E], F32, kind="ExternalInput")
    dr["wq"] = nc.dram_tensor("wq", [E, E], F32, kind="ExternalInput")
    dr["wk"] = nc.dram_tensor("wk", [E, 512], F32, kind="ExternalInput")
    dr["wv"] = nc.dram_tensor("wv", [E, 512], F32, kind="ExternalInput")
    dr["wo"] = nc.dram_tensor("wo", [E, E], F32, kind="ExternalInput")
    dr["gwt"] = nc.dram_tensor("gwt", [E, II], I32, kind="ExternalInput")
    dr["uwt"] = nc.dram_tensor("uwt", [E, II], I32, kind="ExternalInput")
    dr["dwt"] = nc.dram_tensor("dwt", [II, E], I32, kind="ExternalInput")
    dr["gsc"] = nc.dram_tensor("gsc", [128, NIC], F32, kind="ExternalInput")
    dr["usc"] = nc.dram_tensor("usc", [128, NIC], F32, kind="ExternalInput")
    dr["dscb"] = nc.dram_tensor("dscb", [128, E], F32, kind="ExternalInput")
    for t in ("cosk", "sink", "cosq", "sinq"):
        dr[t] = nc.dram_tensor(t, [128, S], F32, kind="ExternalInput")
        dr[t + "_lo"] = nc.dram_tensor(t + "_lo", [128, S], F32, kind="ExternalInput")
    dr["cmask"] = nc.dram_tensor("cmask", [128, 4 * 512], BF16, kind="ExternalInput")
    dr["identb"] = nc.dram_tensor("identb", [128, 128], BF16, kind="ExternalInput")
    dr["identf"] = nc.dram_tensor("identf", [128, 128], F32, kind="ExternalInput")
    dr["out"] = nc.dram_tensor("out", [S, E], F32, kind="ExternalOutput")

    def scratch(name, shape, dt):
        kind = "ExternalOutput" if name in DEV_OUT else "Internal"
        return nc.dram_tensor(name, shape, dt, kind=kind)
    dr["wq_h"] = scratch("wq_h", [HQ, NHC, 128, 128], BF16)
    dr["wq_l"] = scratch("wq_l", [HQ, NHC, 128, 128], BF16)
    dr["wk_h"] = scratch("wk_h", [NHC, 128, 512], BF16)
    dr["wk_l"] = scratch("wk_l", [NHC, 128, 512], BF16)
    dr["wv_h"] = scratch("wv_h", [NHC, 128, 512], BF16)
    dr["wv_l"] = scratch("wv_l", [NHC, 128, 512], BF16)
    dr["wo_h"] = scratch("wo_h", [NHC, 128, E], BF16)
    dr["wo_l"] = scratch("wo_l", [NHC, 128, E], BF16)
    dr["xnt_h"] = scratch("xnt_h", [NQB, 128, 8192], BF16)
    dr["xnt_l"] = scratch("xnt_l", [NQB, 128, 8192], BF16)
    dr["s_s"] = scratch("s_s", [NTC, 128, E], F32)
    dr["g_b"] = scratch("g_b", [NIC, 128, 2048], BF16)
    dr["u_b"] = scratch("u_b", [NIC, 128, 2048], BF16)
    dr["d_b"] = scratch("d_b", [NIC, 128, 2048], BF16)

    sems = {}

    def sem(name):
        if name not in sems:
            sems[name] = nc.alloc_semaphore(name)
        return sems[name]

    ctrs = {}

    def emit_conv_jobs(en, eng, C, jobs, cbuf=None):
        for kind, ic in jobs:
            C.bump("cv_o", 16)
            if en == "gpsimd":
                if kind == "d":
                    srca = dr["dwt"].ap()[ic * 128:(ic + 1) * 128, :]
                    dsts = dr["d_b"].ap()[ic]
                else:
                    w = dr["gwt"] if kind == "g" else dr["uwt"]
                    srca = w.ap()[:, ic * 128:(ic + 1) * 128].rearrange(
                        "(hh p) j -> p hh j", p=128)
                    dsts = dr[{"g": "g_b", "u": "u_b"}[kind]].ap()[ic].rearrange(
                        "p (hh j) -> p hh j", j=128)
                eng.dma_start(out=dsts, in_=srca).then_inc(sem("cv_o"), 16)

    def mk_emit(block, fn):
        for en in ENGINES:
            def closure(eng, en=en):
                fn(en, eng, ctrs.setdefault(en, Ctr()))
            getattr(block, en)(closure)

    pstack = contextlib.ExitStack()
    g_k_h = pstack.enter_context(nc.sbuf_tensor("g_k_h", [128, NKV * S], BF16))
    g_k_l = pstack.enter_context(nc.sbuf_tensor("g_k_l", [128, NKV * S], BF16))
    g_v_f = pstack.enter_context(nc.sbuf_tensor("g_v_f", [128, NTC * 512], F32))

    def _emit_p0(en, eng, C, T):
        wtmp, whi, wlo, chunks = T["wtmp"], T["whi"], T["wlo"], T["chunks"]
        for i, (nm, src, cols, dsth, dstl) in enumerate(chunks):
            b = i % 2
            ldk = "p0_ld%d" % b
            stk = "p0_st%d" % b
            C.bump(ldk, 16)
            if en == "sync":
                if i >= 2:
                    eng.wait_ge(sem(stk), C[stk])
                eng.dma_start(out=wtmp[:, b * 2048: b * 2048 + cols], in_=src).then_inc(sem(ldk), 16)
            C.bump("p0_hi")
            if en == "scalar":
                eng.wait_ge(sem(ldk), C[ldk])
                if i >= 2:
                    eng.wait_ge(sem(stk), C[stk])
                eng.copy(whi[:, b * 2048: b * 2048 + cols],
                         wtmp[:, b * 2048: b * 2048 + cols]).then_inc(sem("p0_hi"), 1)
            C.bump("p0_lo")
            if en == "vector":
                eng.wait_ge(sem("p0_hi"), C["p0_hi"])
                eng.tensor_sub(wlo[:, b * 2048: b * 2048 + cols],
                               wtmp[:, b * 2048: b * 2048 + cols],
                               whi[:, b * 2048: b * 2048 + cols]).then_inc(sem("p0_lo"), 1)
            C.bump(stk, 32)
            if en == "gpsimd":
                eng.wait_ge(sem("p0_hi"), C["p0_hi"])
                srch = whi[:, b * 2048: b * 2048 + cols]
                srcl = wlo[:, b * 2048: b * 2048 + cols]
                if nm == "wq":
                    srch = srch.rearrange("p (a j) -> p a j", j=128)
                    srcl = srcl.rearrange("p (a j) -> p a j", j=128)
                eng.dma_start(out=dsth, in_=srch).then_inc(sem(stk), 16)
                eng.wait_ge(sem("p0_lo"), C["p0_lo"])
                eng.dma_start(out=dstl, in_=srcl).then_inc(sem(stk), 16)
        if en == "gpsimd":
            eng.wait_ge(sem("p0_st0"), C["p0_st0"])
            eng.wait_ge(sem("p0_st1"), C["p0_st1"])

    def _emit_a(en, eng, C, T):
        xt, xn = T["xt"], T["xn"]
        xnt_h, xnt_l = T["xnt_h"], T["xnt_l"]
        wbh, wbl = T["wbh"], T["wbl"]
        ck, sk, stats = T["ck"], T["sk"], T["stats"]
        kf, kc, ks, ksw = T["kf"], T["kc"], T["ks"], T["ksw"]
        ptr, pkv = T["ptr"], T["pkv"]
        identf = T["identf"]

        tbuf = T["tbuf"]
        C.bump("a_id", 16)
        C.bump("a_if", 16 * 2)
        C.bump("a_tl0", 16)
        if en == "sync":
            eng.dma_start(out=identf[:], in_=dr["identf"].ap()[:, :]).then_inc(sem("a_id"), 16)
            eng.dma_start(out=ck[:], in_=dr["cosk"].ap()[:, :]).then_inc(sem("a_if"), 16)
            eng.dma_start(out=sk[:], in_=dr["sink"].ap()[:, :]).then_inc(sem("a_if"), 16)
            eng.dma_start(out=tbuf[:], in_=dr["cosk_lo"].ap()[:, :]).then_inc(sem("a_tl0"), 16)
        C.bump("a_tl1")
        if en == "vector":
            eng.wait_ge(sem("a_if"), 32)
            eng.wait_ge(sem("a_tl0"), C["a_tl0"])
            eng.tensor_add(ck[:], ck[:], tbuf[:]).then_inc(sem("a_tl1"), 1)
        C.bump("a_tl0", 16)
        if en == "sync":
            eng.wait_ge(sem("a_tl1"), C["a_tl1"])
            eng.dma_start(out=tbuf[:], in_=dr["sink_lo"].ap()[:, :]).then_inc(sem("a_tl0"), 16)
        C.bump("a_tl1")
        if en == "vector":
            eng.wait_ge(sem("a_tl0"), C["a_tl0"])
            eng.tensor_add(sk[:], sk[:], tbuf[:]).then_inc(sem("a_tl1"), 1)
        C.bump("a_eps")
        if en == "gpsimd":
            eng.memset(stats[:, 7:8], EPS).then_inc(sem("a_eps"), 1)

        for qb in range(NQB):
            C.bump("a_wv", 32)
            if en == "sync":
                if qb > 0:
                    eng.wait_ge(sem("a_km"), C["a_km"])
                eng.dma_start(out=wbh[:].rearrange("p (h j) -> p h j", j=512),
                              in_=dr["wv_h"].ap().transpose([1, 0, 2])).then_inc(sem("a_wv"), 16)
                eng.dma_start(out=wbl[:].rearrange("p (h j) -> p h j", j=512),
                              in_=dr["wv_l"].ap().transpose([1, 0, 2])).then_inc(sem("a_wv"), 16)
            for j in range(4):
                tc = qb * 4 + j
                b = tc % 2
                axk = "a_x%d" % b
                C.bump(axk, 16)
                if en == "sync":
                    if tc >= 2:
                        eng.wait_ge(sem("a_sq"), tc - 1)
                        eng.wait_ge(sem("a_xn"), 2 * tc - 2)
                    eng.dma_start(out=xt[:, b * 2048:(b + 1) * 2048],
                                  in_=dr["x"].ap()[tc * 128:(tc + 1) * 128, :]).then_inc(sem(axk), 16)
                C.bump("a_sq")
                if en == "scalar":
                    eng.wait_ge(sem(axk), C[axk])
                    if tc >= 2:
                        eng.wait_ge(sem("a_tr"), 2 * tc - 2)
                    eng.activation(xn[:, b * 2048:(b + 1) * 2048],
                                   xt[:, b * 2048:(b + 1) * 2048], AF.Square,
                                   accum_out=stats[:, 3:4]).then_inc(sem("a_sq"), 1)
                C.bump("a_vr", 4)
                if en == "vector":
                    eng.wait_ge(sem(axk), C[axk])
                    eng.reduce_sum(stats[:, 0:1], xt[:, b * 2048:(b + 1) * 2048], AX).then_inc(sem("a_vr"), 1)
                    eng.wait_ge(sem("a_vr"), C["a_vr"] - 3)
                    eng.tensor_scalar_mul(stats[:, 1:2], stats[:, 0:1], 1.0 / E).then_inc(sem("a_vr"), 1)
                    eng.wait_ge(sem("a_vr"), C["a_vr"] - 2)
                    eng.tensor_mul(stats[:, 2:3], stats[:, 1:2], stats[:, 1:2]).then_inc(sem("a_vr"), 1)
                    eng.wait_ge(sem("a_sq"), C["a_sq"])
                    if tc == 0:
                        eng.wait_ge(sem("a_eps"), 1)
                    eng.tensor_scalar(stats[:, 4:5], stats[:, 3:4], 1.0 / E, None, ALU.mult).then_inc(sem("a_vr"), 1)
                    eng.wait_ge(sem("a_vr"), C["a_vr"])
                C.bump("a_vs")
                if en == "vector":
                    eng.tensor_sub(stats[:, 4:5], stats[:, 4:5], stats[:, 2:3]).then_inc(sem("a_vs"), 1)
                C.bump("a_sd")
                if en == "scalar":
                    eng.wait_ge(sem("a_vs"), C["a_vs"])
                    eng.activation(stats[:, 5:6], stats[:, 4:5], AF.Sqrt,
                                   bias=stats[:, 7:8]).then_inc(sem("a_sd"), 1)
                C.bump("a_xn")
                if en == "vector":
                    eng.wait_ge(sem("a_sd"), C["a_sd"])
                    eng.reciprocal(stats[:, 6:7], stats[:, 5:6]).then_inc(sem("a_xn"), 1)
                    eng.wait_ge(sem("a_xn"), C["a_xn"])
                C.bump("a_xn")
                if en == "vector":
                    if tc >= 2:
                        eng.wait_ge(sem("a_tr"), 2 * tc - 2)
                    eng.tensor_scalar(xn[:, b * 2048:(b + 1) * 2048],
                                      xt[:, b * 2048:(b + 1) * 2048],
                                      stats[:, 1:2], stats[:, 6:7],
                                      ALU.subtract, ALU.mult).then_inc(sem("a_xn"), 1)
                for g in range(2):
                    gidx = C.bump("a_tr")
                    if en == "tensor":
                        if tc == 0 and g == 0:
                            eng.wait_ge(sem("a_id"), 16)
                        eng.wait_ge(sem("a_xn"), C["a_xn"])
                        if gidx > 2:
                            eng.wait_ge(sem("a_evh"), gidx - 2)
                            eng.wait_ge(sem("a_evl"), gidx - 2)
                        if j == 0 and g == 0 and qb > 0:
                            eng.wait_ge(sem("a_vev"), C["a_vev"])
                        for hl in range(8):
                            hc = g * 8 + hl
                            ins = eng.transpose(
                                ptr[:, g * 1024 + hl * 128: g * 1024 + (hl + 1) * 128],
                                xn[:, b * 2048 + hc * 128: b * 2048 + (hc + 1) * 128],
                                identf[:])
                        ins.then_inc(sem("a_tr"), 1)
                    dsth = xnt_h[:].rearrange("p (h c) -> p h c", c=512)[
                        :, g * 8:(g + 1) * 8, j * 128:(j + 1) * 128]
                    dstl = xnt_l[:].rearrange("p (h c) -> p h c", c=512)[
                        :, g * 8:(g + 1) * 8, j * 128:(j + 1) * 128]
                    srcp = ptr[:, g * 1024:(g + 1) * 1024].rearrange("p (h c) -> p h c", c=128)
                    C.bump("a_evh")
                    if en == "scalar":
                        eng.wait_ge(sem("a_tr"), C["a_tr"])
                        eng.copy(dsth, srcp).then_inc(sem("a_evh"), 1)
                    C.bump("a_evl")
                    if en == "vector":
                        eng.wait_ge(sem("a_evh"), C["a_evh"])
                        eng.tensor_sub(dstl, srcp, dsth).then_inc(sem("a_evl"), 1)

                pb = j % 2
                nuse = C.bump("a_pkv")
                C["a_pkv_s%d" % nuse] = "a_vev"
                C.bump("a_vm")
                if en == "tensor":
                    eng.wait_ge(sem("a_wv"), C["a_wv"])
                    eng.wait_ge(sem("a_evh"), C["a_evh"])
                    eng.wait_ge(sem("a_evl"), C["a_evl"])
                    if nuse > 2:
                        sname = C["a_pkv_s%d" % (nuse - 2)]
                        eng.wait_ge(sem(sname), C["a_pkv_v%d" % (nuse - 2)])
                    first = True
                    for hc in range(NHC):
                        lt_h = xnt_h[:, hc * 512 + j * 128: hc * 512 + (j + 1) * 128]
                        lt_l = xnt_l[:, hc * 512 + j * 128: hc * 512 + (j + 1) * 128]
                        rh = wbh[:, hc * 512:(hc + 1) * 512]
                        rl = wbl[:, hc * 512:(hc + 1) * 512]
                        dst = pkv[:, pb * 512:(pb + 1) * 512]
                        eng.matmul(dst, lt_h, rh, start=first, stop=False)
                        first = False
                        eng.matmul(dst, lt_h, rl, start=False, stop=False)
                        ins = eng.matmul(dst, lt_l, rh, start=False, stop=(hc == NHC - 1))
                    ins.then_inc(sem("a_vm"), 1)
                C.bump("a_vev")
                C["a_pkv_v%d" % nuse] = C["a_vev"]
                if en == "scalar":
                    eng.wait_ge(sem("a_vm"), C["a_vm"])
                    eng.copy(g_v_f[:, tc * 512:(tc + 1) * 512],
                             pkv[:, pb * 512:(pb + 1) * 512]).then_inc(sem("a_vev"), 1)

            C.bump("a_wk", 32)
            if en == "sync":
                eng.wait_ge(sem("a_vm"), C["a_vm"])
                eng.dma_start(out=wbh[:].rearrange("p (h j) -> p h j", j=512),
                              in_=dr["wk_h"].ap().transpose([1, 0, 2])).then_inc(sem("a_wk"), 16)
                eng.dma_start(out=wbl[:].rearrange("p (h j) -> p h j", j=512),
                              in_=dr["wk_l"].ap().transpose([1, 0, 2])).then_inc(sem("a_wk"), 16)
            for kvh in range(NKV):
                pb = kvh % 2
                nuse = C.bump("a_pkv")
                C["a_pkv_s%d" % nuse] = "a_kev"
                C.bump("a_km")
                if en == "tensor":
                    eng.wait_ge(sem("a_evh"), C["a_evh"])
                    eng.wait_ge(sem("a_evl"), C["a_evl"])
                    eng.wait_ge(sem("a_wk"), C["a_wk"])
                    if nuse > 2:
                        sname = C["a_pkv_s%d" % (nuse - 2)]
                        eng.wait_ge(sem(sname), C["a_pkv_v%d" % (nuse - 2)])
                    first = True
                    for hc in range(NHC):
                        wt_h = wbh[:, hc * 512 + kvh * 128: hc * 512 + (kvh + 1) * 128]
                        wt_l = wbl[:, hc * 512 + kvh * 128: hc * 512 + (kvh + 1) * 128]
                        rh = xnt_h[:, hc * 512:(hc + 1) * 512]
                        rl = xnt_l[:, hc * 512:(hc + 1) * 512]
                        dst = pkv[:, pb * 512:(pb + 1) * 512]
                        eng.matmul(dst, wt_h, rh, start=first, stop=False)
                        first = False
                        eng.matmul(dst, wt_h, rl, start=False, stop=False)
                        ins = eng.matmul(dst, wt_l, rh, start=False, stop=(hc == NHC - 1))
                    ins.then_inc(sem("a_km"), 1)
                C.bump("a_kev")
                C["a_pkv_v%d" % nuse] = C["a_kev"]
                if en == "scalar":
                    eng.wait_ge(sem("a_km"), C["a_km"])
                    if C["a_kev"] > 1:
                        eng.wait_ge(sem("a_krs"), C["a_kev"] - 1)
                    eng.copy(kf[:], pkv[:, pb * 512:(pb + 1) * 512])
                    eng.copy(ksw[0:64, :], pkv[64:128, pb * 512:(pb + 1) * 512])
                    eng.copy(ksw[64:128, :], pkv[0:64, pb * 512:(pb + 1) * 512]).then_inc(sem("a_kev"), 1)
                C.bump("a_vv", 3)
                C.bump("a_krs")
                if en == "vector":
                    eng.wait_ge(sem("a_kev"), C["a_kev"])
                    if qb == 0 and kvh == 0:
                        eng.wait_ge(sem("a_tl1"), 2)
                    cs = ck[:, qb * 512:(qb + 1) * 512]
                    ss = sk[:, qb * 512:(qb + 1) * 512]
                    eng.tensor_mul(kc[:], kf[:], cs)
                    eng.tensor_mul(ks[:], ksw[:], ss).then_inc(sem("a_vv"), 1)
                    eng.wait_ge(sem("a_vv"), C["a_vv"] - 2)
                    eng.tensor_add(kc[:], kc[:], ks[:]).then_inc(sem("a_vv"), 1)
                    eng.wait_ge(sem("a_vv"), C["a_vv"] - 1)
                    dh = g_k_h[:, kvh * S + qb * 512: kvh * S + (qb + 1) * 512]
                    dl = g_k_l[:, kvh * S + qb * 512: kvh * S + (qb + 1) * 512]
                    eng.tensor_copy(dh, kc[:]).then_inc(sem("a_vv"), 1)
                    eng.wait_ge(sem("a_vv"), C["a_vv"])
                    eng.tensor_sub(dl, kc[:], dh).then_inc(sem("a_krs"), 1)
            C.bump("a_sp", 32)
            if en == "gpsimd":
                eng.wait_ge(sem("a_evh"), C["a_evh"])
                eng.wait_ge(sem("a_evl"), C["a_evl"])
                eng.dma_start(out=dr["xnt_h"].ap()[qb], in_=xnt_h[:]).then_inc(sem("a_sp"), 16)
                eng.dma_start(out=dr["xnt_l"].ap()[qb], in_=xnt_l[:]).then_inc(sem("a_sp"), 16)
            emit_conv_jobs(en, eng, C, [("g", qb * 16 + i) for i in range(16)])
            if qb < NQB - 1:
                if en in ("scalar", "vector"):
                    eng.wait_ge(sem("a_sp"), C["a_sp"])
                    eng.wait_ge(sem("a_vm"), C["a_vm"])
        if en == "gpsimd":
            eng.wait_ge(sem("a_sp"), C["a_sp"])

    def _emit_b1(en, eng, C, T):
        xnt_h, xnt_l = T["xnt_h"], T["xnt_l"]
        cq, sq = T["cq"], T["sq"]
        wqb_h, wqb_l = T["wqb_h"], T["wqb_l"]
        qf, qc, qs, qsw = T["qf"], T["qc"], T["qs"], T["qsw"]
        q_h, q_l = T["q_h"], T["q_l"]
        p_f, p_b, p_r = T["p_f"], T["p_b"], T["p_r"]
        ao_h, ao_l, ao_f = T["ao_h"], T["ao_l"], T["ao_f"]
        rr_b, rs_row, rr_row = T["rr_b"], T["rs_row"], T["rr_row"]
        negc, ones_b, ones_rf = T["negc"], T["ones_b"], T["ones_rf"]
        cmask, identb = T["cmask"], T["identb"]
        pARE = T["pARE"]
        pS = pARE[:, 0:2048]
        psQ = pARE[:, 2048:2560]
        pAO = pARE[:, 2560:3072]
        pRS = pARE[:, 3072:3584]
        wob_h, wob_l, s_half = T["wob_h"], T["wob_l"], T["s_half"]

        C.bump("b_cl", 32)
        if en == "sync":
            eng.dma_start(out=cmask[:], in_=dr["cmask"].ap()[:, :]).then_inc(sem("b_cl"), 16)
            eng.dma_start(out=identb[:], in_=dr["identb"].ap()[:, :]).then_inc(sem("b_cl"), 16)
        C.bump("b_ic")
        if en == "gpsimd":
            eng.memset(negc[:], -CSHIFT)
            eng.memset(ones_b[:], 1.0)
            eng.memset(ones_rf[:], 1.0).then_inc(sem("b_ic"), 1)

        gg = 0
        for qb in range(NQB):
            C.bump("b_ld", 16 * 4)
            if en == "sync":
                if qb >= 1:
                    eng.wait_ge(sem("b_qbdone"), qb)
                eng.dma_start(out=xnt_h[:], in_=dr["xnt_h"].ap()[qb]).then_inc(sem("b_ld"), 16)
                eng.dma_start(out=xnt_l[:], in_=dr["xnt_l"].ap()[qb]).then_inc(sem("b_ld"), 16)
                eng.dma_start(out=cq[:], in_=dr["cosq"].ap()[:, qb * 512:(qb + 1) * 512]).then_inc(sem("b_ld"), 16)
                eng.dma_start(out=sq[:], in_=dr["sinq"].ap()[:, qb * 512:(qb + 1) * 512]).then_inc(sem("b_ld"), 16)
            C.bump("b_tl0", 32)
            if en == "sync":
                if qb > 0:
                    eng.wait_ge(sem("b_qrs"), C["b_qrs"])
                eng.dma_start(out=qc[:], in_=dr["cosq_lo"].ap()[:, qb * 512:(qb + 1) * 512]).then_inc(sem("b_tl0"), 16)
                eng.dma_start(out=qs[:], in_=dr["sinq_lo"].ap()[:, qb * 512:(qb + 1) * 512]).then_inc(sem("b_tl0"), 16)
            C.bump("b_tl1")
            if en == "vector":
                eng.wait_ge(sem("b_ld"), C["b_ld"])
                eng.wait_ge(sem("b_tl0"), C["b_tl0"])
                eng.tensor_add(cq[:], cq[:], qc[:]).then_inc(sem("b_tl1"), 1)
                eng.wait_ge(sem("b_tl1"), C["b_tl1"])
            C.bump("b_tl1")
            if en == "vector":
                eng.tensor_add(sq[:], sq[:], qs[:]).then_inc(sem("b_tl1"), 1)
            L = 4 * qb + 4
            groups = [list(range(L))[i:i + 2] for i in range(0, L, 2)]

            def emit_qproj(qh):
                kvh = qh // 4
                qpar = qh % 2
                wqk = "b_wq%d" % (qh % 2)
                C.bump(wqk, 32)
                if en == "sync":
                    if C[wqk] > 32:
                        eng.wait_ge(sem("b_qm"), C["b_qm"] - 1)
                    eng.dma_start(out=wqb_h[:, (qh % 2) * 2048:(qh % 2 + 1) * 2048
                                              ].rearrange("p (h j) -> p h j", j=128),
                                  in_=dr["wq_h"].ap()[qh].transpose([1, 0, 2])
                                  ).then_inc(sem(wqk), 16)
                    eng.dma_start(out=wqb_l[:, (qh % 2) * 2048:(qh % 2 + 1) * 2048
                                              ].rearrange("p (h j) -> p h j", j=128),
                                  in_=dr["wq_l"].ap()[qh].transpose([1, 0, 2])
                                  ).then_inc(sem(wqk), 16)
                C.bump("b_qm")
                if en == "tensor":
                    eng.wait_ge(sem("b_ld"), C["b_ld"])
                    eng.wait_ge(sem(wqk), C[wqk])
                    if qh == 0:
                        eng.wait_ge(sem("w_ev"), C.get("w_ev", 0))
                    if C["b_qm"] > 1:
                        eng.wait_ge(sem("b_qev"), C["b_qm"] - 1)
                    first = True
                    for hc in range(NHC):
                        off = (qh % 2) * 2048 + hc * 128
                        wt_h = wqb_h[:, off: off + 128]
                        wt_l = wqb_l[:, off: off + 128]
                        rh = xnt_h[:, hc * 512:(hc + 1) * 512]
                        rl = xnt_l[:, hc * 512:(hc + 1) * 512]
                        eng.matmul(psQ, wt_h, rh, start=first, stop=False)
                        first = False
                        eng.matmul(psQ, wt_h, rl, start=False, stop=False)
                        ins = eng.matmul(psQ, wt_l, rh, start=False, stop=(hc == NHC - 1))
                    ins.then_inc(sem("b_qm"), 1)
                C.bump("b_qev")
                if en == "scalar":
                    eng.wait_ge(sem("b_qm"), C["b_qm"])
                    if C["b_qev"] > 1:
                        eng.wait_ge(sem("b_qrs"), C["b_qev"] - 1)
                    eng.copy(qf[:], psQ)
                    eng.copy(qsw[0:64, :], psQ[64:128, :])
                    eng.copy(qsw[64:128, :], psQ[0:64, :]).then_inc(sem("b_qev"), 1)
                C.bump("b_vv", 3)
                C.bump("b_qrs")
                if en == "vector":
                    if qh == 0:
                        eng.wait_ge(sem("b_tl1"), C["b_tl1"])
                    eng.wait_ge(sem("b_qev"), C["b_qev"])
                    eng.tensor_mul(qc[:], qf[:], cq[:])
                    eng.tensor_mul(qs[:], qsw[:], sq[:]).then_inc(sem("b_vv"), 1)
                    eng.wait_ge(sem("b_vv"), C["b_vv"] - 2)
                    eng.tensor_add(qc[:], qc[:], qs[:]).then_inc(sem("b_vv"), 1)
                    eng.wait_ge(sem("b_vv"), C["b_vv"] - 1)
                    qhsl = q_h[:, qpar * 512:(qpar + 1) * 512]
                    qlsl = q_l[:, qpar * 512:(qpar + 1) * 512]
                    eng.tensor_copy(qhsl, qc[:]).then_inc(sem("b_vv"), 1)
                    eng.wait_ge(sem("b_vv"), C["b_vv"])
                    eng.tensor_sub(qlsl, qc[:], qhsl).then_inc(sem("b_qrs"), 1)

            def emit_S(qh, gi, grp, gg):
                kvh = qh // 4
                qpar = qh % 2
                slot = gg % 2
                C.bump("b_S")
                if en == "tensor":
                    if gi == 0:
                        eng.wait_ge(sem("b_qrs"), C["b_qrs"])
                        if qb == 0 and qh == 0:
                            eng.wait_ge(sem("b_ic"), 1)
                            eng.wait_ge(sem("b_cl"), 32)
                    if gg > 2:
                        eng.wait_ge(sem("b_exp"), gg - 2)
                    for ti, tkc in enumerate(grp):
                        dst = pS[:, slot * 1024 + ti * 512: slot * 1024 + (ti + 1) * 512]
                        st = True
                        if tkc >= qb * 4:
                            o = tkc - qb * 4
                            eng.matmul(dst, identb[:], cmask[:, o * 512:(o + 1) * 512],
                                       start=True, stop=False)
                            st = False
                        kh = g_k_h[:, kvh * S + tkc * 128: kvh * S + (tkc + 1) * 128]
                        kl = g_k_l[:, kvh * S + tkc * 128: kvh * S + (tkc + 1) * 128]
                        qhsl = q_h[:, qpar * 512:(qpar + 1) * 512]
                        qlsl = q_l[:, qpar * 512:(qpar + 1) * 512]
                        eng.matmul(dst, kh, qhsl, start=st, stop=False)
                        eng.matmul(dst, kh, qlsl, start=False, stop=False)
                        ins = eng.matmul(dst, kl, qhsl, start=False, stop=True)
                    ins.then_inc(sem("b_S"), 1)
                C.bump("b_exp")
                if en == "scalar":
                    eng.wait_ge(sem("b_S"), C["b_S"])
                    if gg > 2:
                        eng.wait_ge(sem("b_pv"), gg - 2)
                        eng.wait_ge(sem("b_cast"), gg - 2)
                    n = len(grp)
                    eng.activation(p_f[:, slot * 1024: slot * 1024 + n * 512],
                                   pS[:, slot * 1024: slot * 1024 + n * 512],
                                   AF.Exp, bias=negc[:, 0:1]).then_inc(sem("b_exp"), 1)
                C.bump("b_vv")
                C.bump("b_cast")
                if en == "vector":
                    eng.wait_ge(sem("b_exp"), C["b_exp"])
                    n = len(grp)
                    eng.tensor_copy(p_b[:, slot * 1024: slot * 1024 + n * 512],
                                    p_f[:, slot * 1024: slot * 1024 + n * 512]
                                    ).then_inc(sem("b_vv"), 1)
                    eng.wait_ge(sem("b_vv"), C["b_vv"])
                    eng.tensor_sub(p_r[:, slot * 1024: slot * 1024 + n * 512],
                                   p_f[:, slot * 1024: slot * 1024 + n * 512],
                                   p_b[:, slot * 1024: slot * 1024 + n * 512]
                                   ).then_inc(sem("b_cast"), 1)

            def emit_PV(qh, gi, grp, gg, ng):
                kvh = qh // 4
                C.bump("b_pv")
                if en == "tensor":
                    eng.wait_ge(sem("b_exp"), C["b_exp_at%d" % gg])
                    if gi == 0:
                        eng.wait_ge(sem("b_aoev"), C.get("b_aoev", 0))
                    slot = gg % 2
                    for ti, tkc in enumerate(grp):
                        vt = g_v_f[:, tkc * 512 + kvh * 128: tkc * 512 + (kvh + 1) * 128]
                        eng.matmul(pAO[:, 0:512], vt,
                                   p_f[:, slot * 1024 + ti * 512: slot * 1024 + (ti + 1) * 512],
                                   start=(gi == 0 and ti == 0),
                                   stop=(gi == ng - 1 and ti == len(grp) - 1))
                    eng.wait_ge(sem("b_cast"), C["b_cast_at%d" % gg])
                    if gi == 0:
                        eng.wait_ge(sem("b_rrev"), C.get("b_rrev", 0))
                    for ti, tkc in enumerate(grp):
                        eng.matmul(pRS[0:1, 0:512], ones_b[:],
                                   p_b[:, slot * 1024 + ti * 512: slot * 1024 + (ti + 1) * 512],
                                   start=(gi == 0 and ti == 0), stop=False)
                        ins = eng.matmul(pRS[0:1, 0:512], ones_b[:],
                                         p_r[:, slot * 1024 + ti * 512: slot * 1024 + (ti + 1) * 512],
                                         start=False,
                                         stop=(gi == ng - 1 and ti == len(grp) - 1))
                    ins.then_inc(sem("b_pv"), 1)

            emit_qproj(0)
            for qh in range(HQ):
                ng = len(groups)
                pend = None
                for gi, grp in enumerate(groups):
                    gg = C.bump("b_gg")
                    C["b_exp_at%d" % gg] = 0
                    emit_S(qh, gi, grp, gg)
                    C["b_exp_at%d" % gg] = C["b_exp"]
                    C["b_cast_at%d" % gg] = C["b_cast"]
                    if pend is not None:
                        emit_PV(qh, pend[0], pend[1], pend[2], ng)
                    pend = (gi, grp, gg)
                if qh + 1 < HQ:
                    emit_qproj(qh + 1)
                emit_PV(qh, pend[0], pend[1], pend[2], ng)
                C.bump("b_rsev")
                if en == "scalar":
                    eng.wait_ge(sem("b_pv"), C["b_pv"])
                    eng.copy(rs_row[:], pRS[0:1, 0:512]).then_inc(sem("b_rsev"), 1)
                C.bump("b_rcp")
                if en == "vector":
                    eng.wait_ge(sem("b_rsev"), C["b_rsev"])
                    eng.reciprocal(rr_row[:], rs_row[:]).then_inc(sem("b_rcp"), 1)
                C.bump("b_bc")
                if en == "tensor":
                    eng.wait_ge(sem("b_rcp"), C["b_rcp"])
                    ins = eng.matmul(pRS[:, 0:512], ones_rf[0:1, :], rr_row[:],
                                     start=True, stop=True)
                    ins.then_inc(sem("b_bc"), 1)
                C.bump("b_rrev")
                if en == "scalar":
                    eng.wait_ge(sem("b_bc"), C["b_bc"])
                    eng.copy(rr_b[:], pRS[:, 0:512]).then_inc(sem("b_rrev"), 1)
                C.bump("b_aoev")
                if en == "scalar":
                    eng.wait_ge(sem("b_pv"), C["b_pv"])
                    if C["b_aoev"] > 1:
                        eng.wait_ge(sem("b_ao"), C["b_aoev"] - 1)
                    eng.copy(ao_f[:], pAO[:, 0:512]).then_inc(sem("b_aoev"), 1)
                C.bump("b_vv", 2)
                C.bump("b_ao")
                if en == "vector":
                    if qh == 0 and qb > 0:
                        eng.wait_ge(sem("w_mm"), C.get("w_mm", 0))
                    eng.wait_ge(sem("b_rrev"), C["b_rrev"])
                    eng.wait_ge(sem("b_aoev"), C["b_aoev"])
                    eng.tensor_mul(ao_f[:], ao_f[:], rr_b[:]).then_inc(sem("b_vv"), 1)
                    eng.wait_ge(sem("b_vv"), C["b_vv"] - 1)
                    dh = ao_h[:, qh * 512:(qh + 1) * 512]
                    dl = ao_l[:, qh * 512:(qh + 1) * 512]
                    eng.tensor_copy(dh, ao_f[:]).then_inc(sem("b_vv"), 1)
                    eng.wait_ge(sem("b_vv"), C["b_vv"])
                    eng.tensor_sub(dl, ao_f[:], dh).then_inc(sem("b_ao"), 1)
            C.bump("b_qbdone")
            if en == "gpsimd":
                eng.wait_ge(sem("b_qm"), C["b_qm"])
                eng.sem_inc(sem("b_qbdone"), 1)
            for eh in range(2):
                for dc in range(NHC):
                    wlk = "w_wl%d" % (dc % 2)
                    C.bump(wlk, 32)
                    if en == "sync":
                        if C[wlk] > 32:
                            eng.wait_ge(sem("w_mm"), C.get("w_mm", 0) - 1)
                        eng.dma_start(out=wob_h[:, (dc % 2) * 1024:(dc % 2 + 1) * 1024],
                                      in_=dr["wo_h"].ap()[dc][:, eh * 1024:(eh + 1) * 1024]
                                      ).then_inc(sem(wlk), 16)
                        eng.dma_start(out=wob_l[:, (dc % 2) * 1024:(dc % 2 + 1) * 1024],
                                      in_=dr["wo_l"].ap()[dc][:, eh * 1024:(eh + 1) * 1024]
                                      ).then_inc(sem(wlk), 16)
                    C.bump("w_mm")
                    if en == "tensor":
                        eng.wait_ge(sem(wlk), C[wlk])
                        if eh == 0 and dc == 0:
                            eng.wait_ge(sem("b_ao"), C["b_ao"])
                        if dc == 0:
                            eng.wait_ge(sem("w_ev"), C.get("w_ev", 0))
                        wb = dc % 2
                        for tqc in range(4):
                            ah = ao_h[:, dc * 512 + tqc * 128: dc * 512 + (tqc + 1) * 128]
                            al = ao_l[:, dc * 512 + tqc * 128: dc * 512 + (tqc + 1) * 128]
                            d0 = pARE[:, tqc * 1024: tqc * 1024 + 512]
                            d1 = pARE[:, tqc * 1024 + 512: (tqc + 1) * 1024]
                            wh0 = wob_h[:, wb * 1024: wb * 1024 + 512]
                            wh1 = wob_h[:, wb * 1024 + 512: (wb + 1) * 1024]
                            wl0 = wob_l[:, wb * 1024: wb * 1024 + 512]
                            wl1 = wob_l[:, wb * 1024 + 512: (wb + 1) * 1024]
                            st = dc == 0
                            sp = dc == NHC - 1
                            eng.matmul(d0, ah, wh0, start=st, stop=False)
                            eng.matmul(d1, ah, wh1, start=st, stop=False)
                            eng.matmul(d0, ah, wl0, start=False, stop=False)
                            eng.matmul(d1, ah, wl1, start=False, stop=False)
                            eng.matmul(d0, al, wh0, start=False, stop=sp)
                            ins = eng.matmul(d1, al, wh1, start=False, stop=sp)
                        ins.then_inc(sem("w_mm"), 1)
                for tqc in range(4):
                    spar = tqc % 2
                    sok = "w_sout%d" % spar
                    C.bump("w_ev")
                    if en == "scalar":
                        eng.wait_ge(sem("w_mm"), C["w_mm"])
                        eng.wait_ge(sem(sok), C.get(sok, 0))
                        eng.copy(s_half[:, spar * 1024:(spar + 1) * 1024],
                                 pARE[:, tqc * 1024:(tqc + 1) * 1024]).then_inc(sem("w_ev"), 1)
                    C.bump(sok, 16)
                    if en == "gpsimd":
                        eng.wait_ge(sem("w_ev"), C["w_ev"])
                        tc = qb * 4 + tqc
                        eng.dma_start(out=dr["s_s"].ap()[tc][:, eh * 1024:(eh + 1) * 1024],
                                      in_=s_half[:, spar * 1024:(spar + 1) * 1024]
                                      ).then_inc(sem(sok), 16)
            emit_conv_jobs(en, eng, C,
                           [("u", qb * 16 + i) for i in range(16)]
                           + [("d", qb * 16 + i) for i in range(16)])


        if en == "gpsimd":
            eng.wait_ge(sem("cv_o"), C.get("cv_o", 0))
            eng.wait_ge(sem("w_sout0"), C.get("w_sout0", 0))
            eng.wait_ge(sem("w_sout1"), C.get("w_sout1", 0))

    def _emit_c(en, eng, C, T):
        xnt_h, xnt_l = T["xnt_h"], T["xnt_l"]
        gb, ub, db = T["gb"], T["ub"], T["db"]
        h_h, h_l = T["h_h"], T["h_l"]
        sg_s, hf_s, facc = T["sg_s"], T["hf_s"], T["facc"]
        gsc, usc, dscb = T["gsc"], T["usc"], T["dscb"]
        hs_row, negh = T["hs_row"], T["negh"]
        onesb, onesf1 = T["onesb"], T["onesf1"]
        st_t, xt_t, ep_g = T["st_t"], T["xt_t"], T["ep_g"]
        ps = T["ps"]

        C.bump("c_ic", 16 * 3)
        if en == "sync":
            eng.dma_start(out=gsc[:], in_=dr["gsc"].ap()[:, :]).then_inc(sem("c_ic"), 16)
            eng.dma_start(out=usc[:], in_=dr["usc"].ap()[:, :]).then_inc(sem("c_ic"), 16)
            eng.dma_start(out=dscb[:], in_=dr["dscb"].ap()[:, :]).then_inc(sem("c_ic"), 16)
        C.bump("c_ic2")
        if en == "gpsimd":
            eng.memset(onesb[:], 1.0)
            eng.memset(onesf1[:], 1.0).then_inc(sem("c_ic2"), 1)

        def emit_epi0(fb):
            C.bump("c_hst", 2)
            if en == "vector":
                eng.wait_ge(sem("c_hsev"), C["c_hsev"])
                eng.tensor_add(hs_row[0:1, 0:512], hs_row[0:1, 0:512],
                               hs_row[0:1, 512:1024]).then_inc(sem("c_hst"), 1)
                eng.wait_ge(sem("c_hst"), C["c_hst"] - 1)
                eng.tensor_scalar_mul(hs_row[0:1, 0:512], hs_row[0:1, 0:512],
                                      -128.0).then_inc(sem("c_hst"), 1)
            for tcl in range(4):
                C.bump("c_ht")
                if en == "tensor":
                    eng.wait_ge(sem("c_hst"), C["c_hst"])
                    eng.wait_ge(sem("c_devs"), C["c_devs"])
                    eng.wait_ge(sem("c_devv"), C["c_devv"])
                    ins = eng.transpose(ps[:, 3584 + tcl * 128: 3584 + tcl * 128 + 1],
                                        hs_row[0:1, tcl * 128:(tcl + 1) * 128],
                                        onesf1[0:1, 0:1])
                    ins.then_inc(sem("c_ht"), 1)
                C.bump("c_htev")
                if en == "scalar":
                    eng.wait_ge(sem("c_ht"), C["c_ht"])
                    eng.copy(negh[:, tcl:tcl + 1],
                             ps[:, 3584 + tcl * 128: 3584 + tcl * 128 + 1]).then_inc(sem("c_htev"), 1)

        def emit_epi_t(fb, tcl):
            tc = fb * 4 + tcl
            n_sx = C.bump("c_sx", 32) // 32
            if en == "gpsimd":
                if n_sx > 1:
                    eng.wait_ge(sem("c_ep3"), C.get("c_ep3", 0))
                eng.dma_start(out=st_t[:],
                              in_=dr["s_s"].ap()[tc]).then_inc(sem("c_sx"), 16)
                eng.dma_start(out=xt_t[:],
                              in_=dr["x"].ap()[tc * 128:(tc + 1) * 128, :]).then_inc(sem("c_sx"), 16)
            C.bump("c_vv")
            C.bump("c_ep1")
            if en == "vector":
                eng.wait_ge(sem("c_htev"), C["c_htev"])
                fsl = facc[:, tcl * 2048:(tcl + 1) * 2048]
                eng.tensor_scalar_add(fsl, fsl, negh[:, tcl:tcl + 1]).then_inc(sem("c_vv"), 1)
                eng.wait_ge(sem("c_vv"), C["c_vv"])
                eng.tensor_mul(fsl, fsl, dscb[:]).then_inc(sem("c_ep1"), 1)
            C.bump("c_ep2")
            if en == "scalar":
                eng.wait_ge(sem("c_ep1"), C["c_ep1"])
                eng.wait_ge(sem("c_out"), C.get("c_out", 0))
                eng.activation(ep_g[:], facc[:, tcl * 2048:(tcl + 1) * 2048],
                               AF.Silu).then_inc(sem("c_ep2"), 1)
            C.bump("c_vv")
            C.bump("c_ep3")
            if en == "vector":
                eng.wait_ge(sem("c_ep2"), C["c_ep2"])
                eng.wait_ge(sem("c_sx"), C["c_sx"])
                eng.tensor_mul(ep_g[:], st_t[:], ep_g[:]).then_inc(sem("c_vv"), 1)
                eng.wait_ge(sem("c_vv"), C["c_vv"])
                eng.tensor_add(ep_g[:], ep_g[:], xt_t[:]).then_inc(sem("c_ep3"), 1)
            C.bump("c_out", 16)
            if en == "gpsimd":
                eng.wait_ge(sem("c_ep3"), C["c_ep3"])
                eng.dma_start(out=dr["out"].ap()[tc * 128:(tc + 1) * 128, :],
                              in_=ep_g[:]).then_inc(sem("c_out"), 16)

        pend_epi = [None]
        for fb in range(C_NFB):
            C.bump("c_xl", 32)
            if en == "sync":
                if fb >= 1:
                    eng.wait_ge(sem("c_gdone"), 2 * fb)
                eng.dma_start(out=xnt_h[:], in_=dr["xnt_h"].ap()[fb]).then_inc(sem("c_xl"), 16)
                eng.dma_start(out=xnt_l[:], in_=dr["xnt_l"].ap()[fb]).then_inc(sem("c_xl"), 16)
            for ih in range(2):
                for il in range(32):
                    ic = ih * 32 + il
                    wbg = il % 2
                    gwk = "c_gw%d" % wbg
                    C.bump(gwk, 32)
                    if en == "sync":
                        if C[gwk] > 32:
                            eng.wait_ge(sem("c_gmm"), C.get("c_gmm", 0) - 1)
                        eng.dma_start(out=gb[:, wbg * 2048:(wbg + 1) * 2048],
                                      in_=dr["g_b"].ap()[ic]).then_inc(sem(gwk), 16)
                        eng.dma_start(out=ub[:, wbg * 2048:(wbg + 1) * 2048],
                                      in_=dr["u_b"].ap()[ic]).then_inc(sem(gwk), 16)
                    gpar = il % 2
                    C.bump("c_gmm")
                    if en == "tensor":
                        eng.wait_ge(sem(gwk), C[gwk])
                        eng.wait_ge(sem("c_xl"), C["c_xl"])
                        if il == 0:
                            eng.wait_ge(sem("c_devs"), C.get("c_devs", 0))
                            eng.wait_ge(sem("c_devv"), C.get("c_devv", 0))
                        if C["c_gmm"] > 2:
                            eng.wait_ge(sem("c_h"), C["c_gmm"] - 2)
                        dstg = ps[:, gpar * 512:(gpar + 1) * 512]
                        dstu = ps[:, 1024 + gpar * 512: 1024 + (gpar + 1) * 512]
                        for hc in range(NHC):
                            wt_g = gb[:, wbg * 2048 + hc * 128: wbg * 2048 + (hc + 1) * 128]
                            wt_u = ub[:, wbg * 2048 + hc * 128: wbg * 2048 + (hc + 1) * 128]
                            rh = xnt_h[:, hc * 512:(hc + 1) * 512]
                            rl = xnt_l[:, hc * 512:(hc + 1) * 512]
                            eng.matmul(dstg, wt_g, rh, start=(hc == 0), stop=False)
                            eng.matmul(dstg, wt_g, rl, start=False, stop=(hc == NHC - 1))
                            eng.matmul(dstu, wt_u, rh, start=(hc == 0), stop=False)
                            ins = eng.matmul(dstu, wt_u, rl, start=False, stop=(hc == NHC - 1))
                        ins.then_inc(sem("c_gmm"), 1)
                    C.bump("c_sg")
                    if en == "scalar":
                        eng.wait_ge(sem("c_gmm"), C["c_gmm"])
                        if fb == 0 and ic == 0:
                            eng.wait_ge(sem("c_ic"), 48)
                        if C["c_sg"] > 2:
                            eng.wait_ge(sem("c_h"), C["c_sg"] - 2)
                        eng.activation(sg_s[:, gpar * 512:(gpar + 1) * 512],
                                       ps[:, gpar * 512:(gpar + 1) * 512],
                                       AF.Silu, scale=gsc[:, ic:ic + 1]).then_inc(sem("c_sg"), 1)
                    C.bump("c_vv", 2)
                    C.bump("c_h")
                    if en == "vector":
                        eng.wait_ge(sem("c_sg"), C["c_sg"])
                        if fb == 0 and ic == 0:
                            eng.wait_ge(sem("c_ic"), 48)
                        if C.get("c_dmm_ph", 0) > 0 and il == 0:
                            eng.wait_ge(sem("c_dmm"), C["c_dmm_ph"])
                        eng.scalar_tensor_tensor(hf_s[:, gpar * 512:(gpar + 1) * 512],
                                                 ps[:, 1024 + gpar * 512: 1024 + (gpar + 1) * 512],
                                                 usc[:, ic:ic + 1],
                                                 sg_s[:, gpar * 512:(gpar + 1) * 512],
                                                 ALU.mult, ALU.mult).then_inc(sem("c_vv"), 1)
                        eng.wait_ge(sem("c_vv"), C["c_vv"] - 1)
                        dh = h_h[:, il * 512:(il + 1) * 512]
                        dl = h_l[:, il * 512:(il + 1) * 512]
                        eng.tensor_copy(dh, hf_s[:, gpar * 512:(gpar + 1) * 512]).then_inc(sem("c_vv"), 1)
                        eng.wait_ge(sem("c_vv"), C["c_vv"])
                        eng.tensor_sub(dl, hf_s[:, gpar * 512:(gpar + 1) * 512],
                                       dh).then_inc(sem("c_h"), 1)
                    if ih == 0 and pend_epi[0] is not None:
                        if il == 2:
                            emit_epi0(pend_epi[0])
                        elif il in (5, 8, 11, 14):
                            emit_epi_t(pend_epi[0], (il - 5) // 3)
                            if il == 14:
                                pend_epi[0] = None
                    for ild in ([il - 2] if il >= 2 else []) + ([30, 31] if il == 31 else []):
                        C.bump("c_hs")
                        if en == "tensor":
                            eng.wait_ge(sem("c_h"), C["c_h"] - (il - ild))
                            if fb == 0 and ih == 0 and ild == 0:
                                eng.wait_ge(sem("c_ic2"), 1)
                            if ild == 0:
                                eng.wait_ge(sem("c_hsev"), C.get("c_hsev", 0))
                            dh = h_h[:, ild * 512:(ild + 1) * 512]
                            dl = h_l[:, ild * 512:(ild + 1) * 512]
                            eng.matmul(ps[0:1, 2048:2560], onesb[:], dh,
                                       start=(ild == 0), stop=False)
                            ins = eng.matmul(ps[0:1, 2048:2560], onesb[:], dl,
                                             start=False, stop=(ild == 31))
                            ins.then_inc(sem("c_hs"), 1)
                C.bump("c_hsev")
                if en == "scalar":
                    eng.wait_ge(sem("c_hs"), C["c_hs"])
                    eng.wait_ge(sem("c_hst"), C.get("c_hst", 0))
                    eng.copy(hs_row[0:1, ih * 512:(ih + 1) * 512],
                             ps[0:1, 2048:2560]).then_inc(sem("c_hsev"), 1)
                C.bump("c_gdone")
                if en == "tensor":
                    eng.nop().then_inc(sem("c_gdone"), 1)
                for hhh in range(2):
                    for il in range(32):
                        ic = ih * 32 + il
                        wbd = il % 2
                        dwk = "c_dw%d" % wbd
                        C.bump(dwk, 16)
                        if en == "gpsimd":
                            if C[dwk] > 16:
                                eng.wait_ge(sem("c_dmm"), C.get("c_dmm", 0) - 1)
                            src = dr["d_b"].ap()[ic][:, hhh * 1024:(hhh + 1) * 1024]
                            eng.dma_start(out=db[:, wbd * 1024:(wbd + 1) * 1024],
                                          in_=src).then_inc(sem(dwk), 16)
                        C.bump("c_dmm")
                        if en == "tensor":
                            eng.wait_ge(sem(dwk), C[dwk])
                            eng.wait_ge(sem("c_h"), C["c_h"])
                            if il == 0:
                                eng.wait_ge(sem("c_devs"), C.get("c_devs", 0))
                                eng.wait_ge(sem("c_devv"), C.get("c_devv", 0))
                                eng.wait_ge(sem("c_hsev"), C["c_hsev"])
                                eng.wait_ge(sem("c_htev"), C.get("c_htev", 0))
                            for tcl in range(4):
                                lt_h = h_h[:, il * 512 + tcl * 128: il * 512 + (tcl + 1) * 128]
                                lt_l = h_l[:, il * 512 + tcl * 128: il * 512 + (tcl + 1) * 128]
                                d0 = ps[:, tcl * 1024: tcl * 1024 + 512]
                                d1 = ps[:, tcl * 1024 + 512: (tcl + 1) * 1024]
                                st = il == 0
                                sp = il == 31
                                eng.matmul(d0, lt_h, db[:, wbd * 1024: wbd * 1024 + 512],
                                           start=st, stop=False)
                                eng.matmul(d1, lt_h, db[:, wbd * 1024 + 512:(wbd + 1) * 1024],
                                           start=st, stop=False)
                                eng.matmul(d0, lt_l, db[:, wbd * 1024: wbd * 1024 + 512],
                                           start=False, stop=sp)
                                ins = eng.matmul(d1, lt_l, db[:, wbd * 1024 + 512:(wbd + 1) * 1024],
                                                 start=False, stop=sp)
                            ins.then_inc(sem("c_dmm"), 1)
                    for tcl in range(4):
                        if ih == 0:
                            C.bump("c_devs")
                            if en == "scalar":
                                eng.wait_ge(sem("c_dmm"), C["c_dmm"])
                                if fb > 0 and hhh == 0 and tcl == 0:
                                    eng.wait_ge(sem("c_ep2"), 4 * fb)
                                eng.copy(facc[:, tcl * 2048 + hhh * 1024:
                                              tcl * 2048 + (hhh + 1) * 1024],
                                         ps[:, tcl * 1024:(tcl + 1) * 1024]).then_inc(sem("c_devs"), 1)
                        else:
                            C.bump("c_devv")
                            if en == "vector":
                                eng.wait_ge(sem("c_dmm"), C["c_dmm"])
                                dstf = facc[:, tcl * 2048 + hhh * 1024:
                                            tcl * 2048 + (hhh + 1) * 1024]
                                eng.tensor_add(dstf, ps[:, tcl * 1024:(tcl + 1) * 1024],
                                               dstf).then_inc(sem("c_devv"), 1)
                C["c_dmm_ph"] = C["c_dmm"]
            pend_epi[0] = fb
            if fb == C_NFB - 1:
                emit_epi0(fb)
                for tcl in range(4):
                    emit_epi_t(fb, tcl)
                pend_epi[0] = None
        if en == "gpsimd":
            eng.wait_ge(sem("c_out"), C["c_out"])

    if "p0" in PHASES:
        with contextlib.ExitStack() as es:
            sb = lambda n, s, d: es.enter_context(nc.sbuf_tensor(n, s, d))
            T = {
                "wtmp": sb("p0_wtmp", [128, 4096], F32),
                "whi": sb("p0_whi", [128, 4096], BF16),
                "wlo": sb("p0_wlo", [128, 4096], BF16),
            }
            chunks = []
            for hc in range(NHC):
                src = dr["wq"].ap()[hc * 128:(hc + 1) * 128, :]
                dsth = dr["wq_h"].ap()[:, hc].transpose([1, 0, 2])
                dstl = dr["wq_l"].ap()[:, hc].transpose([1, 0, 2])
                chunks.append(("wq", src, 2048, dsth, dstl))
            for w, cols in (("wk", 512), ("wv", 512), ("wo", 2048)):
                for hc in range(NHC):
                    src = dr[w].ap()[hc * 128:(hc + 1) * 128, :]
                    chunks.append((w, src, cols, dr[w + "_h"].ap()[hc], dr[w + "_l"].ap()[hc]))
            T["chunks"] = chunks
            with nc.Block() as blk:
                mk_emit(blk, lambda en, eng, C: _emit_p0(en, eng, C, T))

    if "a" in PHASES:
        with contextlib.ExitStack() as es:
            sb = lambda n, s, d: es.enter_context(nc.sbuf_tensor(n, s, d))
            pt = lambda n, s: es.enter_context(nc.psum_tensor(n, s, F32))
            T = {
                "xt": sb("a_xt", [128, 4096], F32),
                "xn": sb("a_xn", [128, 4096], F32),
                "xnt_h": sb("a_xnth", [128, 8192], BF16),
                "xnt_l": sb("a_xntl", [128, 8192], BF16),
                "wbh": sb("a_wbh", [128, 8192], BF16),
                "wbl": sb("a_wbl", [128, 8192], BF16),
                "ck": sb("a_ck", [128, 2048], F32),
                "sk": sb("a_sk", [128, 2048], F32),
                "stats": sb("a_stats", [128, 8], F32),
                "kf": sb("a_kf", [128, 512], F32),
                "kc": sb("a_kc", [128, 512], F32),
                "ks": sb("a_ks", [128, 512], F32),
                "ksw": sb("a_ksw", [128, 512], F32),
                "identf": sb("a_identf", [128, 128], F32),
                "tbuf": sb("a_tbuf", [128, 2048], F32),
                "ptr": pt("a_ptr", [128, 2048]),
                "pkv": pt("a_pkv", [128, 1024]),
            }
            with nc.Block() as blk:
                mk_emit(blk, lambda en, eng, C: _emit_a(en, eng, C, T))

    if "b1" in PHASES:
        with contextlib.ExitStack() as es:
            sb = lambda n, s, d: es.enter_context(nc.sbuf_tensor(n, s, d))
            pt = lambda n, s: es.enter_context(nc.psum_tensor(n, s, F32))
            T = {
                "xnt_h": sb("b_xnth", [128, 8192], BF16),
                "xnt_l": sb("b_xntl", [128, 8192], BF16),
                "cq": sb("b_cq", [128, 512], F32),
                "sq": sb("b_sq", [128, 512], F32),
                "wqb_h": sb("b_wqbh", [128, 4096], BF16),
                "wqb_l": sb("b_wqbl", [128, 4096], BF16),
                "qf": sb("b_qf", [128, 512], F32),
                "qc": sb("b_qc", [128, 512], F32),
                "qs": sb("b_qs", [128, 512], F32),
                "qsw": sb("b_qsw", [128, 512], F32),
                "q_h": sb("b_qh", [128, 1024], BF16),
                "q_l": sb("b_ql", [128, 1024], BF16),
                "p_f": sb("b_pf", [128, 2048], F32),
                "p_b": sb("b_pb", [128, 2048], BF16),
                "p_r": sb("b_pr", [128, 2048], BF16),
                "ao_h": sb("b_aoh", [128, HQ * 512], BF16),
                "ao_l": sb("b_aol", [128, HQ * 512], BF16),
                "ao_f": sb("b_aof", [128, 512], F32),
                "rr_b": sb("b_rrb", [128, 512], F32),
                "rs_row": sb("b_rsrow", [1, 512], F32),
                "rr_row": sb("b_rrrow", [1, 512], F32),
                "negc": sb("b_negc", [128, 1], F32),
                "ones_b": sb("b_onesb", [128, 1], BF16),
                "ones_rf": sb("b_onesrf", [1, 128], F32),
                "cmask": sb("b_cmask", [128, 2048], BF16),
                "identb": sb("b_identb", [128, 128], BF16),
                "pARE": pt("b_pARE", [128, 4096]),
                "wob_h": sb("b_wobh", [128, 2048], BF16),
                "wob_l": sb("b_wobl", [128, 2048], BF16),
                "s_half": sb("b_shalf", [128, 2048], F32),
            }
            with nc.Block() as blk:
                mk_emit(blk, lambda en, eng, C: _emit_b1(en, eng, C, T))

    if "devkv" in PHASES:
        dkh = nc.dram_tensor("dev_kh", [128, NKV * S], BF16, kind="ExternalOutput")
        dkl = nc.dram_tensor("dev_kl", [128, NKV * S], BF16, kind="ExternalOutput")
        dvf = nc.dram_tensor("dev_vf", [128, NTC * 512], F32, kind="ExternalOutput")
        with nc.Block() as blk:
            @blk.gpsimd
            def _(eng):
                eng.dma_start(out=dkh.ap()[:, :], in_=g_k_h[:]).then_inc(sem("devkv"), 16)
                eng.dma_start(out=dkl.ap()[:, :], in_=g_k_l[:]).then_inc(sem("devkv"), 16)
                eng.dma_start(out=dvf.ap()[:, :], in_=g_v_f[:]).then_inc(sem("devkv"), 16)
                eng.wait_ge(sem("devkv"), 48)

    pstack.close()

    if "c" in PHASES:
        with contextlib.ExitStack() as es:
            sb = lambda n, s, d: es.enter_context(nc.sbuf_tensor(n, s, d))
            pt = lambda n, s: es.enter_context(nc.psum_tensor(n, s, F32))
            T = {
                "xnt_h": sb("c_xnth", [128, 8192], BF16),
                "xnt_l": sb("c_xntl", [128, 8192], BF16),
                "gb": sb("c_gb", [128, 4096], BF16),
                "ub": sb("c_ub", [128, 4096], BF16),
                "db": sb("c_db", [128, 2048], BF16),
                "h_h": sb("c_hh", [128, 32 * 512], BF16),
                "h_l": sb("c_hl", [128, 32 * 512], BF16),
                "sg_s": sb("c_sgs", [128, 1024], F32),
                "hf_s": sb("c_hfs", [128, 1024], F32),
                "facc": sb("c_facc", [128, 4 * 2048], F32),
                "gsc": sb("c_gsc", [128, NIC], F32),
                "usc": sb("c_usc", [128, NIC], F32),
                "dscb": sb("c_dscb", [128, E], F32),
                "hs_row": sb("c_hsrow", [1, 1024], F32),
                "negh": sb("c_negh", [128, 4], F32),
                "onesb": sb("c_onesb", [128, 1], BF16),
                "onesf1": sb("c_onesf1", [1, 1], F32),
                "st_t": sb("c_stt", [128, 2048], F32),
                "xt_t": sb("c_xtt", [128, 2048], F32),
                "ep_g": sb("c_epg", [128, 2048], F32),
                "ps": pt("c_ps", [128, 4096]),
            }
            with nc.Block() as blk:
                mk_emit(blk, lambda en, eng, C: _emit_c(en, eng, C, T))

    return nc


def kernel(**inputs):
    in_maps = _host_prep(**inputs)
    nc = build_nc()
    res = run_bass_kernel_spmd(nc, in_maps, core_ids=list(range(8)))
    return np.stack([np.asarray(res.results[c]["out"]) for c in range(8)], axis=0)
